# version 05
# baseline (speedup 1.0000x reference)
import numpy as np
from contextlib import ExitStack

import concourse.bass as bass
import concourse.mybir as mybir
from concourse.bass_utils import run_bass_kernel_spmd

F32 = mybir.dt.float32
BF16 = mybir.dt.bfloat16
AF = mybir.ActivationFunctionType
OP = mybir.AluOpType
AX = mybir.AxisListType

D = 1024
T = 4096
TC = 512
NCH = 8
MY0 = 4
NDFF = 32
NS = 64
RS2 = 0.7071067811865476
INV_D = 1.0 / D
ENGS = ("tensor", "vector", "scalar", "sync", "gpsimd")


class Ch:
    __slots__ = ("sem", "n")

    def __init__(self, sem):
        self.sem = sem
        self.n = 0


class Prog:
    def __init__(self, nc, stack):
        self.nc = nc
        self.stack = stack
        self.q = {e: [] for e in ENGS}
        self._ch = {}

    def ch(self, name):
        c = self._ch.get(name)
        if c is None:
            c = Ch(self.stack.enter_context(self.nc.semaphore(name)))
            self._ch[name] = c
        return c

    def wait(self, eng, ch, n=None):
        v = ch.n if n is None else n
        if v <= 0:
            return
        self.q[eng].append(lambda e, s=ch.sem, vv=v: e.wait_ge(s, vv))

    def emit(self, eng, fn, inc=None, k=1):
        if eng == "vector":
            self.q[eng].append(lambda e: e.engine_nop())
            self.q[eng].append(lambda e: e.engine_nop())
            self.q[eng].append(lambda e: e.engine_nop())
        if inc is None:
            self.q[eng].append(fn)
        elif k == 1:
            inc.n += k
            self.q[eng].append(fn)
            self.q[eng].append(lambda e, s=inc.sem, kk=k: e.drain().then_inc(s, kk))
        else:
            inc.n += k
            self.q[eng].append(lambda e, f=fn, s=inc.sem, kk=k: f(e).then_inc(s, kk))

    act_trailer = None

    def barrier(self):
        ch = self.ch("bar")
        for eng in ENGS:
            self.q[eng].append(lambda e, s=ch.sem: e.drain().then_inc(s, 1))
        ch.n += len(ENGS)
        for eng in ENGS:
            self.wait(eng, ch)

    def flush(self, block):
        for eng in ENGS:
            lst = self.q[eng]
            if not lst:
                continue
            self.q[eng] = []
            getattr(block, eng)(lambda e, L=lst: [f(e) for f in L])


def _mm(out_ap, lhsT_ap, rhs_ap, start, stop, is_t=False):
    return lambda e: e.matmul(out_ap, lhsT_ap, rhs_ap, start=start, stop=stop,
                              is_transpose=is_t)


def build_program(debug_outputs=()):
    nc = bass.Bass("TRN2", target_bir_lowering=False)
    dt = lambda name, shape, kind="ExternalInput": nc.dram_tensor(name, shape, F32, kind=kind)

    xin = dt("xin", [T, D])
    wg_d = dt("wg", [128, 8, 8, 128])
    wc_d = dt("wc", [128, 8, 8, 128])
    w1a_d = dt("w1a", [128, 8, 4, 128])
    w1b_d = dt("w1b", [8, 128, 512])
    w2r_d = dt("w2r", [4, 128, 64])
    b1row_d = dt("b1row", [1, 512])
    b2col_d = dt("b2col", [64, 1])
    tau_d = dt("tau", [1, 1])
    n1col_d = dt("n1col", [128, 8])
    n2col_d = dt("n2col", [128, 8])
    bgcol_d = dt("bgcol", [128, 8])
    bccol_d = dt("bccol", [128, 8])
    bocol_d = dt("bocol", [128, 8])
    sinitT_d = dt("sinitT", [128, 8, 64])
    sinit_d = dt("sinit", [64, D])
    wihT_d = dt("wihT", [8, 128, 3072])
    whhT_d = dt("whhT", [8, 128, 3072])
    bihrow_d = dt("bihrow", [1, 3072])
    bhhrow_d = dt("bhhrow", [1, 3072])
    wq_d = dt("wq", [128, 8, 256])
    wk_d = dt("wk", [128, 8, 256])
    wv_d = dt("wv", [128, 8, D])
    wvp_d = dt("wvp", [128, 8, D])
    bqrow_d = dt("bqrow", [1, 256])
    bkrow_d = dt("bkrow", [1, 256])
    bvrow_d = dt("bvrow", [1, D])
    bvprow_d = dt("bvprow", [1, D])
    lngrow_d = dt("lngrow", [1, D])
    lnbrow_d = dt("lnbrow", [1, D])
    wo_d = dt("wo", [8, 128, 8, 128])
    w1f_d = dt("w1f", [32, 128, 8, 128])
    w3f_d = dt("w3f", [32, 128, 8, 128])
    w2f_d = dt("w2f", [8, 128, 32, 128])
    eye_d = dt("eye", [128, 128])
    onesc_d = dt("onesc", [128, 1])
    onesr_d = dt("onesr", [1, 128])
    flag_d = dt("flag", [128, 1])
    out_d = dt("out", [2048, D], kind="ExternalOutput")

    w1h_d = nc.dram_tensor("w1h", [32, 128, 1024], BF16, kind="Internal")
    w1l_d = nc.dram_tensor("w1l", [32, 128, 1024], BF16, kind="Internal")
    w3h_d = nc.dram_tensor("w3h", [32, 128, 1024], BF16, kind="Internal")
    w3l_d = nc.dram_tensor("w3l", [32, 128, 1024], BF16, kind="Internal")
    w2h_d = nc.dram_tensor("w2h", [8, 128, 4096], BF16, kind="Internal")
    w2l_d = nc.dram_tensor("w2l", [8, 128, 4096], BF16, kind="Internal")
    woh_d = nc.dram_tensor("woh", [8, 128, 1024], BF16, kind="Internal")
    wol_d = nc.dram_tensor("wol", [8, 128, 1024], BF16, kind="Internal")

    alpha_d = nc.dram_tensor("alphasm", [4, 64, TC], F32, kind="Internal")
    x2kind = "ExternalOutput" if "x2" in debug_outputs else "Internal"
    x2s_d = nc.dram_tensor("dbg_x2" if x2kind == "ExternalOutput" else "x2scr",
                           [4, 8, 128, TC], F32, kind=x2kind)

    dbg = {}
    def dbg_out(name, shape):
        if name in debug_outputs:
            h = nc.dram_tensor("dbg_" + name, shape, F32, kind="ExternalOutput")
            dbg[name] = h
            return h
        return None

    dbg_xn = dbg_out("xn", [NCH, 128, 4 * D])
    dbg_st = dbg_out("st", [NCH, 128, 40])
    dbg_lg = dbg_out("lg", [NCH, 128, 256])
    dbg_h1 = dbg_out("h1", [NCH, 128, 2048])
    dbg_tk = dbg_out("tk", [NCH, 128, 16])
    dbg_bp = dbg_out("bp", [14, 64, 3072])
    dbg_hin = dbg_out("hin", [NCH, 128, 8 * TC])
    dbg_alpha = dbg_out("alpha", [NCH, 128, 4 * 64])
    dbg_si = dbg_out("si", [128, 512])
    dbg_den = dbg_out("den", [1, 64])
    dbg_snew = dbg_out("snew", [64, D])
    dbg_sv = dbg_out("sv", [64, D])
    dbg_fin0 = dbg_out("fin0", [128, 8 * TC])
    dbg_s0 = dbg_out("s0", [128, 8 * TC])
    dbg_g0 = dbg_out("g0", [128, 8 * TC])

    stack = ExitStack()
    with stack:
        p = Prog(nc, stack)
        sb = lambda name, shape: stack.enter_context(nc.sbuf_tensor(name, shape, F32))

        eye = sb("eye_t", [128, 128])
        onesc = sb("onesc_t", [128, 1])
        onesr = sb("onesr_t", [1, 128])
        flagc = sb("flagc_t", [128, 1])
        n1c = sb("n1c_t", [128, 8])
        n2c = sb("n2c_t", [128, 8])
        bgc = sb("bgc_t", [128, 8])
        bcc = sb("bcc_t", [128, 8])
        boc = sb("boc_t", [128, 8])
        b2c = sb("b2c_t", [64, 1])
        taub = sb("taub_t", [1, 1])
        itau1 = sb("itau1_t", [1, 1])
        itau64 = sb("itau64_t", [64, 1])
        b2s = sb("b2s_t", [64, 1])
        sinitT = sb("sinitT_t", [128, 8 * 64])
        smdm = sb("smdm_t", [128, 8])
        sm1c = sb("sm1c_t", [128, 4])
        sm1sc = sb("sm1sc_t", [128, 4])
        state = sb("state_t", [128, 8])
        actw = sb("actw_t", [128, 1])
        gwarm = sb("gwarm_t", [128, 128])
        p.act_trailer = lambda e: e.activation(actw[:], onesc[:], AF.Square)
        svb = sb("svb_t", [64, D])
        pb = [stack.enter_context(nc.psum_tensor("pb%d" % i, [128, 512], F32))
              for i in range(8)]

        def tclose(ch):
            return lambda e: e.matmul(pb[7][0:1, 0:1], onesr[0:1, 0:1],
                                      onesr[0:1, 0:1], start=True, stop=True)

        DBG = p.ch("DBG")
        RQF = p.ch("RQF")
        AKF = p.ch("AKF")

        def act_fence(func):
            p.emit("scalar", lambda e, f=func: e.activation(actw[:], onesc[:], f))
            p.emit("scalar", lambda e, s_=RQF.sem: e.sem_inc(s_, 1))
            RQF.n += 1
            p.wait("gpsimd", RQF, RQF.n)
            for _ in range(6):
                p.emit("gpsimd", lambda e: e.tensor_copy(gwarm[:], gwarm[:]))
            p.emit("gpsimd", lambda e, s_=AKF.sem: e.drain().then_inc(s_, 1))
            AKF.n += 1
            p.wait("scalar", AKF, AKF.n)

        with nc.Block() as block:
            with ExitStack() as ascope:
                asb = lambda name, shape, dt_=F32: ascope.enter_context(nc.sbuf_tensor(name, shape, dt_))
                wgh = asb("wgh", [128, 8192], BF16)
                wgl = asb("wgl", [128, 8192], BF16)
                wch = asb("wch", [128, 8192], BF16)
                wcl = asb("wcl", [128, 8192], BF16)
                w1a = asb("w1at", [128, 4096])
                w2r = asb("w2rt", [128, 256])
                xld = asb("xld", [128, 2 * 4 * D])
                w1bt = xld
                xn_tm = asb("xn_tm", [128, 4 * D])
                xn_fm = asb("xn_fm", [128, 8 * TC])
                hinh = asb("hinh", [128, 8 * TC], BF16)
                hinl = asb("hinl", [128, 8 * TC], BF16)
                h1b = asb("h1b", [128, 4 * TC])
                scr = asb("scr", [128, D])
                b1r = scr[0:1, 0:512]
                sm1r = scr[0:1, 512:1024]
                ab = asb("ab_t", [128, TC])
                cb = asb("cb_t", [128, TC])
                bbb = asb("bb_t", [128, TC])
                hbb = asb("hb_t", [128, TC])
                lgf = asb("lgf", [64, TC])
                lgtm = asb("lgtm", [128, 256])
                cur = asb("cur", [128, 256])
                eb = asb("eb", [128, 256])
                aet = asb("aet", [128, 256])
                alf = asb("alf", [128, 256])
                r1r4 = asb("r1r4", [4, 128])
                rowr1 = asb("rowr1", [1, 512])
                mall = asb("mall", [128, 32])
                rhoall = asb("rhoall", [128, 32])
                r1all = asb("r1all", [128, 32])
                mkc = asb("mkc", [128, 4])
                nmc = asb("nmc", [128, 4])
                thrc = asb("thrc", [128, 4])
                zc = asb("zc", [128, 4])
                rzc = asb("rzc", [128, 4])
                spst = asb("spst", [128, 2 * 512])
                sphi = asb("sphi", [128, 2 * 512], BF16)
                splo = asb("splo", [128, 2 * 512], BF16)

                LDW = p.ch("LDW"); LDX = p.ch("LDX")
                IT = p.ch("IT"); IV = p.ch("IV"); IS = p.ch("IS")
                RST0 = p.ch("RST0")
                WLD = p.ch("WLD"); WSP = p.ch("WSP")

                def ld(dst_ap, src_ap):
                    p.emit("sync", lambda e, o=dst_ap, i=src_ap: e.dma_start(out=o, in_=i),
                           inc=LDW, k=16)

                ld(eye[:], eye_d.ap())
                ld(onesc[:], onesc_d.ap())
                ld(onesr[:], onesr_d.ap())
                ld(flagc[:], flag_d.ap())
                ld(n1c[:], n1col_d.ap())
                ld(n2c[:], n2col_d.ap())
                ld(bgc[:], bgcol_d.ap())
                ld(bcc[:], bccol_d.ap())
                ld(boc[:], bocol_d.ap())
                ld(b2c[:], b2col_d.ap())
                ld(taub[:], tau_d.ap())
                ld(sinitT[:].rearrange("p (k s) -> p k s", k=8), sinitT_d.ap())
                ld(w1a[:].rearrange("p (k e c) -> p k e c", k=8, e=4), w1a_d.ap())
                ld(w2r[:].rearrange("p (d s) -> p d s", d=4), w2r_d.ap().rearrange("d p s -> p d s"))
                N_LDW = LDW.n

                for (wd_, hi_, lo_) in ((wg_d, wgh, wgl), (wc_d, wch, wcl)):
                    p.wait("sync", WSP, WSP.n)
                    p.emit("sync", lambda e, i=wd_: e.dma_start(
                        out=xld[:].rearrange("p (k e c) -> p k e c", k=8, e=8),
                        in_=i.ap()), inc=WLD, k=16)
                    p.wait("vector", LDW, N_LDW)
                    p.wait("vector", WLD, WLD.n)
                    for kc in range(8):
                        p.emit("vector", lambda e, k=kc: e.tensor_scalar(
                            xld[:, k * 1024:(k + 1) * 1024], xld[:, k * 1024:(k + 1) * 1024],
                            n1c[:, k:k + 1], None, OP.mult))
                    p.emit("vector", lambda e, h_=hi_: e.tensor_copy(h_[:], xld[:]))
                    p.emit("vector", lambda e, h_=hi_, l_=lo_: e.tensor_tensor(
                        l_[:], xld[:], h_[:], OP.subtract), inc=WSP)
                p.wait("sync", WSP, 2)
                ld(xld[:, 0:4096].rearrange("p (k t) -> p k t", k=8),
                   w1b_d.ap().rearrange("k p t -> p k t"))
                ld(b1r, b1row_d.ap())
                N_LDW2 = LDW.n
                p.wait("vector", LDW, N_LDW)
                p.emit("vector", lambda e: e.memzero(state[:]))
                p.emit("vector", lambda e: e.tensor_reduce(
                    smdm[:].rearrange("p (k o) -> p k o", k=8, o=1),
                    sinitT[:].rearrange("p (k s) -> p k s", k=8), AX.X, OP.add))
                p.emit("vector", lambda e: e.tensor_scalar(smdm[:], smdm[:], 1.0 / 64, None, OP.mult),
                       inc=IV)
                p.emit("gpsimd", lambda e: e.memset(gwarm[:], 1.0))
                p.wait("scalar", LDW, N_LDW)
                act_fence(AF.Ln)
                p.emit("scalar", lambda e: e.activation(itau1[:], taub[:], AF.Abs), inc=IS)
                p.wait("vector", IS)
                p.emit("vector", lambda e: e.tensor_scalar(itau1[:], itau1[:], 0.1, None, OP.add))
                p.emit("vector", lambda e: e.reciprocal(itau1[:], itau1[:]), inc=IV)
                p.wait("tensor", LDW, N_LDW2)
                p.wait("tensor", IV)
                p.emit("tensor", _mm(pb[7][0:64, 0:1], onesr[:, 0:64], itau1[:], True, True), inc=IT)
                for kc in range(8):
                    p.emit("tensor", _mm(pb[6][0:1, 0:512], smdm[:, kc:kc + 1],
                                         w1bt[:, kc * 512:(kc + 1) * 512], kc == 0, kc == 7),
                           inc=(IT if kc == 7 else None))
                p.wait("vector", IT)
                p.emit("vector", lambda e: e.tensor_copy(itau64[:], pb[7][0:64, 0:1]))
                p.emit("vector", lambda e: e.tensor_tensor(b2s[:], b2c[:], itau64[:], OP.mult))
                p.emit("vector", lambda e: e.tensor_tensor(sm1r, pb[6][0:1, 0:512], b1r, OP.add),
                       inc=IV)
                p.wait("tensor", IV)
                for j in range(4):
                    p.emit("tensor", _mm(pb[5][0:128, j:j + 1], scr[0:1, 512 + j * 128:512 + (j + 1) * 128],
                                         onesc[0:1, 0:1], j == 0, j == 3),
                           inc=(IT if j == 3 else None))
                p.wait("vector", IT)
                p.emit("vector", lambda e: e.tensor_copy(sm1c[:], pb[5][0:128, 0:4]))
                p.emit("vector", lambda e: e.tensor_scalar(sm1sc[:], sm1c[:], RS2, None, OP.mult),
                       inc=IV)
                N_IV = IV.n

                PLX = p.ch("PLX"); PSQ = p.ch("PSQ"); PRD = p.ch("PRD")
                s1c = h1b[:, 0:32]
                s2c = h1b[:, 32:64]
                varc = h1b[:, 64:96]
                vpec = h1b[:, 96:128]
                recc = h1b[:, 128:160]
                rpc = h1b[:, 160:192]
                for c in range(NCH):
                    xb0 = xld[:, (c % 2) * 4 * D:(c % 2 + 1) * 4 * D]
                    p.wait("sync", PSQ, 4 * (c - 1))
                    p.wait("sync", PRD, 4 * (c - 1))
                    if c < 2:
                        p.wait("sync", WSP, 2)
                    if c == 0:
                        p.wait("sync", IT, IT.n)
                    p.emit("sync", lambda e, c=c, o=xb0: e.dma_start(
                        out=o.rearrange("p (j d) -> p j d", j=4),
                        in_=xin.ap()[c * TC:(c + 1) * TC, :].rearrange(
                            "(j p) d -> p j d", j=4)), inc=PLX, k=16)
                    p.wait("scalar", PLX, 16 * (c + 1))
                    if c == 0:
                        p.wait("scalar", LDW, N_LDW)
                        act_fence(AF.Ln)
                    for j in range(4):
                        p.emit("scalar", lambda e, x=xb0[:, j * D:(j + 1) * D], c=c, j=j:
                               e.activation(scr[:], x, AF.Square,
                                            accum_out=h1b[:, 32 + c * 4 + j:32 + c * 4 + j + 1]),
                               inc=PSQ)
                    p.wait("vector", PLX, 16 * (c + 1))
                    for j in range(4):
                        p.emit("vector", lambda e, x=xb0[:, j * D:(j + 1) * D], c=c, j=j:
                               e.tensor_reduce(h1b[:, c * 4 + j:c * 4 + j + 1], x,
                                               AX.X, OP.add), inc=PRD)
                p.wait("vector", PSQ, 32)
                p.emit("vector", lambda e: e.tensor_scalar(mall[:], s1c, INV_D, None, OP.mult))
                p.emit("vector", lambda e: e.tensor_scalar(s2c, s2c, INV_D, None, OP.mult))
                p.emit("vector", lambda e: e.tensor_tensor(varc, mall[:], mall[:], OP.mult))
                p.emit("vector", lambda e: e.tensor_tensor(varc, s2c, varc, OP.subtract))
                p.emit("vector", lambda e: e.tensor_scalar(vpec, varc, 1e-5, None, OP.add))
                p.emit("vector", lambda e: e.reciprocal(recc, vpec))
                p.emit("vector", lambda e: e.tensor_tensor(rpc, varc, recc, OP.mult))
                p.emit("vector", lambda e: e.tensor_scalar(rpc, rpc, 1e-6, None, OP.add),
                       inc=PRD)
                p.wait("scalar", PRD, PRD.n)
                p.emit("scalar", lambda e: e.activation(s1c, vpec, AF.Ln))
                p.emit("scalar", lambda e: e.activation(s2c, rpc, AF.Ln))
                p.emit("scalar", lambda e: e.activation(rhoall[:], s1c, AF.Exp, scale=-0.5))
                p.emit("scalar", lambda e: e.activation(r1all[:], s2c, AF.Exp, scale=-0.5),
                       inc=RST0)
                act_fence(AF.Sigmoid)

                SQ = p.ch("SQ"); VST = p.ch("VST"); RST = p.ch("RST"); XNT = p.ch("XNT")
                TG = p.ch("TG"); TGE = p.ch("TGE"); BC = p.ch("BC"); HIN = p.ch("HIN")
                H1T = p.ch("H1T"); H1S = p.ch("H1S"); H1V = p.ch("H1V")
                LGT = p.ch("LGT"); LGE = p.ch("LGE"); TOPK = p.ch("TOPK")
                SEXP = p.ch("SEXP"); ALPHA = p.ch("ALPHA")
                GCT = p.ch("GCT"); GCE = p.ch("GCE"); SCE = p.ch("SCE")
                SLT = p.ch("SLT"); SPL = p.ch("SPL"); RR = p.ch("RR")

                WIN = p.ch("WIN"); SPV = p.ch("SPV"); SPW = p.ch("SPW")
                ASM = p.ch("ASM")
                split_jobs = []
                for i in range(32):
                    for (srcw, hiw, low) in ((w1f_d, w1h_d, w1l_d),
                                             (w3f_d, w3h_d, w3l_d)):
                        for q in range(2):
                            split_jobs.append((
                                srcw.ap()[i][:, q * 4:(q + 1) * 4, :].rearrange(
                                    "p k e -> p (k e)"),
                                hiw.ap()[i][:, q * 512:(q + 1) * 512],
                                low.ap()[i][:, q * 512:(q + 1) * 512]))
                for g in range(8):
                    for q in range(8):
                        split_jobs.append((
                            w2f_d.ap()[g][:, q * 4:(q + 1) * 4, :].rearrange(
                                "p k e -> p (k e)"),
                            w2h_d.ap()[g][:, q * 512:(q + 1) * 512],
                            w2l_d.ap()[g][:, q * 512:(q + 1) * 512]))
                for g in range(8):
                    for q in range(2):
                        split_jobs.append((
                            wo_d.ap()[g][:, q * 4:(q + 1) * 4, :].rearrange(
                                "p k e -> p (k e)"),
                            woh_d.ap()[g][:, q * 512:(q + 1) * 512],
                            wol_d.ap()[g][:, q * 512:(q + 1) * 512]))

                def pump_split(n):
                    for _ in range(n):
                        if not split_jobs:
                            return
                        src, hi_, lo_ = split_jobs.pop(0)
                        i = pump_split.i
                        o = (i % 2) * 512
                        p.wait("sync", SPV, i - 1)
                        p.emit("sync", lambda e, oo=spst[:, o:o + 512], ii=src:
                               e.dma_start(out=oo, in_=ii), inc=WIN, k=16)
                        p.wait("vector", WIN, WIN.n)
                        p.wait("vector", SPW, 32 * (i - 1))
                        p.emit("vector", lambda e, o=o: e.tensor_copy(
                            sphi[:, o:o + 512], spst[:, o:o + 512]))
                        p.emit("vector", lambda e, o=o: e.tensor_tensor(
                            splo[:, o:o + 512], spst[:, o:o + 512],
                            sphi[:, o:o + 512], OP.subtract), inc=SPV)
                        p.wait("gpsimd", SPV, SPV.n)
                        p.emit("gpsimd", lambda e, oo=hi_, o=o: e.dma_start(
                            out=oo, in_=sphi[:, o:o + 512]), inc=SPW, k=16)
                        p.emit("gpsimd", lambda e, oo=lo_, o=o: e.dma_start(
                            out=oo, in_=splo[:, o:o + 512]), inc=SPW, k=16)
                        pump_split.i += 1
                pump_split.i = 0

                for c in range(NCH):
                    mine = c >= MY0
                    xb = xld[:, (c % 2) * 4 * D:(c % 2 + 1) * 4 * D]
                    tg_base = TG.n

                    if c < 2:
                        p.wait("sync", PSQ, 32)
                        p.wait("sync", PRD, 32)
                    p.wait("sync", XNT, c - 1)
                    p.emit("sync", lambda e, c=c, o=xb: e.dma_start(
                        out=o.rearrange("p (j d) -> p j d", j=4),
                        in_=xin.ap()[c * TC:(c + 1) * TC, :].rearrange("(j p) d -> p j d", j=4)),
                        inc=LDX, k=16)

                    for dh in range(4):
                        p.wait("scalar", H1T, 4 * c + dh + 1)
                        p.wait("scalar", H1V, 4 * c + dh)
                        p.emit("scalar", lambda e, dh=dh: e.activation(
                            scr[:, 0:512], pb[5][:, 0:512], AF.Erf,
                            bias=sm1sc[:, dh:dh + 1], scale=RS2))
                        p.emit("scalar", lambda e, dh=dh: e.activation(
                            scr[:, 512:1024], pb[5][:, 0:512], AF.Identity,
                            bias=sm1c[:, dh:dh + 1]), inc=H1S)
                    p.wait("scalar", LGT, c + 1)
                    if c > MY0:
                        p.wait("scalar", ASM, 16 * (c - MY0))
                    p.emit("scalar", lambda e: e.activation(
                        lgf[:], pb[6][0:64, 0:512], AF.Identity, bias=b2s[:], scale=itau64[:]),
                        inc=LGE)
                    for ec in range(8):
                        p.wait("scalar", GCT, 16 * c + 2 * ec + 1)
                        p.wait("scalar", SCE, 8 * c + ec)
                        p.emit("scalar", lambda e, ec=ec: e.activation(
                            ab[:], pb[2][:, 0:512], AF.Sigmoid, bias=bgc[:, ec:ec + 1]), inc=GCE)
                        p.wait("scalar", GCT, 16 * c + 2 * ec + 2)
                        p.emit("scalar", lambda e, ec=ec: e.activation(
                            cb[:], pb[3][:, 0:512], AF.Tanh, bias=bcc[:, ec:ec + 1]), inc=GCE)
                    p.wait("scalar", TOPK, c + 1)
                    for j in range(4):
                        p.emit("scalar", lambda e, j=j: e.activation(
                            eb[:, j * 64:(j + 1) * 64], lgtm[:, j * 64:(j + 1) * 64],
                            AF.Tanh, bias=nmc[:, j:j + 1], scale=0.5),
                            inc=(SEXP if j == 3 else None))

                    p.wait("vector", LDX, 16 * (c + 1))
                    if c == 0:
                        p.wait("vector", RST0, 1)
                    p.wait("vector", SLT, c)
                    for j in range(4):
                        p.emit("vector", lambda e, x=xb[:, j * D:(j + 1) * D], c=c, j=j:
                               e.tensor_scalar(
                            xn_tm[:, j * D:(j + 1) * D], x,
                            mall[:, c * 4 + j:c * 4 + j + 1],
                            rhoall[:, c * 4 + j:c * 4 + j + 1],
                            OP.subtract, OP.mult), inc=(XNT if j == 3 else None))
                    if dbg_xn is not None:
                        p.wait("gpsimd", XNT, c + 1)
                        p.emit("gpsimd", lambda e, c=c: e.dma_start(
                            out=dbg_xn.ap()[c], in_=xn_tm[:]), inc=DBG, k=16)
                    if dbg_st is not None:
                        p.emit("gpsimd", lambda e, c=c: e.dma_start(
                            out=dbg_st.ap()[c][:, 0:4], in_=mall[:, c * 4:(c + 1) * 4]),
                            inc=DBG, k=16)
                        p.emit("gpsimd", lambda e, c=c: e.dma_start(
                            out=dbg_st.ap()[c][:, 4:8], in_=rhoall[:, c * 4:(c + 1) * 4]),
                            inc=DBG, k=16)
                    p.wait("vector", H1T, 4 * c)
                    if c > MY0:
                        p.wait("vector", SPL, 16 * (c - MY0))
                    for g in range(8):
                        p.wait("vector", TG, tg_base + g + 1)
                        p.emit("vector", lambda e, g=g: e.tensor_copy(
                            xn_fm[:, g * TC:(g + 1) * TC], pb[4][:, 0:512]), inc=TGE)
                    p.wait("vector", TG, tg_base + 9)
                    p.wait("vector", RR, 16 * c)
                    p.emit("vector", lambda e: e.tensor_copy(r1r4[:], pb[4][0:4, 0:128]), inc=TGE)
                    p.wait("sync", TGE, TGE.n)
                    p.emit("sync", lambda e: e.dma_start(out=rowr1[:], in_=r1r4[:]),
                           inc=RR, k=16)
                    p.wait("vector", BC, c + 1)
                    for g in range(8):
                        p.emit("vector", lambda e, g=g: e.tensor_tensor(
                            hbb[:], xn_fm[:, g * TC:(g + 1) * TC],
                            pb[6][:, 0:512], OP.mult))
                        p.emit("vector", lambda e, g=g: e.tensor_copy(
                            hinh[:, g * TC:(g + 1) * TC], hbb[:]))
                        p.emit("vector", lambda e, g=g: e.tensor_tensor(
                            hinl[:, g * TC:(g + 1) * TC], hbb[:],
                            hinh[:, g * TC:(g + 1) * TC], OP.subtract),
                            inc=(HIN if g == 7 else None))
                    if dbg_hin is not None:
                        p.wait("gpsimd", HIN, c + 1)
                        p.emit("gpsimd", lambda e, c=c: e.dma_start(
                            out=dbg_hin.ap()[c], in_=hinh[:]), inc=DBG, k=16)
                    pump_split(26)
                    for dh in range(4):
                        p.wait("vector", H1S, 4 * c + dh + 1)
                        p.emit("vector", lambda e: e.tensor_scalar(
                            scr[:, 0:512], scr[:, 0:512], 0.5, 0.5, OP.mult, OP.add))
                        p.emit("vector", lambda e, dh=dh: e.tensor_tensor(
                            h1b[:, dh * TC:(dh + 1) * TC], scr[:, 0:512], scr[:, 512:1024],
                            OP.mult), inc=H1V)
                    p.wait("vector", TG, tg_base + 10)
                    p.emit("vector", lambda e: e.tensor_copy(lgtm[:], pb[4][:, 0:256]), inc=TGE)
                    p.emit("vector", lambda e: e.tensor_copy(cur[:], lgtm[:]))
                    for r in range(8):
                        dst = mkc if r == 0 else (thrc if r == 7 else zc)
                        p.emit("vector", lambda e, d=dst: e.tensor_reduce(
                            d[:].rearrange("p (j o) -> p j o", j=4, o=1),
                            cur[:].rearrange("p (j s) -> p j s", j=4), AX.X, OP.max))
                        if r < 7:
                            for j in range(4):
                                p.emit("vector", lambda e, j=j, d=dst: e.tensor_scalar(
                                    eb[:, j * 64:(j + 1) * 64], cur[:, j * 64:(j + 1) * 64],
                                    d[:, j:j + 1], None, OP.is_equal))
                            for j in range(4):
                                p.emit("vector", lambda e, j=j: e.scalar_tensor_tensor(
                                    cur[:, j * 64:(j + 1) * 64], eb[:, j * 64:(j + 1) * 64],
                                    -1e30, cur[:, j * 64:(j + 1) * 64], OP.mult, OP.add))
                    p.emit("vector", lambda e: e.tensor_scalar(nmc[:], mkc[:], -0.5, None, OP.mult),
                           inc=TOPK)
                    for ec in range(8):
                        p.wait("vector", GCE, 16 * c + 2 * ec + 2)
                        p.emit("vector", lambda e: e.tensor_scalar(
                            bbb[:], ab[:], -1.0, 1.0, OP.mult, OP.add))
                        p.emit("vector", lambda e: e.tensor_tensor(bbb[:], bbb[:], cb[:], OP.mult))
                        p.emit("vector", lambda e, ec=ec: e.tensor_tensor_scan(
                            hbb[:], ab[:], bbb[:], state[:, ec:ec + 1], OP.mult, OP.add))
                        if mine:
                            p.emit("vector", lambda e, ec=ec: e.tensor_tensor(
                                xn_fm[:, ec * TC:(ec + 1) * TC], xn_fm[:, ec * TC:(ec + 1) * TC],
                                hbb[:], OP.add), inc=SCE)
                        else:
                            p.emit("vector", lambda e: e.tensor_copy(bbb[:], hbb[:]), inc=SCE)
                        p.emit("vector", lambda e, ec=ec: e.tensor_copy(
                            state[:, ec:ec + 1], hbb[:, TC - 1:TC]))
                    if c == MY0 - 1:
                        p.emit("vector", lambda e: e.tensor_scalar(
                            state[:], state[:], flagc[:, 0:1], None, OP.mult))
                    p.wait("vector", SEXP, c + 1)
                    p.emit("vector", lambda e: e.tensor_scalar(cur[:], eb[:], -1.0, 1.0,
                                                               OP.mult, OP.add))
                    p.emit("vector", lambda e: e.reciprocal(cur[:], cur[:]))
                    p.emit("vector", lambda e: e.tensor_scalar(eb[:], eb[:], 1.0, None, OP.add))
                    p.emit("vector", lambda e: e.tensor_tensor(eb[:], eb[:], cur[:], OP.mult))
                    for j in range(4):
                        p.emit("vector", lambda e, j=j: e.tensor_scalar(
                            aet[:, j * 64:(j + 1) * 64], lgtm[:, j * 64:(j + 1) * 64],
                            thrc[:, j:j + 1], None, OP.is_ge))
                    p.emit("vector", lambda e: e.tensor_tensor(aet[:], aet[:], eb[:], OP.mult))
                    p.emit("vector", lambda e: e.tensor_reduce(
                        zc[:].rearrange("p (j o) -> p j o", j=4, o=1),
                        aet[:].rearrange("p (j s) -> p j s", j=4), AX.X, OP.add))
                    p.emit("vector", lambda e: e.reciprocal(rzc[:], zc[:]))
                    for j in range(4):
                        p.emit("vector", lambda e, j=j: e.tensor_scalar(
                            alf[:, j * 64:(j + 1) * 64], aet[:, j * 64:(j + 1) * 64],
                            rzc[:, j:j + 1], None, OP.mult), inc=(ALPHA if j == 3 else None))
                    if dbg_lg is not None:
                        p.wait("gpsimd", TOPK, c + 1)
                        p.emit("gpsimd", lambda e, c=c: e.dma_start(
                            out=dbg_lg.ap()[c], in_=lgtm[:]), inc=DBG, k=16)
                        for i_, t_ in enumerate([mkc, nmc, thrc, zc]):
                            p.emit("gpsimd", lambda e, c=c, i_=i_, t_=t_: e.dma_start(
                                out=dbg_tk.ap()[c][:, i_ * 4:(i_ + 1) * 4], in_=t_[:]),
                                inc=DBG, k=16)
                    if dbg_h1 is not None:
                        p.wait("gpsimd", H1V, 4 * (c + 1))
                        p.emit("gpsimd", lambda e, c=c: e.dma_start(
                            out=dbg_h1.ap()[c], in_=h1b[:]), inc=DBG, k=16)
                    if dbg_alpha is not None:
                        p.wait("gpsimd", ALPHA, c + 1)
                        p.emit("gpsimd", lambda e, c=c: e.dma_start(
                            out=dbg_alpha.ap()[c], in_=alf[:]), inc=DBG, k=16)
                    if mine:
                        p.wait("vector", TG, tg_base + 11)
                        p.emit("vector", lambda e: e.tensor_copy(
                            lgf[0:64, 0:512], pb[4][0:64, 0:512]), inc=TGE)
                        p.wait("gpsimd", TGE, TGE.n)
                        p.emit("gpsimd", lambda e, c=c: e.dma_start(
                            out=alpha_d.ap()[c - MY0], in_=lgf[0:64, 0:512]),
                            inc=ASM, k=16)

                    if c == 0:
                        p.wait("tensor", IV, N_IV)
                    p.wait("tensor", XNT, c + 1)
                    for g in range(8):
                        p.wait("tensor", TGE, tg_base + g)
                        for j in range(4):
                            p.emit("tensor", _mm(pb[4][:, j * 128:(j + 1) * 128],
                                                 xn_tm[:, j * D + g * 128:j * D + (g + 1) * 128],
                                                 eye[:], j == 0, j == 3, is_t=True))
                        p.emit("tensor", tclose(TG), inc=TG)
                    if c == 0:
                        p.wait("tensor", RST0, 1)
                    p.wait("tensor", TGE, tg_base + 8)
                    p.emit("tensor", _mm(pb[4][0:4, 0:128], r1all[:, c * 4:(c + 1) * 4],
                                         eye[:], True, True, is_t=True))
                    p.emit("tensor", tclose(TG), inc=TG)
                    p.wait("tensor", RR, 16 * (c + 1))
                    p.wait("tensor", LGE, c)
                    p.emit("tensor", _mm(pb[6][:, 0:512], onesr[:], rowr1[:], True, True),
                           inc=BC)
                    for dh in range(4):
                        p.wait("tensor", H1S, 4 * c + dh)
                        for kc in range(8):
                            p.emit("tensor", _mm(pb[5][:, 0:512],
                                                 w1a[:, (kc * 4 + dh) * 128:(kc * 4 + dh + 1) * 128],
                                                 xn_fm[:, kc * TC:(kc + 1) * TC], kc == 0, kc == 7),
                                   inc=(H1T if kc == 7 else None))
                    p.wait("tensor", HIN, c + 1)
                    p.wait("tensor", H1V, 4 * (c + 1))
                    for dh in range(4):
                        p.emit("tensor", _mm(pb[6][0:64, 0:512], w2r[:, dh * 64:(dh + 1) * 64],
                                             h1b[:, dh * TC:(dh + 1) * TC], dh == 0, dh == 3),
                               inc=(LGT if dh == 3 else None))
                    p.wait("tensor", LGE, c + 1)
                    p.wait("tensor", TGE, tg_base + 9)
                    for j in range(4):
                        p.emit("tensor", _mm(pb[4][:, j * 64:(j + 1) * 64],
                                             lgf[:, j * 128:(j + 1) * 128], eye[0:64, 0:64],
                                             j == 0, j == 3, is_t=True))
                    p.emit("tensor", tclose(TG), inc=TG)
                    p.wait("tensor", HIN, c + 1)
                    for ec in range(8):
                        for bank, wh_, wl_, gch in ((2, wgh, wgl, 0), (3, wch, wcl, 1)):
                            p.wait("tensor", GCE, 16 * c + 2 * ec - 1 + gch)
                            for kc in range(8):
                                sl = slice((kc * 8 + ec) * 128, (kc * 8 + ec + 1) * 128)
                                tsl = slice(kc * TC, (kc + 1) * TC)
                                p.emit("tensor", _mm(pb[bank][:, 0:512], wh_[:, sl],
                                                     hinh[:, tsl], kc == 0, False))
                                p.emit("tensor", _mm(pb[bank][:, 0:512], wh_[:, sl],
                                                     hinl[:, tsl], False, False))
                            for kc in range(8):
                                sl = slice((kc * 8 + ec) * 128, (kc * 8 + ec + 1) * 128)
                                tsl = slice(kc * TC, (kc + 1) * TC)
                                p.emit("tensor", _mm(pb[bank][:, 0:512], wl_[:, sl],
                                                     hinh[:, tsl], False, kc == 7),
                                       inc=(GCT if kc == 7 else None))
                    p.wait("tensor", ALPHA, c + 1)
                    first = (c == 0)
                    last = (c == NCH - 1)
                    for j in range(4):
                        for dchi in range(8):
                            p.emit("tensor", _mm(
                                pb[0][:, dchi * 64:(dchi + 1) * 64],
                                xn_tm[:, j * D + dchi * 128:j * D + (dchi + 1) * 128],
                                alf[:, j * 64:(j + 1) * 64],
                                first and j == 0 and dchi == 0, last and j == 3 and dchi == 7))
                        p.emit("tensor", _mm(pb[1][0:1, 0:64], onesc[:],
                                             alf[:, j * 64:(j + 1) * 64],
                                             first and j == 0, last and j == 3),
                               inc=(SLT if j == 3 else None))
                    if mine:
                        p.wait("tensor", TGE, tg_base + 10)
                        for j in range(4):
                            p.emit("tensor", _mm(pb[4][0:64, j * 128:(j + 1) * 128],
                                                 alf[:, j * 64:(j + 1) * 64], eye[:],
                                                 j == 0, j == 3, is_t=True))
                        p.emit("tensor", tclose(TG), inc=TG)

                    if dbg and "serial" in debug_outputs:
                        p.wait("gpsimd", DBG, DBG.n)
                        p.barrier()
                    if mine:
                        p.wait("gpsimd", SCE, 8 * c + 8)
                        p.emit("gpsimd", lambda e, c=c: e.dma_start(
                            out=x2s_d.ap()[c - MY0].rearrange("g p t -> p g t"),
                            in_=xn_fm[:].rearrange("p (g t) -> p g t", g=8)), inc=SPL, k=16)
                assert not split_jobs
                p.wait("gpsimd", SPL, 64)
                p.wait("gpsimd", SPW, SPW.n)
                p.wait("gpsimd", ASM, 64)
                if dbg:
                    p.wait("gpsimd", DBG, DBG.n)
                p.barrier()
                p.flush(block)

            with ExitStack() as bscope:
                bsb = lambda name, shape: bscope.enter_context(nc.sbuf_tensor(name, shape, F32))
                wstr = bsb("wstr", [128, 2 * 3072])
                gis = bsb("gis", [64, 3072])
                ghs = bsb("ghs", [64, 3072])
                bihb = bsb("bihb", [64, 3072])
                bhhb = bsb("bhhb", [64, 3072])
                bqb = bsb("bqb", [64, 256])
                bkb = bsb("bkb", [64, 256])
                bvb = bsb("bvb", [64, D])
                bvpb = bsb("bvpb", [64, D])
                lngb = bsb("lngb", [64, D])
                lnbb = bsb("lnbb", [64, D])
                rowB = bsb("rowB", [1, 3072])
                sis = bsb("sis", [128, 512])
                sidm = bsb("sidm", [128, 512])
                dres = bsb("dres", [1, 64])
                sS = bsb("sS", [64, D])
                tg1 = bsb("tg1", [64, D])
                tg2 = bsb("tg2", [64, D])
                rg = bsb("rg", [64, D])
                zg = bsb("zg", [64, D])
                ngb = bsb("ngb", [64, D])
                snew = bsb("snew", [64, D])
                sndm = bsb("sndm", [128, 512])
                qv = bsb("qv", [64, 256])
                kvb = bsb("kvb", [64, 256])
                vv = bsb("vv", [64, D])
                qdm = bsb("qdm", [128, 128])
                kdm = bsb("kdm", [128, 128])
                sc64 = bsb("sc64", [64, 64])
                at64 = bsb("at64", [64, 64])
                yv = bsb("yv", [64, D])
                ydm = bsb("ydm", [128, 512])
                wqb = bsb("wqb", [128, 2048])
                wkb = bsb("wkb", [128, 2048])
                cB = bsb("cB", [64, 8])

                BL = p.ch("BL"); BT = p.ch("BT"); BV = p.ch("BV"); BS = p.ch("BS")

                def bld(dst_ap, src_ap):
                    p.emit("sync", lambda e, o=dst_ap, i=src_ap: e.dma_start(out=o, in_=i),
                           inc=BL, k=16)

                def ts_(fns, waits=()):
                    for ch, n in waits:
                        p.wait("tensor", ch, n)
                    for i, f in enumerate(fns):
                        p.emit("tensor", f, inc=(BT if i == len(fns) - 1 else None))

                def vs_(fns, waits=()):
                    for ch, n in waits:
                        p.wait("vector", ch, n)
                    for i, f in enumerate(fns):
                        p.emit("vector", f, inc=(BV if i == len(fns) - 1 else None))

                def ss_(fns, waits=()):
                    for ch, n in waits:
                        p.wait("scalar", ch, n)
                    for i, f in enumerate(fns):
                        p.emit("scalar", f, inc=(BS if i == len(fns) - 1 else None))

                bld(sS[:], sinit_d.ap())
                bld(wqb[:].rearrange("p (k t) -> p k t", k=8), wq_d.ap())
                bld(wkb[:].rearrange("p (k t) -> p k t", k=8), wk_d.ap())
                NB0 = BL.n

                p.wait("vector", p.ch("SLT"), 8)
                vs_([lambda e: e.tensor_copy(dres[:], pb[1][0:1, 0:64]),
                     lambda e: e.tensor_scalar(dres[:], dres[:], 1e-8, None, OP.add),
                     lambda e: e.reciprocal(dres[:], dres[:]),
                     lambda e: e.tensor_copy(sis[:], pb[0][:, 0:512])])
                if dbg_den is not None:
                    p.wait("gpsimd", BV, BV.n)
                    p.emit("gpsimd", lambda e: e.dma_start(out=dbg_den.ap(), in_=dres[:]),
                           inc=DBG, k=16)
                ts_([_mm(pb[2][0:128, 0:64], onesr[:], dres[:], True, True)], waits=[(BV, BV.n)])
                vs_([(lambda e, g=g: e.tensor_tensor(
                        sidm[:, g * 64:(g + 1) * 64], sis[:, g * 64:(g + 1) * 64],
                        pb[2][:, 0:64], OP.mult)) for g in range(8)],
                    waits=[(BT, BT.n)])
                if dbg_si is not None:
                    p.wait("gpsimd", BV, BV.n)
                    p.emit("gpsimd", lambda e: e.dma_start(out=dbg_si.ap(), in_=sidm[:]),
                           inc=DBG, k=16)

                RB = p.ch("RB")

                def bcast64(dst, src_d, n):
                    p.wait("sync", RB, RB.n)
                    bld(rowB[0:1, 0:n], src_d.ap())
                    nldw = BL.n
                    nseg = (n + 511) // 512
                    for i in range(nseg):
                        w = min(512, n - i * 512)
                        p.wait("tensor", p.ch("BL"), nldw)
                        p.wait("tensor", BV, BV.n)
                        p.emit("tensor", _mm(pb[3][0:64, 0:w], onesr[:, 0:64],
                                             rowB[0:1, i * 512:i * 512 + w], True, True),
                               inc=RB)
                        vs_([lambda e, dd=dst, ii=i, ww=w: e.tensor_copy(
                            dd[0:64, ii * 512:ii * 512 + ww], pb[3][0:64, 0:ww])],
                            waits=[(RB, RB.n)])

                bcast64(bihb, bihrow_d, 3072)
                bcast64(bhhb, bhhrow_d, 3072)
                bcast64(bqb, bqrow_d, 256)
                bcast64(bkb, bkrow_d, 256)
                bcast64(bvb, bvrow_d, D)
                bcast64(bvpb, bvprow_d, D)
                bcast64(lngb, lngrow_d, D)
                bcast64(lnbb, lnbrow_d, D)

                WS = p.ch("WS")
                ws_free = {0: 0, 1: 0}

                for wd, dstbuf, lhs in ((wihT_d, gis, sidm), (whhT_d, ghs, sinitT)):
                    for kc in range(8):
                        par = kc % 2
                        buf = wstr[:, par * 3072:(par + 1) * 3072]
                        p.wait("sync", WS, ws_free[par])
                        p.emit("sync", lambda e, o=buf, i=wd.ap()[kc]:
                               e.dma_start(out=o, in_=i), inc=BL, k=16)
                        p.wait("tensor", BL, BL.n)
                        if kc == 0:
                            p.wait("tensor", BV, BV.n)
                        for seg in range(6):
                            p.emit("tensor", _mm(pb[2 + seg][0:64, 0:512],
                                                 lhs[:, kc * 64:(kc + 1) * 64],
                                                 buf[:, seg * 512:(seg + 1) * 512],
                                                 kc == 0, kc == 7),
                                   inc=(WS if seg == 5 else None))
                        ws_free[par] = WS.n
                    vs_([(lambda e, dd=dstbuf, s=seg: e.tensor_copy(
                            dd[0:64, s * 512:(s + 1) * 512], pb[2 + s][0:64, 0:512]))
                         for seg in range(6)], waits=[(WS, WS.n)])

                for gate, dstg in ((0, rg), (1, zg)):
                    o = gate * D
                    vs_([lambda e, o=o: e.tensor_tensor(tg1[:], gis[0:64, o:o + D],
                                                        bihb[0:64, o:o + D], OP.add),
                         lambda e, o=o: e.tensor_tensor(tg2[:], ghs[0:64, o:o + D],
                                                        bhhb[0:64, o:o + D], OP.add),
                         lambda e: e.tensor_tensor(tg1[:], tg1[:], tg2[:], OP.add)])
                    if gate == 0:
                        act_fence(AF.Sigmoid)
                    ss_([lambda e, d=dstg: e.activation(d[:], tg1[:], AF.Sigmoid)],
                        waits=[(BV, BV.n)])
                    p.wait("vector", BS, BS.n)
                o = 2 * D
                vs_([lambda e: e.tensor_tensor(tg1[:], ghs[0:64, o:o + D],
                                               bhhb[0:64, o:o + D], OP.add),
                     lambda e: e.tensor_tensor(tg1[:], rg[:], tg1[:], OP.mult),
                     lambda e: e.tensor_tensor(tg2[:], gis[0:64, o:o + D],
                                               bihb[0:64, o:o + D], OP.add),
                     lambda e: e.tensor_tensor(tg1[:], tg1[:], tg2[:], OP.add)])
                ss_([lambda e: e.activation(ngb[:], tg1[:], AF.Tanh)], waits=[(BV, BV.n)])
                vs_([lambda e: e.tensor_scalar(tg1[:], zg[:], -1.0, 1.0, OP.mult, OP.add),
                     lambda e: e.tensor_tensor(tg1[:], tg1[:], ngb[:], OP.mult),
                     lambda e: e.tensor_tensor(tg2[:], zg[:], sS[:], OP.mult),
                     lambda e: e.tensor_tensor(snew[:], tg1[:], tg2[:], OP.add)],
                    waits=[(BS, BS.n), (BL, NB0)])
                if dbg_snew is not None:
                    p.wait("gpsimd", BV, BV.n)
                    p.emit("gpsimd", lambda e: e.dma_start(out=dbg_snew.ap(), in_=snew[:]),
                           inc=DBG, k=16)
                for ch_, n_ in [(BV, BV.n)]:
                    p.wait("tensor", ch_, n_)
                for g in range(8):
                    p.emit("tensor", _mm(pb[2][:, g * 64:(g + 1) * 64],
                                         snew[0:64, g * 128:(g + 1) * 128],
                                         eye[0:64, 0:64], g == 0, g == 7, is_t=True))
                p.emit("tensor", tclose(BT), inc=BT)
                vs_([lambda e: e.tensor_copy(sndm[:], pb[2][:, 0:512])], waits=[(BT, BT.n)])
                ts_([_mm(pb[3][0:64, 0:256], sndm[:, kc * 64:(kc + 1) * 64],
                         wqb[:, kc * 256:(kc + 1) * 256], kc == 0, kc == 7) for kc in range(8)]
                    + [_mm(pb[4][0:64, 0:256], sndm[:, kc * 64:(kc + 1) * 64],
                           wkb[:, kc * 256:(kc + 1) * 256], kc == 0, kc == 7) for kc in range(8)],
                    waits=[(BV, BV.n)])
                for kc in range(8):
                    par = kc % 2
                    vbuf = wstr[:, par * 3072:par * 3072 + 1024]
                    p.wait("sync", WS, ws_free[par])
                    p.emit("sync", lambda e, o=vbuf, i=wv_d.ap()[:, kc, :]:
                           e.dma_start(out=o, in_=i), inc=BL, k=16)
                    p.wait("tensor", BL, BL.n)
                    p.emit("tensor", _mm(pb[5][0:64, 0:512], sndm[:, kc * 64:(kc + 1) * 64],
                                         vbuf[:, 0:512], kc == 0, kc == 7))
                    p.emit("tensor", _mm(pb[6][0:64, 0:512], sndm[:, kc * 64:(kc + 1) * 64],
                                         vbuf[:, 512:1024], kc == 0, kc == 7),
                           inc=WS)
                    ws_free[par] = WS.n
                p.emit("tensor", lambda e: e.nop(), inc=BT)
                vs_([lambda e: e.tensor_tensor(qv[:], pb[3][0:64, 0:256], bqb[:], OP.add),
                     lambda e: e.tensor_tensor(kvb[:], pb[4][0:64, 0:256], bkb[:], OP.add),
                     lambda e: e.tensor_tensor(vv[:, 0:512], pb[5][0:64, 0:512],
                                               bvb[:, 0:512], OP.add),
                     lambda e: e.tensor_tensor(vv[:, 512:1024], pb[6][0:64, 0:512],
                                               bvb[:, 512:1024], OP.add)],
                    waits=[(BT, BT.n)])
                p.wait("tensor", BV, BV.n)
                for i in range(2):
                    p.emit("tensor", _mm(pb[2][:, i * 64:(i + 1) * 64],
                                         qv[0:64, i * 128:(i + 1) * 128],
                                         eye[0:64, 0:64], i == 0, i == 1, is_t=True))
                for i in range(2):
                    p.emit("tensor", _mm(pb[3][:, i * 64:(i + 1) * 64],
                                         kvb[0:64, i * 128:(i + 1) * 128],
                                         eye[0:64, 0:64], i == 0, i == 1, is_t=True))
                p.emit("tensor", tclose(BT), inc=BT)
                vs_([lambda e: e.tensor_copy(qdm[:], pb[2][:, 0:128]),
                     lambda e: e.tensor_copy(kdm[:], pb[3][:, 0:128])], waits=[(BT, BT.n)])
                ts_([_mm(pb[4][0:64, 0:64], qdm[:, i * 64:(i + 1) * 64],
                         kdm[:, i * 64:(i + 1) * 64], i == 0, i == 1) for i in range(2)],
                    waits=[(BV, BV.n)])
                ss_([lambda e: e.activation(sc64[:], pb[4][0:64, 0:64], AF.Copy, scale=0.0625)],
                    waits=[(BT, BT.n)])
                vs_([lambda e: e.tensor_reduce(cB[:, 0:1], sc64[:], AX.X, OP.max),
                     lambda e: e.tensor_scalar(cB[:, 1:2], cB[:, 0:1], -1.0, None, OP.mult)],
                    waits=[(BS, BS.n)])
                act_fence(AF.Exp)
                ss_([lambda e: e.activation(at64[:], sc64[:], AF.Exp, bias=cB[:, 1:2])],
                    waits=[(BV, BV.n)])
                vs_([lambda e: e.tensor_reduce(cB[:, 2:3], at64[:], AX.X, OP.add),
                     lambda e: e.reciprocal(cB[:, 3:4], cB[:, 2:3]),
                     lambda e: e.tensor_scalar(at64[:], at64[:], cB[:, 3:4], None, OP.mult)],
                    waits=[(BS, BS.n)])
                p.wait("tensor", BV, BV.n)
                p.emit("tensor", _mm(pb[2][0:64, 0:64], at64[:], eye[0:64, 0:64],
                                     True, True, is_t=True))
                p.emit("tensor", tclose(BT), inc=BT)
                vs_([lambda e: e.tensor_copy(sc64[:], pb[2][0:64, 0:64])], waits=[(BT, BT.n)])
                ts_([_mm(pb[5][0:64, 0:512], sc64[:], vv[:, 0:512], True, True),
                     _mm(pb[6][0:64, 0:512], sc64[:], vv[:, 512:1024], True, True)],
                    waits=[(BV, BV.n)])
                vs_([lambda e: e.tensor_tensor(tg2[:, 0:512], snew[0:64, 0:512],
                                               pb[5][0:64, 0:512], OP.add),
                     lambda e: e.tensor_tensor(tg2[:, 512:1024], snew[0:64, 512:1024],
                                               pb[6][0:64, 0:512], OP.add),
                     lambda e: e.tensor_reduce(cB[:, 0:1], tg2[:], AX.X, OP.add)],
                    waits=[(BT, BT.n)])
                ss_([lambda e: e.activation(tg1[:], tg2[:], AF.Square, accum_out=cB[:, 1:2])],
                    waits=[(BV, BV.n)])
                vs_([lambda e: e.tensor_scalar(cB[:, 4:5], cB[:, 0:1], INV_D, None, OP.mult),
                     lambda e: e.tensor_scalar(cB[:, 5:6], cB[:, 1:2], INV_D, None, OP.mult),
                     lambda e: e.tensor_tensor(cB[:, 2:3], cB[:, 4:5], cB[:, 4:5], OP.mult),
                     lambda e: e.tensor_tensor(cB[:, 2:3], cB[:, 5:6], cB[:, 2:3], OP.subtract),
                     lambda e: e.tensor_scalar(cB[:, 2:3], cB[:, 2:3], 1e-5, None, OP.add)],
                    waits=[(BS, BS.n)])
                ss_([lambda e: e.activation(cB[:, 6:7], cB[:, 2:3], AF.Ln),
                     lambda e: e.activation(cB[:, 3:4], cB[:, 6:7], AF.Exp, scale=-0.5)],
                    waits=[(BV, BV.n)])
                vs_([lambda e: e.tensor_scalar(yv[:], tg2[:], cB[:, 4:5], cB[:, 3:4],
                                               OP.subtract, OP.mult),
                     lambda e: e.tensor_tensor(tg1[:], yv[:], lngb[:], OP.mult),
                     lambda e: e.tensor_tensor(yv[:], tg1[:], lnbb[:], OP.add)],
                    waits=[(BS, BS.n)])
                p.wait("tensor", BV, BV.n)
                for g in range(8):
                    p.emit("tensor", _mm(pb[2][:, g * 64:(g + 1) * 64],
                                         yv[0:64, g * 128:(g + 1) * 128],
                                         eye[0:64, 0:64], g == 0, g == 7, is_t=True))
                p.emit("tensor", tclose(BT), inc=BT)
                vs_([lambda e: e.tensor_copy(ydm[:], pb[2][:, 0:512])], waits=[(BT, BT.n)])
                p.wait("tensor", BV, BV.n)
                for kc in range(8):
                    par = kc % 2
                    vbuf = wstr[:, par * 3072:par * 3072 + 1024]
                    p.wait("sync", WS, ws_free[par])
                    p.emit("sync", lambda e, o=vbuf, i=wvp_d.ap()[:, kc, :]:
                           e.dma_start(out=o, in_=i), inc=BL, k=16)
                    p.wait("tensor", BL, BL.n)
                    p.emit("tensor", _mm(pb[5][0:64, 0:512], ydm[:, kc * 64:(kc + 1) * 64],
                                         vbuf[:, 0:512], kc == 0, kc == 7))
                    p.emit("tensor", _mm(pb[6][0:64, 0:512], ydm[:, kc * 64:(kc + 1) * 64],
                                         vbuf[:, 512:1024], kc == 0, kc == 7),
                           inc=WS)
                    ws_free[par] = WS.n
                p.emit("tensor", lambda e: e.nop(), inc=BT)
                vs_([lambda e: e.tensor_tensor(svb[0:64, 0:512], pb[5][0:64, 0:512],
                                               bvpb[:, 0:512], OP.add),
                     lambda e: e.tensor_tensor(svb[0:64, 512:1024], pb[6][0:64, 0:512],
                                               bvpb[:, 512:1024], OP.add)],
                    waits=[(BT, BT.n)])
                if dbg_sv is not None:
                    p.wait("gpsimd", BV, BV.n)
                    p.emit("gpsimd", lambda e: e.dma_start(out=dbg_sv.ap(), in_=svb[:]),
                           inc=DBG, k=16)
                if dbg_bp is not None:
                    p.wait("gpsimd", BV, BV.n)
                    p.wait("gpsimd", BS, BS.n)
                    taps = [(gis[0:64, :], 3072), (ghs[0:64, :], 3072), (rg[:], 1024),
                            (zg[:], 1024), (ngb[:], 1024), (snew[:], 1024),
                            (qv[:], 256), (kvb[:], 256), (vv[:], 1024),
                            (sndm[0:64, :], 512), (at64[:], 64), (yv[:], 1024), (cB[:], 8), (tg2[:], 1024)]
                    for i_, (ap_, w_) in enumerate(taps):
                        p.emit("gpsimd", lambda e, i_=i_, a_=ap_, w_=w_: e.dma_start(
                            out=dbg_bp.ap()[i_][:, 0:w_], in_=a_), inc=DBG, k=16)
                p.barrier()
                p.flush(block)

            with ExitStack() as cscope:
                csb = lambda name, shape: cscope.enter_context(nc.sbuf_tensor(name, shape, F32))
                csbb = lambda name, shape: cscope.enter_context(
                    nc.sbuf_tensor(name, shape, BF16))
                x2t = csb("x2t", [128, 8 * TC])
                finth = csbb("finth", [128, 8 * TC])
                fintl = csbb("fintl", [128, 8 * TC])
                Pth = csbb("Pth", [128, 16 * TC])
                Ptl = csbb("Ptl", [128, 16 * TC])
                ust = csb("ust", [128, TC])
                st_ = csb("st_", [128, 8 * TC])
                osth = csbb("osth", [128, 8 * TC])
                ostl = csbb("ostl", [128, 8 * TC])
                gt = csb("gt", [128, TC])
                sgt = csb("sgt", [128, 8 * TC])
                finb = csb("finb", [128, 2 * D])
                xmt = csb("xmt", [128, 2 * D])
                w1b2h = csbb("w1b2h", [128, 2 * 1024])
                w1b2l = csbb("w1b2l", [128, 2 * 1024])
                w3b2h = csbb("w3b2h", [128, 2 * 1024])
                w3b2l = csbb("w3b2l", [128, 2 * 1024])
                w2b2h = csbb("w2b2h", [128, 2 * 2048])
                w2b2l = csbb("w2b2l", [128, 2 * 2048])
                wob2h = csbb("wob2h", [128, 2 * 1024])
                wob2l = csbb("wob2l", [128, 2 * 1024])
                rowC = csb("rowC", [1, 512])
                sqsc = csb("sqsc", [128, TC])
                ptmp = csb("ptmp", [128, TC])
                r2rows = csb("r2rows", [1, 4 * TC])
                gs_ = csb("gs_", [128, TC])
                us2 = csb("us2", [128, TC])
                asmt = csb("asmt", [64, 2 * TC])

                CUS = p.ch("CUS")
                LD2 = p.ch("LD2"); LW13 = p.ch("LW13"); LW2 = p.ch("LW2")
                LWO = p.ch("LWO"); LXM = p.ch("LXM")
                CSQ = p.ch("CSQ"); CR2M = p.ch("CR2M"); CR2V = p.ch("CR2V")
                CR2S = p.ch("CR2S"); CBC = p.ch("CBC"); CFIN = p.ch("CFIN")
                CU = p.ch("CU"); CSIL = p.ch("CSIL"); CV = p.ch("CV"); CP = p.ch("CP")
                CW2G = p.ch("CW2G"); CSEV = p.ch("CSEV"); COS = p.ch("COS")
                COSE = p.ch("COSE"); CWOT = p.ch("CWOT"); CGE = p.ch("CGE")
                CSG = p.ch("CSG"); CTRF = p.ch("CTRF"); CFV = p.ch("CFV")
                COUT = p.ch("COUT"); LAS = p.ch("LAS")


                for tt in range(4):
                    if tt == 0:
                        p.wait("sync", SPV, SPV.n)
                    p.wait("sync", CSQ, 8 * tt)
                    p.emit("sync", lambda e, tt=tt: e.dma_start(
                        out=x2t[:].rearrange("p (g t) -> p g t", g=8),
                        in_=x2s_d.ap()[tt].rearrange("g p t -> p g t")), inc=LD2, k=16)
                    p.wait("scalar", LD2, 16 * (tt + 1))
                    for g in range(8):
                        if g > 0:
                            p.wait("scalar", CR2M, 8 * tt + g)
                        p.emit("scalar", lambda e, g=g: e.activation(
                            sqsc[:], x2t[:, g * TC:(g + 1) * TC], AF.Square), inc=CSQ)
                    for g in range(8):
                        p.wait("tensor", CSQ, 8 * tt + g + 1)
                        p.emit("tensor", _mm(pb[0][0:1, 0:512], onesc[:], sqsc[:],
                                             g == 0, g == 7), inc=CR2M)
                    p.wait("vector", CR2M, 8 * (tt + 1))
                    p.emit("vector", lambda e, tt=tt: e.tensor_scalar(
                        r2rows[0:1, tt * TC:(tt + 1) * TC], pb[0][0:1, 0:512],
                        INV_D, 1e-6, OP.mult, OP.add), inc=CR2V)
                p.wait("scalar", CR2V, 4)
                p.emit("scalar", lambda e: e.activation(r2rows[:], r2rows[:], AF.Ln))
                p.emit("scalar", lambda e: e.activation(r2rows[:], r2rows[:], AF.Exp, scale=-0.5),
                       inc=CR2S)
                act_fence(AF.Sigmoid)

                for tt in range(4):
                    if tt == 0:
                        p.wait("sync", CSQ, 32)
                    p.wait("sync", CSEV, 16 * tt)
                    p.wait("sync", CFIN, tt)
                    p.emit("sync", lambda e, tt=tt: e.dma_start(
                        out=x2t[:].rearrange("p (g t) -> p g t", g=8),
                        in_=x2s_d.ap()[tt].rearrange("g p t -> p g t")), inc=LD2, k=16)
                    p.wait("sync", COS, 8 * (tt - 1))
                    p.emit("sync", lambda e, tt=tt: e.dma_start(
                        out=asmt[0:64, (tt % 2) * TC:(tt % 2 + 1) * TC],
                        in_=alpha_d.ap()[tt]), inc=LAS, k=16)
                    p.wait("tensor", CR2S, 1)
                    if tt > 0:
                        p.wait("tensor", CFIN, tt)
                    p.emit("tensor", _mm(pb[1][:, 0:512], onesr[:],
                                         r2rows[0:1, tt * TC:(tt + 1) * TC], True, True),
                           inc=CBC)
                    p.wait("vector", CBC, tt + 1)
                    p.wait("vector", LD2, 16 * (4 + tt + 1))
                    if tt == 0:
                        p.wait("vector", SPW, SPW.n)
                    for g in range(8):
                        p.emit("vector", lambda e, g=g: e.scalar_tensor_tensor(
                            ptmp[:], x2t[:, g * TC:(g + 1) * TC],
                            n2c[:, g:g + 1], pb[1][:, 0:512], OP.mult, OP.mult))
                        p.emit("vector", lambda e, g=g: e.tensor_copy(
                            finth[:, g * TC:(g + 1) * TC], ptmp[:]))
                        p.emit("vector", lambda e, g=g: e.tensor_tensor(
                            fintl[:, g * TC:(g + 1) * TC], ptmp[:],
                            finth[:, g * TC:(g + 1) * TC], OP.subtract),
                            inc=(CFIN if g == 7 else None))
                    if dbg_fin0 is not None and tt == 0:
                        p.wait("gpsimd", CFIN, 1)
                        p.emit("gpsimd", lambda e: e.dma_start(out=dbg_fin0.ap(), in_=finth[:]),
                               inc=DBG, k=16)

                    cu_base = CU.n
                    sev_base = CSEV.n
                    w2g_base = CW2G.n
                    for h in range(2):
                        for i in range(16):
                            idx = h * 16 + i
                            par_ = idx % 2
                            wb1h = w1b2h[:, par_ * 1024:(par_ + 1) * 1024]
                            wb1l = w1b2l[:, par_ * 1024:(par_ + 1) * 1024]
                            wb3h = w3b2h[:, par_ * 1024:(par_ + 1) * 1024]
                            wb3l = w3b2l[:, par_ * 1024:(par_ + 1) * 1024]
                            dff = h * 16 + i
                            if tt == 0:
                                p.wait("sync", SPW, 128 * (dff + 1))
                            if idx >= 2:
                                p.wait("sync", CU, cu_base + idx - 1)
                            p.emit("sync", lambda e, o=wb1h, d=dff: e.dma_start(
                                out=o, in_=w1h_d.ap()[d]), inc=LW13, k=16)
                            p.emit("sync", lambda e, o=wb1l, d=dff: e.dma_start(
                                out=o, in_=w1l_d.ap()[d]), inc=LW13, k=16)
                            if idx >= 2:
                                p.wait("sync", CV, cu_base + idx - 1)
                            p.emit("sync", lambda e, o=wb3h, d=dff: e.dma_start(
                                out=o, in_=w3h_d.ap()[d]), inc=LW13, k=16)
                            p.emit("sync", lambda e, o=wb3l, d=dff: e.dma_start(
                                out=o, in_=w3l_d.ap()[d]), inc=LW13, k=16)
                            p.wait("tensor", LW13, LW13.n - 32)
                            p.wait("tensor", CUS, cu_base + idx)
                            if tt > 0 and idx == 0:
                                p.wait("tensor", CFV, 4 * tt)
                            if idx == 0:
                                p.wait("tensor", CFIN, tt + 1)
                            for kc in range(8):
                                p.emit("tensor", _mm(pb[2][:, 0:512],
                                                     wb1h[:, kc * 128:(kc + 1) * 128],
                                                     finth[:, kc * TC:(kc + 1) * TC],
                                                     kc == 0, False))
                                p.emit("tensor", _mm(pb[2][:, 0:512],
                                                     wb1h[:, kc * 128:(kc + 1) * 128],
                                                     fintl[:, kc * TC:(kc + 1) * TC],
                                                     False, False))
                            for kc in range(8):
                                p.emit("tensor", _mm(pb[2][:, 0:512],
                                                     wb1l[:, kc * 128:(kc + 1) * 128],
                                                     finth[:, kc * TC:(kc + 1) * TC],
                                                     False, kc == 7),
                                       inc=(CU if kc == 7 else None))
                            p.wait("scalar", CU, cu_base + idx + 1)
                            p.wait("scalar", CUS, cu_base + idx)
                            p.emit("scalar", lambda e: e.activation(us2[:], pb[2][:, 0:512],
                                                                    AF.Sigmoid), inc=CSIL)
                            p.wait("tensor", LW13, LW13.n)
                            p.wait("tensor", CP, cu_base + idx)
                            for kc in range(8):
                                p.emit("tensor", _mm(pb[3][:, 0:512],
                                                     wb3h[:, kc * 128:(kc + 1) * 128],
                                                     finth[:, kc * TC:(kc + 1) * TC],
                                                     kc == 0, False))
                                p.emit("tensor", _mm(pb[3][:, 0:512],
                                                     wb3h[:, kc * 128:(kc + 1) * 128],
                                                     fintl[:, kc * TC:(kc + 1) * TC],
                                                     False, False))
                            for kc in range(8):
                                p.emit("tensor", _mm(pb[3][:, 0:512],
                                                     wb3l[:, kc * 128:(kc + 1) * 128],
                                                     finth[:, kc * TC:(kc + 1) * TC],
                                                     False, kc == 7),
                                       inc=(CV if kc == 7 else None))
                            p.wait("vector", CSIL, cu_base + idx + 1)
                            p.emit("vector", lambda e: e.tensor_tensor(
                                ust[:], us2[:], pb[2][:, 0:512], OP.mult), inc=CUS)
                            p.wait("vector", CV, cu_base + idx + 1)
                            if h == 1 and i == 0:
                                p.wait("vector", CW2G, w2g_base + 8)
                            p.emit("vector", lambda e: e.tensor_tensor(
                                ptmp[:], ust[:], pb[3][:, 0:512], OP.mult))
                            p.emit("vector", lambda e, i=i: e.tensor_copy(
                                Pth[:, i * TC:(i + 1) * TC], ptmp[:]))
                            p.emit("vector", lambda e, i=i: e.tensor_tensor(
                                Ptl[:, i * TC:(i + 1) * TC], ptmp[:],
                                Pth[:, i * TC:(i + 1) * TC], OP.subtract),
                                inc=CP)
                        for g in range(8):
                            gi_ = h * 8 + g
                            wb2h = w2b2h[:, (g % 2) * 2048:(g % 2 + 1) * 2048]
                            wb2l = w2b2l[:, (g % 2) * 2048:(g % 2 + 1) * 2048]
                            if tt == 0 and gi_ < 2:
                                p.wait("sync", SPW, SPW.n)
                            if gi_ >= 2:
                                p.wait("sync", CW2G, w2g_base + gi_ - 1)
                            p.emit("sync", lambda e, o=wb2h, g=g, h=h: e.dma_start(
                                out=o, in_=w2h_d.ap()[g][:, h * 2048:(h + 1) * 2048]),
                                inc=LW2, k=16)
                            p.emit("sync", lambda e, o=wb2l, g=g, h=h: e.dma_start(
                                out=o, in_=w2l_d.ap()[g][:, h * 2048:(h + 1) * 2048]),
                                inc=LW2, k=16)
                            p.wait("tensor", LW2, LW2.n)
                            p.wait("tensor", CP, cu_base + (h + 1) * 16)
                            if gi_ >= 2:
                                p.wait("tensor", CSEV, sev_base + gi_ - 1)
                            bank = pb[4 + g % 2]
                            for di in range(16):
                                p.emit("tensor", _mm(bank[:, 0:512],
                                                     wb2h[:, di * 128:(di + 1) * 128],
                                                     Pth[:, di * TC:(di + 1) * TC],
                                                     di == 0, False))
                                p.emit("tensor", _mm(bank[:, 0:512],
                                                     wb2h[:, di * 128:(di + 1) * 128],
                                                     Ptl[:, di * TC:(di + 1) * TC],
                                                     False, False))
                            for di in range(16):
                                p.emit("tensor", _mm(bank[:, 0:512],
                                                     wb2l[:, di * 128:(di + 1) * 128],
                                                     Pth[:, di * TC:(di + 1) * TC],
                                                     False, di == 15),
                                       inc=(CW2G if di == 15 else None))
                            p.wait("vector", CW2G, w2g_base + gi_ + 1)
                            if h == 0:
                                p.emit("vector", lambda e, g=g, b=bank: e.tensor_tensor(
                                    st_[:, g * TC:(g + 1) * TC], b[:, 0:512],
                                    x2t[:, g * TC:(g + 1) * TC], OP.add), inc=CSEV)
                            else:
                                p.emit("vector", lambda e, g=g, b=bank: e.tensor_tensor(
                                    st_[:, g * TC:(g + 1) * TC], b[:, 0:512],
                                    st_[:, g * TC:(g + 1) * TC], OP.add), inc=CSEV)
                    if dbg_s0 is not None and tt == 0:
                        p.wait("gpsimd", CSEV, 16)
                        p.emit("gpsimd", lambda e: e.dma_start(out=dbg_s0.ap(), in_=st_[:]),
                               inc=DBG, k=16)

                    for g in range(8):
                        if g == 0:
                            p.wait("tensor", CGE, 8 * tt)
                            p.wait("tensor", LAS, 16 * (tt + 1))
                        p.wait("tensor", COSE, 8 * tt + g - 1)
                        p.emit("tensor", _mm(pb[6 + g % 2][:, 0:512],
                                             svb[0:64, g * 128:(g + 1) * 128],
                                             asmt[0:64, (tt % 2) * TC:(tt % 2 + 1) * TC],
                                             True, True), inc=COS)
                        p.wait("vector", COS, 8 * tt + g + 1)
                        p.wait("vector", CWOT, 8 * tt)
                        p.emit("vector", lambda e, g=g: e.tensor_copy(ptmp[:],
                            pb[6 + g % 2][:, 0:512]))
                        p.emit("vector", lambda e, g=g: e.tensor_copy(
                            osth[:, g * TC:(g + 1) * TC], ptmp[:]))
                        p.emit("vector", lambda e, g=g: e.tensor_tensor(
                            ostl[:, g * TC:(g + 1) * TC], ptmp[:],
                            osth[:, g * TC:(g + 1) * TC], OP.subtract), inc=COSE)
                    for ec in range(8):
                        wobh = wob2h[:, (ec % 2) * 1024:(ec % 2 + 1) * 1024]
                        wobl = wob2l[:, (ec % 2) * 1024:(ec % 2 + 1) * 1024]
                        if tt == 0 and ec < 2:
                            p.wait("sync", SPW, SPW.n)
                        if 8 * tt + ec >= 2:
                            p.wait("sync", CWOT, 8 * tt + ec - 1)
                        p.emit("sync", lambda e, o=wobh, ec=ec: e.dma_start(
                            out=o, in_=woh_d.ap()[ec]), inc=LWO, k=16)
                        p.emit("sync", lambda e, o=wobl, ec=ec: e.dma_start(
                            out=o, in_=wol_d.ap()[ec]), inc=LWO, k=16)
                        p.wait("tensor", LWO, LWO.n)
                        p.wait("tensor", COSE, 8 * (tt + 1))
                        if ec >= 2:
                            p.wait("tensor", CGE, 8 * tt + ec - 1)
                        for kc in range(8):
                            p.emit("tensor", _mm(pb[6 + ec % 2][:, 0:512],
                                                 wobh[:, kc * 128:(kc + 1) * 128],
                                                 osth[:, kc * TC:(kc + 1) * TC],
                                                 kc == 0, False))
                            p.emit("tensor", _mm(pb[6 + ec % 2][:, 0:512],
                                                 wobh[:, kc * 128:(kc + 1) * 128],
                                                 ostl[:, kc * TC:(kc + 1) * TC],
                                                 False, False))
                        for kc in range(8):
                            p.emit("tensor", _mm(pb[6 + ec % 2][:, 0:512],
                                                 wobl[:, kc * 128:(kc + 1) * 128],
                                                 osth[:, kc * TC:(kc + 1) * TC],
                                                 False, kc == 7),
                                   inc=(CWOT if kc == 7 else None))
                        p.wait("scalar", CWOT, 8 * tt + ec + 1)
                        p.wait("scalar", CSG, 8 * tt + ec)
                        p.emit("scalar", lambda e, ec=ec: e.activation(
                            gt[:], pb[6 + ec % 2][:, 0:512], AF.Identity,
                            bias=boc[:, ec:ec + 1]))
                        p.emit("scalar", lambda e: e.activation(gs_[:], gt[:], AF.Sigmoid),
                               inc=CGE)
                        p.wait("vector", CGE, 8 * tt + ec + 1)
                        p.wait("vector", CSEV, sev_base + 16)
                        if tt > 0 and ec == 0:
                            p.wait("vector", CTRF, 4 * tt)
                        p.emit("vector", lambda e: e.tensor_tensor(gt[:], gt[:], gs_[:], OP.mult))
                        p.emit("vector", lambda e, ec=ec: e.tensor_tensor(
                            sgt[:, ec * TC:(ec + 1) * TC], gt[:], st_[:, ec * TC:(ec + 1) * TC],
                            OP.mult), inc=CSG)
                    if dbg_g0 is not None and tt == 0:
                        p.wait("gpsimd", CSG, 8)
                        p.emit("gpsimd", lambda e: e.dma_start(out=dbg_g0.ap(), in_=sgt[:]),
                               inc=DBG, k=16)
                    for j in range(4):
                        jj = 4 * tt + j
                        xm = xmt[:, (j % 2) * D:(j % 2 + 1) * D]
                        if jj >= 2:
                            p.wait("sync", CFV, jj - 1)
                        p.emit("sync", lambda e, o=xm, tt=tt, j=j: e.dma_start(
                            out=o, in_=xin.ap()[2048 + tt * TC + j * 128:
                                                2048 + tt * TC + (j + 1) * 128, :]),
                            inc=LXM, k=16)
                        p.wait("tensor", CSG, 8 * (tt + 1))
                        p.wait("tensor", CFV, jj)
                        for g in range(8):
                            bank = pb[2 + g // 4]
                            p.emit("tensor", _mm(bank[:, (g % 4) * 128:(g % 4 + 1) * 128],
                                                 sgt[:, g * TC + j * 128:g * TC + (j + 1) * 128],
                                                 eye[:], g % 4 == 0, g % 4 == 3, is_t=True))
                        p.emit("tensor", tclose(CTRF), inc=CTRF)
                        p.wait("vector", CTRF, jj + 1)
                        p.wait("vector", LXM, 16 * (jj + 1))
                        if jj >= 2:
                            p.wait("vector", COUT, 16 * (jj - 1))
                        fb = finb[:, (j % 2) * D:(j % 2 + 1) * D]
                        p.emit("vector", lambda e, fb=fb, xm=xm: e.tensor_tensor(
                            fb[:, 0:512], pb[2][:, 0:512], xm[:, 0:512], OP.add))
                        p.emit("vector", lambda e, fb=fb, xm=xm: e.tensor_tensor(
                            fb[:, 512:1024], pb[3][:, 0:512], xm[:, 512:1024], OP.add), inc=CFV)
                        p.wait("gpsimd", CFV, jj + 1)
                        p.emit("gpsimd", lambda e, tt=tt, j=j, fb=fb: e.dma_start(
                            out=out_d.ap()[tt * TC + j * 128:tt * TC + (j + 1) * 128, :],
                            in_=fb), inc=COUT, k=16)

                p.wait("gpsimd", COUT, 16 * 16)
                if dbg:
                    p.wait("gpsimd", DBG, DBG.n)
                p.flush(block)

    return nc


def _stage(inputs, core):
    b, h = core // 2, core % 2
    C = np.ascontiguousarray
    x = inputs["L0_attn_x"][b]
    if h == 0:
        xr = np.concatenate([x[2048:], x[:2048]], axis=0)
    else:
        xr = x
    blk = lambda w, K, E: C(w.reshape(K // 128, 128, E // 128, 128).transpose(1, 0, 2, 3))
    m = {
        "xin": C(xr),
        "wg": blk(inputs["L0_attn_Wg"], 1024, 1024),
        "wc": blk(inputs["L0_attn_Wc"], 1024, 1024),
        "w1a": blk(inputs["L0_ffn_w1"][:1024], 1024, 512),
        "w1b": C(inputs["L0_ffn_w1"][1024:].reshape(8, 128, 512)),
        "w2r": C(inputs["L0_ffn_w2"].reshape(4, 128, 64)),
        "b1row": C(inputs["L0_ffn_b1"].reshape(1, 512)),
        "b2col": C(inputs["L0_ffn_b2"].reshape(64, 1)),
        "tau": C(inputs["L0_ffn_tau"].reshape(1, 1)),
        "n1col": C(inputs["L0_attn_n1_w"].reshape(8, 128).T),
        "n2col": C(inputs["L0_attn_n2_w"].reshape(8, 128).T),
        "bgcol": C(inputs["L0_attn_bg"].reshape(8, 128).T),
        "bccol": C(inputs["L0_attn_bc"].reshape(8, 128).T),
        "bocol": C(inputs["L0_ffn_bo"].reshape(8, 128).T),
        "sinitT": C(inputs["L0_ffn_slot_init"].T.reshape(8, 128, 64).transpose(1, 0, 2)),
        "sinit": C(inputs["L0_ffn_slot_init"]),
        "wihT": C(inputs["L0_ffn_Wih"].T.reshape(8, 128, 3072)),
        "whhT": C(inputs["L0_ffn_Whh"].T.reshape(8, 128, 3072)),
        "bihrow": C(inputs["L0_ffn_bih"].reshape(1, 3072)),
        "bhhrow": C(inputs["L0_ffn_bhh"].reshape(1, 3072)),
        "wq": C(inputs["L0_ffn_wq"].reshape(8, 128, 256).transpose(1, 0, 2)),
        "wk": C(inputs["L0_ffn_wk"].reshape(8, 128, 256).transpose(1, 0, 2)),
        "wv": C(inputs["L0_ffn_wv"].reshape(8, 128, 1024).transpose(1, 0, 2)),
        "wvp": C(inputs["L0_ffn_wvp"].reshape(8, 128, 1024).transpose(1, 0, 2)),
        "bqrow": C(inputs["L0_ffn_bq"].reshape(1, 256)),
        "bkrow": C(inputs["L0_ffn_bk"].reshape(1, 256)),
        "bvrow": C(inputs["L0_ffn_bv"].reshape(1, 1024)),
        "bvprow": C(inputs["L0_ffn_bvp"].reshape(1, 1024)),
        "lngrow": C(inputs["L0_ffn_ln_g"].reshape(1, 1024)),
        "lnbrow": C(inputs["L0_ffn_ln_b"].reshape(1, 1024)),
        "wo": C(inputs["L0_ffn_wo"].reshape(8, 128, 8, 128).transpose(2, 1, 0, 3)),
        "w1f": C(inputs["L0_attn_W1"].reshape(8, 128, 32, 128).transpose(2, 1, 0, 3)),
        "w3f": C(inputs["L0_attn_W3"].reshape(8, 128, 32, 128).transpose(2, 1, 0, 3)),
        "w2f": C(inputs["L0_attn_W2"].reshape(32, 128, 8, 128).transpose(2, 1, 0, 3)),
        "eye": np.eye(128, dtype=np.float32),
        "onesc": np.ones((128, 1), np.float32),
        "onesr": np.ones((1, 128), np.float32),
        "flag": np.full((128, 1), float(h), np.float32),
    }
    return {k: np.asarray(v, dtype=np.float32) for k, v in m.items()}


def kernel(**inputs):
    nc = build_program()
    in_maps = [_stage(inputs, core) for core in range(8)]
    res = run_bass_kernel_spmd(nc, in_maps, core_ids=list(range(8)))
    out = np.empty((4, 4096, 1024), dtype=np.float32)
    for core in range(8):
        b, h = core // 2, core % 2
        out[b, h * 2048:(h + 1) * 2048] = res.results[core]["out"]
    return out


# version 06
# speedup vs baseline: 1.0568x; 1.0568x over previous
import numpy as np
from contextlib import ExitStack

import concourse.bass as bass
import concourse.mybir as mybir
from concourse.bass_utils import run_bass_kernel_spmd

F32 = mybir.dt.float32
BF16 = mybir.dt.bfloat16
AF = mybir.ActivationFunctionType
OP = mybir.AluOpType
AX = mybir.AxisListType

D = 1024
T = 4096
TC = 512
NCH = 8
MY0 = 4
NDFF = 32
NS = 64
RS2 = 0.7071067811865476
INV_D = 1.0 / D
ENGS = ("tensor", "vector", "scalar", "sync", "gpsimd")


class Ch:
    __slots__ = ("sem", "n")

    def __init__(self, sem):
        self.sem = sem
        self.n = 0


class Prog:
    def __init__(self, nc, stack):
        self.nc = nc
        self.stack = stack
        self.q = {e: [] for e in ENGS}
        self._ch = {}

    def ch(self, name):
        c = self._ch.get(name)
        if c is None:
            c = Ch(self.stack.enter_context(self.nc.semaphore(name)))
            self._ch[name] = c
        return c

    def wait(self, eng, ch, n=None):
        v = ch.n if n is None else n
        if v <= 0:
            return
        self.q[eng].append(lambda e, s=ch.sem, vv=v: e.wait_ge(s, vv))

    def emit(self, eng, fn, inc=None, k=1):
        if eng == "vector":
            self.q[eng].append(lambda e: e.engine_nop())
            self.q[eng].append(lambda e: e.engine_nop())
            self.q[eng].append(lambda e: e.engine_nop())
        if inc is None:
            self.q[eng].append(fn)
        elif k == 1:
            inc.n += k
            self.q[eng].append(fn)
            self.q[eng].append(lambda e, s=inc.sem, kk=k: e.drain().then_inc(s, kk))
        else:
            inc.n += k
            self.q[eng].append(lambda e, f=fn, s=inc.sem, kk=k: f(e).then_inc(s, kk))

    act_trailer = None

    def barrier(self):
        ch = self.ch("bar")
        for eng in ENGS:
            self.q[eng].append(lambda e, s=ch.sem: e.drain().then_inc(s, 1))
        ch.n += len(ENGS)
        for eng in ENGS:
            self.wait(eng, ch)

    def flush(self, block):
        for eng in ENGS:
            lst = self.q[eng]
            if not lst:
                continue
            self.q[eng] = []
            getattr(block, eng)(lambda e, L=lst: [f(e) for f in L])


def _mm(out_ap, lhsT_ap, rhs_ap, start, stop, is_t=False):
    return lambda e: e.matmul(out_ap, lhsT_ap, rhs_ap, start=start, stop=stop,
                              is_transpose=is_t)


def build_program(debug_outputs=()):
    nc = bass.Bass("TRN2", target_bir_lowering=False)
    dt = lambda name, shape, kind="ExternalInput": nc.dram_tensor(name, shape, F32, kind=kind)

    xin = dt("xin", [T, D])
    wg_d = dt("wg", [128, 8, 8, 128])
    wc_d = dt("wc", [128, 8, 8, 128])
    w1a_d = dt("w1a", [128, 8, 4, 128])
    w1b_d = dt("w1b", [8, 128, 512])
    w2r_d = dt("w2r", [4, 128, 64])
    b1row_d = dt("b1row", [1, 512])
    b2col_d = dt("b2col", [64, 1])
    tau_d = dt("tau", [1, 1])
    n1col_d = dt("n1col", [128, 8])
    n2col_d = dt("n2col", [128, 8])
    bgcol_d = dt("bgcol", [128, 8])
    bccol_d = dt("bccol", [128, 8])
    bocol_d = dt("bocol", [128, 8])
    sinitT_d = dt("sinitT", [128, 8, 64])
    sinit_d = dt("sinit", [64, D])
    wihT_d = dt("wihT", [8, 128, 3072])
    whhT_d = dt("whhT", [8, 128, 3072])
    bihrow_d = dt("bihrow", [1, 3072])
    bhhrow_d = dt("bhhrow", [1, 3072])
    wq_d = dt("wq", [128, 8, 256])
    wk_d = dt("wk", [128, 8, 256])
    wv_d = dt("wv", [128, 8, D])
    wvp_d = dt("wvp", [128, 8, D])
    bqrow_d = dt("bqrow", [1, 256])
    bkrow_d = dt("bkrow", [1, 256])
    bvrow_d = dt("bvrow", [1, D])
    bvprow_d = dt("bvprow", [1, D])
    lngrow_d = dt("lngrow", [1, D])
    lnbrow_d = dt("lnbrow", [1, D])
    wo_d = dt("wo", [8, 128, 8, 128])
    w1f_d = dt("w1f", [32, 128, 8, 128])
    w3f_d = dt("w3f", [32, 128, 8, 128])
    w2f_d = dt("w2f", [8, 128, 32, 128])
    eye_d = dt("eye", [128, 128])
    onesc_d = dt("onesc", [128, 1])
    onesr_d = dt("onesr", [1, 128])
    flag_d = dt("flag", [128, 1])
    out_d = dt("out", [2048, D], kind="ExternalOutput")

    w1h_d = nc.dram_tensor("w1h", [32, 128, 1024], BF16, kind="Internal")
    w1l_d = nc.dram_tensor("w1l", [32, 128, 1024], BF16, kind="Internal")
    w3h_d = nc.dram_tensor("w3h", [32, 128, 1024], BF16, kind="Internal")
    w3l_d = nc.dram_tensor("w3l", [32, 128, 1024], BF16, kind="Internal")
    w2h_d = nc.dram_tensor("w2h", [8, 128, 4096], BF16, kind="Internal")
    w2l_d = nc.dram_tensor("w2l", [8, 128, 4096], BF16, kind="Internal")
    woh_d = nc.dram_tensor("woh", [8, 128, 1024], BF16, kind="Internal")
    wol_d = nc.dram_tensor("wol", [8, 128, 1024], BF16, kind="Internal")

    alpha_d = nc.dram_tensor("alphasm", [4, 64, TC], F32, kind="Internal")
    x2kind = "ExternalOutput" if "x2" in debug_outputs else "Internal"
    x2s_d = nc.dram_tensor("dbg_x2" if x2kind == "ExternalOutput" else "x2scr",
                           [4, 8, 128, TC], F32, kind=x2kind)

    dbg = {}
    def dbg_out(name, shape):
        if name in debug_outputs:
            h = nc.dram_tensor("dbg_" + name, shape, F32, kind="ExternalOutput")
            dbg[name] = h
            return h
        return None

    dbg_xn = dbg_out("xn", [NCH, 128, 4 * D])
    dbg_st = dbg_out("st", [NCH, 128, 40])
    dbg_lg = dbg_out("lg", [NCH, 128, 256])
    dbg_h1 = dbg_out("h1", [NCH, 128, 2048])
    dbg_tk = dbg_out("tk", [NCH, 128, 16])
    dbg_bp = dbg_out("bp", [14, 64, 3072])
    dbg_hin = dbg_out("hin", [NCH, 128, 8 * TC])
    dbg_alpha = dbg_out("alpha", [NCH, 128, 4 * 64])
    dbg_si = dbg_out("si", [128, 512])
    dbg_den = dbg_out("den", [1, 64])
    dbg_snew = dbg_out("snew", [64, D])
    dbg_sv = dbg_out("sv", [64, D])
    dbg_fin0 = dbg_out("fin0", [128, 8 * TC])
    dbg_s0 = dbg_out("s0", [128, 8 * TC])
    dbg_g0 = dbg_out("g0", [128, 8 * TC])

    stack = ExitStack()
    with stack:
        p = Prog(nc, stack)
        sb = lambda name, shape: stack.enter_context(nc.sbuf_tensor(name, shape, F32))

        eye = sb("eye_t", [128, 128])
        onesc = sb("onesc_t", [128, 1])
        onesr = sb("onesr_t", [1, 128])
        flagc = sb("flagc_t", [128, 1])
        n1c = sb("n1c_t", [128, 8])
        n2c = sb("n2c_t", [128, 8])
        bgc = sb("bgc_t", [128, 8])
        bcc = sb("bcc_t", [128, 8])
        boc = sb("boc_t", [128, 8])
        b2c = sb("b2c_t", [64, 1])
        taub = sb("taub_t", [1, 1])
        itau1 = sb("itau1_t", [1, 1])
        itau64 = sb("itau64_t", [64, 1])
        b2s = sb("b2s_t", [64, 1])
        sinitT = sb("sinitT_t", [128, 8 * 64])
        smdm = sb("smdm_t", [128, 8])
        sm1c = sb("sm1c_t", [128, 4])
        sm1sc = sb("sm1sc_t", [128, 4])
        state = sb("state_t", [128, 8])
        actw = sb("actw_t", [128, 1])
        gwarm = sb("gwarm_t", [128, 128])
        p.act_trailer = lambda e: e.activation(actw[:], onesc[:], AF.Square)
        svb = sb("svb_t", [64, D])
        pb = [stack.enter_context(nc.psum_tensor("pb%d" % i, [128, 512], F32))
              for i in range(8)]

        def tclose(ch):
            return lambda e: e.matmul(pb[7][0:1, 0:1], onesr[0:1, 0:1],
                                      onesr[0:1, 0:1], start=True, stop=True)

        DBG = p.ch("DBG")
        RQF = p.ch("RQF")
        AKF = p.ch("AKF")

        def act_fence(func):
            p.emit("scalar", lambda e, f=func: e.activation(actw[:], onesc[:], f))
            p.emit("scalar", lambda e, s_=RQF.sem: e.sem_inc(s_, 1))
            RQF.n += 1
            p.wait("gpsimd", RQF, RQF.n)
            for _ in range(6):
                p.emit("gpsimd", lambda e: e.tensor_copy(gwarm[:], gwarm[:]))
            p.emit("gpsimd", lambda e, s_=AKF.sem: e.drain().then_inc(s_, 1))
            AKF.n += 1
            p.wait("scalar", AKF, AKF.n)

        with nc.Block() as block:
            with ExitStack() as ascope:
                asb = lambda name, shape, dt_=F32: ascope.enter_context(nc.sbuf_tensor(name, shape, dt_))
                wgh = asb("wgh", [128, 8192], BF16)
                wgl = asb("wgl", [128, 8192], BF16)
                wch = asb("wch", [128, 8192], BF16)
                wcl = asb("wcl", [128, 8192], BF16)
                w1a = asb("w1at", [128, 4096])
                w2r = asb("w2rt", [128, 256])
                xld = asb("xld", [128, 2 * 4 * D])
                w1bt = xld
                xn_tm = asb("xn_tm", [128, 4 * D])
                xn_fm = asb("xn_fm", [128, 8 * TC])
                hinh = asb("hinh", [128, 8 * TC], BF16)
                hinl = asb("hinl", [128, 8 * TC], BF16)
                h1b = asb("h1b", [128, 4 * TC])
                scr = asb("scr", [128, D])
                b1r = scr[0:1, 0:512]
                sm1r = scr[0:1, 512:1024]
                ab = asb("ab_t", [128, TC])
                cb = asb("cb_t", [128, TC])
                bbb = asb("bb_t", [128, TC])
                hbb = asb("hb_t", [128, TC])
                lgf = asb("lgf", [64, TC])
                lgtm = asb("lgtm", [128, 256])
                cur = asb("cur", [128, 256])
                eb = asb("eb", [128, 256])
                aet = asb("aet", [128, 256])
                alf = asb("alf", [128, 256])
                r1r4 = asb("r1r4", [4, 128])
                rowr1 = asb("rowr1", [1, 512])
                mall = asb("mall", [128, 32])
                rhoall = asb("rhoall", [128, 32])
                r1all = asb("r1all", [128, 32])
                mkc = asb("mkc", [128, 4])
                nmc = asb("nmc", [128, 4])
                thrc = asb("thrc", [128, 4])
                zc = asb("zc", [128, 4])
                rzc = asb("rzc", [128, 4])
                spst = asb("spst", [128, 2 * 512])
                sphi = asb("sphi", [128, 2 * 512], BF16)
                splo = asb("splo", [128, 2 * 512], BF16)

                LDW = p.ch("LDW"); LDX = p.ch("LDX")
                IT = p.ch("IT"); IV = p.ch("IV"); IS = p.ch("IS")
                RST0 = p.ch("RST0")
                WLD = p.ch("WLD"); WSP = p.ch("WSP")

                def ld(dst_ap, src_ap):
                    p.emit("sync", lambda e, o=dst_ap, i=src_ap: e.dma_start(out=o, in_=i),
                           inc=LDW, k=16)

                ld(eye[:], eye_d.ap())
                ld(onesc[:], onesc_d.ap())
                ld(onesr[:], onesr_d.ap())
                ld(flagc[:], flag_d.ap())
                ld(n1c[:], n1col_d.ap())
                ld(n2c[:], n2col_d.ap())
                ld(bgc[:], bgcol_d.ap())
                ld(bcc[:], bccol_d.ap())
                ld(boc[:], bocol_d.ap())
                ld(b2c[:], b2col_d.ap())
                ld(taub[:], tau_d.ap())
                ld(sinitT[:].rearrange("p (k s) -> p k s", k=8), sinitT_d.ap())
                ld(w1a[:].rearrange("p (k e c) -> p k e c", k=8, e=4), w1a_d.ap())
                ld(w2r[:].rearrange("p (d s) -> p d s", d=4), w2r_d.ap().rearrange("d p s -> p d s"))
                N_LDW = LDW.n

                for (wd_, hi_, lo_) in ((wg_d, wgh, wgl), (wc_d, wch, wcl)):
                    p.wait("sync", WSP, WSP.n)
                    p.emit("sync", lambda e, i=wd_: e.dma_start(
                        out=xld[:].rearrange("p (k e c) -> p k e c", k=8, e=8),
                        in_=i.ap()), inc=WLD, k=16)
                    p.wait("vector", LDW, N_LDW)
                    p.wait("vector", WLD, WLD.n)
                    for kc in range(8):
                        p.emit("vector", lambda e, k=kc: e.tensor_scalar(
                            xld[:, k * 1024:(k + 1) * 1024], xld[:, k * 1024:(k + 1) * 1024],
                            n1c[:, k:k + 1], None, OP.mult))
                    p.emit("vector", lambda e, h_=hi_: e.tensor_copy(h_[:], xld[:]))
                    p.emit("vector", lambda e, h_=hi_, l_=lo_: e.tensor_tensor(
                        l_[:], xld[:], h_[:], OP.subtract), inc=WSP)
                p.wait("sync", WSP, 2)
                ld(xld[:, 0:4096].rearrange("p (k t) -> p k t", k=8),
                   w1b_d.ap().rearrange("k p t -> p k t"))
                ld(b1r, b1row_d.ap())
                N_LDW2 = LDW.n
                p.wait("vector", LDW, N_LDW)
                p.emit("vector", lambda e: e.memzero(state[:]))
                p.emit("vector", lambda e: e.tensor_reduce(
                    smdm[:].rearrange("p (k o) -> p k o", k=8, o=1),
                    sinitT[:].rearrange("p (k s) -> p k s", k=8), AX.X, OP.add))
                p.emit("vector", lambda e: e.tensor_scalar(smdm[:], smdm[:], 1.0 / 64, None, OP.mult),
                       inc=IV)
                p.emit("gpsimd", lambda e: e.memset(gwarm[:], 1.0))
                p.wait("scalar", LDW, N_LDW)
                act_fence(AF.Ln)
                p.emit("scalar", lambda e: e.activation(itau1[:], taub[:], AF.Abs), inc=IS)
                p.wait("vector", IS)
                p.emit("vector", lambda e: e.tensor_scalar(itau1[:], itau1[:], 0.1, None, OP.add))
                p.emit("vector", lambda e: e.reciprocal(itau1[:], itau1[:]), inc=IV)
                p.wait("tensor", LDW, N_LDW2)
                p.wait("tensor", IV)
                p.emit("tensor", _mm(pb[7][0:64, 0:1], onesr[:, 0:64], itau1[:], True, True), inc=IT)
                for kc in range(8):
                    p.emit("tensor", _mm(pb[6][0:1, 0:512], smdm[:, kc:kc + 1],
                                         w1bt[:, kc * 512:(kc + 1) * 512], kc == 0, kc == 7),
                           inc=(IT if kc == 7 else None))
                p.wait("vector", IT)
                p.emit("vector", lambda e: e.tensor_copy(itau64[:], pb[7][0:64, 0:1]))
                p.emit("vector", lambda e: e.tensor_tensor(b2s[:], b2c[:], itau64[:], OP.mult))
                p.emit("vector", lambda e: e.tensor_tensor(sm1r, pb[6][0:1, 0:512], b1r, OP.add),
                       inc=IV)
                p.wait("tensor", IV)
                for j in range(4):
                    p.emit("tensor", _mm(pb[5][0:128, j:j + 1], scr[0:1, 512 + j * 128:512 + (j + 1) * 128],
                                         onesc[0:1, 0:1], j == 0, j == 3),
                           inc=(IT if j == 3 else None))
                p.wait("vector", IT)
                p.emit("vector", lambda e: e.tensor_copy(sm1c[:], pb[5][0:128, 0:4]))
                p.emit("vector", lambda e: e.tensor_scalar(sm1sc[:], sm1c[:], RS2, None, OP.mult),
                       inc=IV)
                N_IV = IV.n

                PLX = p.ch("PLX"); PSQ = p.ch("PSQ"); PRD = p.ch("PRD")
                s1c = h1b[:, 0:32]
                s2c = h1b[:, 32:64]
                varc = h1b[:, 64:96]
                vpec = h1b[:, 96:128]
                recc = h1b[:, 128:160]
                rpc = h1b[:, 160:192]
                for c in range(NCH):
                    xb0 = xld[:, (c % 2) * 4 * D:(c % 2 + 1) * 4 * D]
                    p.wait("sync", PSQ, 4 * (c - 1))
                    p.wait("sync", PRD, 4 * (c - 1))
                    if c < 2:
                        p.wait("sync", WSP, 2)
                    if c == 0:
                        p.wait("sync", IT, IT.n)
                    p.emit("sync", lambda e, c=c, o=xb0: e.dma_start(
                        out=o.rearrange("p (j d) -> p j d", j=4),
                        in_=xin.ap()[c * TC:(c + 1) * TC, :].rearrange(
                            "(j p) d -> p j d", j=4)), inc=PLX, k=16)
                    p.wait("scalar", PLX, 16 * (c + 1))
                    if c == 0:
                        p.wait("scalar", LDW, N_LDW)
                        act_fence(AF.Ln)
                    for j in range(4):
                        p.emit("scalar", lambda e, x=xb0[:, j * D:(j + 1) * D], c=c, j=j:
                               e.activation(scr[:], x, AF.Square,
                                            accum_out=h1b[:, 32 + c * 4 + j:32 + c * 4 + j + 1]),
                               inc=PSQ)
                    p.wait("vector", PLX, 16 * (c + 1))
                    for j in range(4):
                        p.emit("vector", lambda e, x=xb0[:, j * D:(j + 1) * D], c=c, j=j:
                               e.tensor_reduce(h1b[:, c * 4 + j:c * 4 + j + 1], x,
                                               AX.X, OP.add), inc=PRD)
                p.wait("vector", PSQ, 32)
                p.emit("vector", lambda e: e.tensor_scalar(mall[:], s1c, INV_D, None, OP.mult))
                p.emit("vector", lambda e: e.tensor_scalar(s2c, s2c, INV_D, None, OP.mult))
                p.emit("vector", lambda e: e.tensor_tensor(varc, mall[:], mall[:], OP.mult))
                p.emit("vector", lambda e: e.tensor_tensor(varc, s2c, varc, OP.subtract))
                p.emit("vector", lambda e: e.tensor_scalar(vpec, varc, 1e-5, None, OP.add))
                p.emit("vector", lambda e: e.reciprocal(recc, vpec))
                p.emit("vector", lambda e: e.tensor_tensor(rpc, varc, recc, OP.mult))
                p.emit("vector", lambda e: e.tensor_scalar(rpc, rpc, 1e-6, None, OP.add),
                       inc=PRD)
                p.wait("scalar", PRD, PRD.n)
                p.emit("scalar", lambda e: e.activation(s1c, vpec, AF.Ln))
                p.emit("scalar", lambda e: e.activation(s2c, rpc, AF.Ln))
                p.emit("scalar", lambda e: e.activation(rhoall[:], s1c, AF.Exp, scale=-0.5))
                p.emit("scalar", lambda e: e.activation(r1all[:], s2c, AF.Exp, scale=-0.5),
                       inc=RST0)
                act_fence(AF.Sigmoid)

                SQ = p.ch("SQ"); VST = p.ch("VST"); RST = p.ch("RST"); XNT = p.ch("XNT")
                TG = p.ch("TG"); TGE = p.ch("TGE"); BC = p.ch("BC"); HIN = p.ch("HIN")
                H1T = p.ch("H1T"); H1S = p.ch("H1S"); H1V = p.ch("H1V")
                LGT = p.ch("LGT"); LGE = p.ch("LGE"); TOPK = p.ch("TOPK")
                SEXP = p.ch("SEXP"); ALPHA = p.ch("ALPHA")
                GCT = p.ch("GCT"); GCE = p.ch("GCE"); SCE = p.ch("SCE")
                SLT = p.ch("SLT"); SPL = p.ch("SPL"); RR = p.ch("RR")

                WIN = p.ch("WIN"); SPV = p.ch("SPV"); SPW = p.ch("SPW")
                ASM = p.ch("ASM")
                split_jobs = []
                for i in range(32):
                    for (srcw, hiw, low) in ((w1f_d, w1h_d, w1l_d),
                                             (w3f_d, w3h_d, w3l_d)):
                        for q in range(2):
                            split_jobs.append((
                                srcw.ap()[i][:, q * 4:(q + 1) * 4, :].rearrange(
                                    "p k e -> p (k e)"),
                                hiw.ap()[i][:, q * 512:(q + 1) * 512],
                                low.ap()[i][:, q * 512:(q + 1) * 512]))
                for g in range(8):
                    for q in range(8):
                        split_jobs.append((
                            w2f_d.ap()[g][:, q * 4:(q + 1) * 4, :].rearrange(
                                "p k e -> p (k e)"),
                            w2h_d.ap()[g][:, q * 512:(q + 1) * 512],
                            w2l_d.ap()[g][:, q * 512:(q + 1) * 512]))
                for g in range(8):
                    for q in range(2):
                        split_jobs.append((
                            wo_d.ap()[g][:, q * 4:(q + 1) * 4, :].rearrange(
                                "p k e -> p (k e)"),
                            woh_d.ap()[g][:, q * 512:(q + 1) * 512],
                            wol_d.ap()[g][:, q * 512:(q + 1) * 512]))

                def pump_split(n):
                    for _ in range(n):
                        if not split_jobs:
                            return
                        src, hi_, lo_ = split_jobs.pop(0)
                        i = pump_split.i
                        o = (i % 2) * 512
                        p.wait("sync", SPV, i - 1)
                        p.emit("sync", lambda e, oo=spst[:, o:o + 512], ii=src:
                               e.dma_start(out=oo, in_=ii), inc=WIN, k=16)
                        p.wait("gpsimd", WIN, WIN.n)
                        p.wait("gpsimd", SPW, 32 * (i - 1))
                        p.emit("gpsimd", lambda e, o=o: e.tensor_copy(
                            sphi[:, o:o + 512], spst[:, o:o + 512]))
                        for _ in range(3):
                            p.emit("gpsimd", lambda e: e.engine_nop())
                        p.emit("gpsimd", lambda e, o=o: e.tensor_tensor(
                            splo[:, o:o + 512], spst[:, o:o + 512],
                            sphi[:, o:o + 512], OP.subtract), inc=SPV)
                        p.wait("gpsimd", SPV, SPV.n)
                        p.emit("gpsimd", lambda e, oo=hi_, o=o: e.dma_start(
                            out=oo, in_=sphi[:, o:o + 512]), inc=SPW, k=16)
                        p.emit("gpsimd", lambda e, oo=lo_, o=o: e.dma_start(
                            out=oo, in_=splo[:, o:o + 512]), inc=SPW, k=16)
                        pump_split.i += 1
                pump_split.i = 0

                for c in range(NCH):
                    mine = c >= MY0
                    xb = xld[:, (c % 2) * 4 * D:(c % 2 + 1) * 4 * D]
                    tg_base = TG.n

                    if c < 2:
                        p.wait("sync", PSQ, 32)
                        p.wait("sync", PRD, 32)
                    p.wait("sync", XNT, c - 1)
                    p.emit("sync", lambda e, c=c, o=xb: e.dma_start(
                        out=o.rearrange("p (j d) -> p j d", j=4),
                        in_=xin.ap()[c * TC:(c + 1) * TC, :].rearrange("(j p) d -> p j d", j=4)),
                        inc=LDX, k=16)

                    for dh in range(4):
                        p.wait("scalar", H1T, 4 * c + dh + 1)
                        p.wait("scalar", H1V, 4 * c + dh)
                        p.emit("scalar", lambda e, dh=dh: e.activation(
                            scr[:, 0:512], pb[5][:, 0:512], AF.Erf,
                            bias=sm1sc[:, dh:dh + 1], scale=RS2))
                        p.emit("scalar", lambda e, dh=dh: e.activation(
                            scr[:, 512:1024], pb[5][:, 0:512], AF.Identity,
                            bias=sm1c[:, dh:dh + 1]), inc=H1S)
                    p.wait("scalar", LGT, c + 1)
                    if c > MY0:
                        p.wait("scalar", ASM, 16 * (c - MY0))
                    p.emit("scalar", lambda e: e.activation(
                        lgf[:], pb[6][0:64, 0:512], AF.Identity, bias=b2s[:], scale=itau64[:]),
                        inc=LGE)
                    for ec in range(8):
                        p.wait("scalar", GCT, 16 * c + 2 * ec + 1)
                        p.wait("scalar", SCE, 8 * c + ec)
                        p.emit("scalar", lambda e, ec=ec: e.activation(
                            ab[:], pb[2][:, 0:512], AF.Sigmoid, bias=bgc[:, ec:ec + 1]), inc=GCE)
                        p.wait("scalar", GCT, 16 * c + 2 * ec + 2)
                        p.emit("scalar", lambda e, ec=ec: e.activation(
                            cb[:], pb[3][:, 0:512], AF.Tanh, bias=bcc[:, ec:ec + 1]), inc=GCE)
                    p.wait("scalar", TOPK, c + 1)
                    for j in range(4):
                        p.emit("scalar", lambda e, j=j: e.activation(
                            eb[:, j * 64:(j + 1) * 64], lgtm[:, j * 64:(j + 1) * 64],
                            AF.Tanh, bias=nmc[:, j:j + 1], scale=0.5),
                            inc=(SEXP if j == 3 else None))

                    p.wait("vector", LDX, 16 * (c + 1))
                    if c == 0:
                        p.wait("vector", RST0, 1)
                    p.wait("vector", SLT, c)
                    for j in range(4):
                        p.emit("vector", lambda e, x=xb[:, j * D:(j + 1) * D], c=c, j=j:
                               e.tensor_scalar(
                            xn_tm[:, j * D:(j + 1) * D], x,
                            mall[:, c * 4 + j:c * 4 + j + 1],
                            rhoall[:, c * 4 + j:c * 4 + j + 1],
                            OP.subtract, OP.mult), inc=(XNT if j == 3 else None))
                    if dbg_xn is not None:
                        p.wait("gpsimd", XNT, c + 1)
                        p.emit("gpsimd", lambda e, c=c: e.dma_start(
                            out=dbg_xn.ap()[c], in_=xn_tm[:]), inc=DBG, k=16)
                    if dbg_st is not None:
                        p.emit("gpsimd", lambda e, c=c: e.dma_start(
                            out=dbg_st.ap()[c][:, 0:4], in_=mall[:, c * 4:(c + 1) * 4]),
                            inc=DBG, k=16)
                        p.emit("gpsimd", lambda e, c=c: e.dma_start(
                            out=dbg_st.ap()[c][:, 4:8], in_=rhoall[:, c * 4:(c + 1) * 4]),
                            inc=DBG, k=16)
                    p.wait("vector", H1T, 4 * c)
                    if c > MY0:
                        p.wait("vector", SPL, 16 * (c - MY0))
                    for g in range(8):
                        p.wait("vector", TG, tg_base + g + 1)
                        p.emit("vector", lambda e, g=g: e.tensor_copy(
                            xn_fm[:, g * TC:(g + 1) * TC], pb[4][:, 0:512]), inc=TGE)
                    p.wait("vector", TG, tg_base + 9)
                    p.wait("vector", RR, 16 * c)
                    p.emit("vector", lambda e: e.tensor_copy(r1r4[:], pb[4][0:4, 0:128]), inc=TGE)
                    p.wait("sync", TGE, TGE.n)
                    p.emit("sync", lambda e: e.dma_start(out=rowr1[:], in_=r1r4[:]),
                           inc=RR, k=16)
                    pump_split(26)
                    p.wait("vector", BC, c + 1)
                    for g in range(8):
                        p.emit("vector", lambda e, g=g: e.tensor_tensor(
                            hbb[:], xn_fm[:, g * TC:(g + 1) * TC],
                            pb[6][:, 0:512], OP.mult))
                        p.emit("vector", lambda e, g=g: e.tensor_copy(
                            hinh[:, g * TC:(g + 1) * TC], hbb[:]))
                        p.emit("vector", lambda e, g=g: e.tensor_tensor(
                            hinl[:, g * TC:(g + 1) * TC], hbb[:],
                            hinh[:, g * TC:(g + 1) * TC], OP.subtract),
                            inc=(HIN if g == 7 else None))
                    if dbg_hin is not None:
                        p.wait("gpsimd", HIN, c + 1)
                        p.emit("gpsimd", lambda e, c=c: e.dma_start(
                            out=dbg_hin.ap()[c], in_=hinh[:]), inc=DBG, k=16)
                    for dh in range(4):
                        p.wait("vector", H1S, 4 * c + dh + 1)
                        p.emit("vector", lambda e: e.tensor_scalar(
                            scr[:, 0:512], scr[:, 0:512], 0.5, 0.5, OP.mult, OP.add))
                        p.emit("vector", lambda e, dh=dh: e.tensor_tensor(
                            h1b[:, dh * TC:(dh + 1) * TC], scr[:, 0:512], scr[:, 512:1024],
                            OP.mult), inc=H1V)
                    p.wait("vector", TG, tg_base + 10)
                    p.emit("vector", lambda e: e.tensor_copy(lgtm[:], pb[4][:, 0:256]), inc=TGE)
                    p.emit("vector", lambda e: e.tensor_copy(cur[:], lgtm[:]))
                    for r in range(8):
                        dst = mkc if r == 0 else (thrc if r == 7 else zc)
                        p.emit("vector", lambda e, d=dst: e.tensor_reduce(
                            d[:].rearrange("p (j o) -> p j o", j=4, o=1),
                            cur[:].rearrange("p (j s) -> p j s", j=4), AX.X, OP.max))
                        if r < 7:
                            for j in range(4):
                                p.emit("vector", lambda e, j=j, d=dst: e.tensor_scalar(
                                    eb[:, j * 64:(j + 1) * 64], cur[:, j * 64:(j + 1) * 64],
                                    d[:, j:j + 1], None, OP.is_equal))
                            for j in range(4):
                                p.emit("vector", lambda e, j=j: e.scalar_tensor_tensor(
                                    cur[:, j * 64:(j + 1) * 64], eb[:, j * 64:(j + 1) * 64],
                                    -1e30, cur[:, j * 64:(j + 1) * 64], OP.mult, OP.add))
                    p.emit("vector", lambda e: e.tensor_scalar(nmc[:], mkc[:], -0.5, None, OP.mult),
                           inc=TOPK)
                    for ec in range(8):
                        p.wait("vector", GCE, 16 * c + 2 * ec + 2)
                        p.emit("vector", lambda e: e.tensor_scalar(
                            bbb[:], ab[:], -1.0, 1.0, OP.mult, OP.add))
                        p.emit("vector", lambda e: e.tensor_tensor(bbb[:], bbb[:], cb[:], OP.mult))
                        p.emit("vector", lambda e, ec=ec: e.tensor_tensor_scan(
                            hbb[:], ab[:], bbb[:], state[:, ec:ec + 1], OP.mult, OP.add))
                        if mine:
                            p.emit("vector", lambda e, ec=ec: e.tensor_tensor(
                                xn_fm[:, ec * TC:(ec + 1) * TC], xn_fm[:, ec * TC:(ec + 1) * TC],
                                hbb[:], OP.add), inc=SCE)
                        else:
                            p.emit("vector", lambda e: e.tensor_copy(bbb[:], hbb[:]), inc=SCE)
                        p.emit("vector", lambda e, ec=ec: e.tensor_copy(
                            state[:, ec:ec + 1], hbb[:, TC - 1:TC]))
                    if c == MY0 - 1:
                        p.emit("vector", lambda e: e.tensor_scalar(
                            state[:], state[:], flagc[:, 0:1], None, OP.mult))
                    p.wait("vector", SEXP, c + 1)
                    p.emit("vector", lambda e: e.tensor_scalar(cur[:], eb[:], -1.0, 1.0,
                                                               OP.mult, OP.add))
                    p.emit("vector", lambda e: e.reciprocal(cur[:], cur[:]))
                    p.emit("vector", lambda e: e.tensor_scalar(eb[:], eb[:], 1.0, None, OP.add))
                    p.emit("vector", lambda e: e.tensor_tensor(eb[:], eb[:], cur[:], OP.mult))
                    for j in range(4):
                        p.emit("vector", lambda e, j=j: e.tensor_scalar(
                            aet[:, j * 64:(j + 1) * 64], lgtm[:, j * 64:(j + 1) * 64],
                            thrc[:, j:j + 1], None, OP.is_ge))
                    p.emit("vector", lambda e: e.tensor_tensor(aet[:], aet[:], eb[:], OP.mult))
                    p.emit("vector", lambda e: e.tensor_reduce(
                        zc[:].rearrange("p (j o) -> p j o", j=4, o=1),
                        aet[:].rearrange("p (j s) -> p j s", j=4), AX.X, OP.add))
                    p.emit("vector", lambda e: e.reciprocal(rzc[:], zc[:]))
                    for j in range(4):
                        p.emit("vector", lambda e, j=j: e.tensor_scalar(
                            alf[:, j * 64:(j + 1) * 64], aet[:, j * 64:(j + 1) * 64],
                            rzc[:, j:j + 1], None, OP.mult), inc=(ALPHA if j == 3 else None))
                    if dbg_lg is not None:
                        p.wait("gpsimd", TOPK, c + 1)
                        p.emit("gpsimd", lambda e, c=c: e.dma_start(
                            out=dbg_lg.ap()[c], in_=lgtm[:]), inc=DBG, k=16)
                        for i_, t_ in enumerate([mkc, nmc, thrc, zc]):
                            p.emit("gpsimd", lambda e, c=c, i_=i_, t_=t_: e.dma_start(
                                out=dbg_tk.ap()[c][:, i_ * 4:(i_ + 1) * 4], in_=t_[:]),
                                inc=DBG, k=16)
                    if dbg_h1 is not None:
                        p.wait("gpsimd", H1V, 4 * (c + 1))
                        p.emit("gpsimd", lambda e, c=c: e.dma_start(
                            out=dbg_h1.ap()[c], in_=h1b[:]), inc=DBG, k=16)
                    if dbg_alpha is not None:
                        p.wait("gpsimd", ALPHA, c + 1)
                        p.emit("gpsimd", lambda e, c=c: e.dma_start(
                            out=dbg_alpha.ap()[c], in_=alf[:]), inc=DBG, k=16)
                    if mine:
                        p.wait("vector", TG, tg_base + 11)
                        p.emit("vector", lambda e: e.tensor_copy(
                            lgf[0:64, 0:512], pb[4][0:64, 0:512]), inc=TGE)
                        p.wait("gpsimd", TGE, TGE.n)
                        p.emit("gpsimd", lambda e, c=c: e.dma_start(
                            out=alpha_d.ap()[c - MY0], in_=lgf[0:64, 0:512]),
                            inc=ASM, k=16)

                    if c == 0:
                        p.wait("tensor", IV, N_IV)
                    p.wait("tensor", XNT, c + 1)
                    for g in range(8):
                        p.wait("tensor", TGE, tg_base + g)
                        for j in range(4):
                            p.emit("tensor", _mm(pb[4][:, j * 128:(j + 1) * 128],
                                                 xn_tm[:, j * D + g * 128:j * D + (g + 1) * 128],
                                                 eye[:], j == 0, j == 3, is_t=True))
                        p.emit("tensor", tclose(TG), inc=TG)
                    if c == 0:
                        p.wait("tensor", RST0, 1)
                    p.wait("tensor", TGE, tg_base + 8)
                    p.emit("tensor", _mm(pb[4][0:4, 0:128], r1all[:, c * 4:(c + 1) * 4],
                                         eye[:], True, True, is_t=True))
                    p.emit("tensor", tclose(TG), inc=TG)
                    p.wait("tensor", RR, 16 * (c + 1))
                    p.wait("tensor", LGE, c)
                    p.emit("tensor", _mm(pb[6][:, 0:512], onesr[:], rowr1[:], True, True),
                           inc=BC)
                    for dh in range(4):
                        p.wait("tensor", H1S, 4 * c + dh)
                        for kc in range(8):
                            p.emit("tensor", _mm(pb[5][:, 0:512],
                                                 w1a[:, (kc * 4 + dh) * 128:(kc * 4 + dh + 1) * 128],
                                                 xn_fm[:, kc * TC:(kc + 1) * TC], kc == 0, kc == 7),
                                   inc=(H1T if kc == 7 else None))
                    p.wait("tensor", HIN, c + 1)
                    p.wait("tensor", H1V, 4 * (c + 1))
                    for dh in range(4):
                        p.emit("tensor", _mm(pb[6][0:64, 0:512], w2r[:, dh * 64:(dh + 1) * 64],
                                             h1b[:, dh * TC:(dh + 1) * TC], dh == 0, dh == 3),
                               inc=(LGT if dh == 3 else None))
                    p.wait("tensor", LGE, c + 1)
                    p.wait("tensor", TGE, tg_base + 9)
                    for j in range(4):
                        p.emit("tensor", _mm(pb[4][:, j * 64:(j + 1) * 64],
                                             lgf[:, j * 128:(j + 1) * 128], eye[0:64, 0:64],
                                             j == 0, j == 3, is_t=True))
                    p.emit("tensor", tclose(TG), inc=TG)
                    p.wait("tensor", HIN, c + 1)
                    for ec in range(8):
                        for bank, wh_, wl_, gch in ((2, wgh, wgl, 0), (3, wch, wcl, 1)):
                            p.wait("tensor", GCE, 16 * c + 2 * ec - 1 + gch)
                            for kc in range(8):
                                sl = slice((kc * 8 + ec) * 128, (kc * 8 + ec + 1) * 128)
                                tsl = slice(kc * TC, (kc + 1) * TC)
                                p.emit("tensor", _mm(pb[bank][:, 0:512], wh_[:, sl],
                                                     hinh[:, tsl], kc == 0, False))
                                p.emit("tensor", _mm(pb[bank][:, 0:512], wh_[:, sl],
                                                     hinl[:, tsl], False, False))
                            for kc in range(8):
                                sl = slice((kc * 8 + ec) * 128, (kc * 8 + ec + 1) * 128)
                                tsl = slice(kc * TC, (kc + 1) * TC)
                                p.emit("tensor", _mm(pb[bank][:, 0:512], wl_[:, sl],
                                                     hinh[:, tsl], False, kc == 7),
                                       inc=(GCT if kc == 7 else None))
                    p.wait("tensor", ALPHA, c + 1)
                    first = (c == 0)
                    last = (c == NCH - 1)
                    for j in range(4):
                        for dchi in range(8):
                            p.emit("tensor", _mm(
                                pb[0][:, dchi * 64:(dchi + 1) * 64],
                                xn_tm[:, j * D + dchi * 128:j * D + (dchi + 1) * 128],
                                alf[:, j * 64:(j + 1) * 64],
                                first and j == 0 and dchi == 0, last and j == 3 and dchi == 7))
                        p.emit("tensor", _mm(pb[1][0:1, 0:64], onesc[:],
                                             alf[:, j * 64:(j + 1) * 64],
                                             first and j == 0, last and j == 3),
                               inc=(SLT if j == 3 else None))
                    if mine:
                        p.wait("tensor", TGE, tg_base + 10)
                        for j in range(4):
                            p.emit("tensor", _mm(pb[4][0:64, j * 128:(j + 1) * 128],
                                                 alf[:, j * 64:(j + 1) * 64], eye[:],
                                                 j == 0, j == 3, is_t=True))
                        p.emit("tensor", tclose(TG), inc=TG)

                    if dbg and "serial" in debug_outputs:
                        p.wait("gpsimd", DBG, DBG.n)
                        p.barrier()
                    if mine:
                        p.wait("gpsimd", SCE, 8 * c + 8)
                        p.emit("gpsimd", lambda e, c=c: e.dma_start(
                            out=x2s_d.ap()[c - MY0].rearrange("g p t -> p g t"),
                            in_=xn_fm[:].rearrange("p (g t) -> p g t", g=8)), inc=SPL, k=16)
                assert not split_jobs
                p.wait("gpsimd", SPL, 64)
                p.wait("gpsimd", SPW, SPW.n)
                p.wait("gpsimd", ASM, 64)
                if dbg:
                    p.wait("gpsimd", DBG, DBG.n)
                p.barrier()
                p.flush(block)

            with ExitStack() as bscope:
                bsb = lambda name, shape: bscope.enter_context(nc.sbuf_tensor(name, shape, F32))
                wstr = bsb("wstr", [128, 2 * 3072])
                gis = bsb("gis", [64, 3072])
                ghs = bsb("ghs", [64, 3072])
                bihb = bsb("bihb", [64, 3072])
                bhhb = bsb("bhhb", [64, 3072])
                bqb = bsb("bqb", [64, 256])
                bkb = bsb("bkb", [64, 256])
                bvb = bsb("bvb", [64, D])
                bvpb = bsb("bvpb", [64, D])
                lngb = bsb("lngb", [64, D])
                lnbb = bsb("lnbb", [64, D])
                rowB = bsb("rowB", [1, 3072])
                sis = bsb("sis", [128, 512])
                sidm = bsb("sidm", [128, 512])
                dres = bsb("dres", [1, 64])
                sS = bsb("sS", [64, D])
                tg1 = bsb("tg1", [64, D])
                tg2 = bsb("tg2", [64, D])
                rg = bsb("rg", [64, D])
                zg = bsb("zg", [64, D])
                ngb = bsb("ngb", [64, D])
                snew = bsb("snew", [64, D])
                sndm = bsb("sndm", [128, 512])
                qv = bsb("qv", [64, 256])
                kvb = bsb("kvb", [64, 256])
                vv = bsb("vv", [64, D])
                qdm = bsb("qdm", [128, 128])
                kdm = bsb("kdm", [128, 128])
                sc64 = bsb("sc64", [64, 64])
                at64 = bsb("at64", [64, 64])
                yv = bsb("yv", [64, D])
                ydm = bsb("ydm", [128, 512])
                wqb = bsb("wqb", [128, 2048])
                wkb = bsb("wkb", [128, 2048])
                cB = bsb("cB", [64, 8])

                BL = p.ch("BL"); BT = p.ch("BT"); BV = p.ch("BV"); BS = p.ch("BS")

                def bld(dst_ap, src_ap):
                    p.emit("sync", lambda e, o=dst_ap, i=src_ap: e.dma_start(out=o, in_=i),
                           inc=BL, k=16)

                def ts_(fns, waits=()):
                    for ch, n in waits:
                        p.wait("tensor", ch, n)
                    for i, f in enumerate(fns):
                        p.emit("tensor", f, inc=(BT if i == len(fns) - 1 else None))

                def vs_(fns, waits=()):
                    for ch, n in waits:
                        p.wait("vector", ch, n)
                    for i, f in enumerate(fns):
                        p.emit("vector", f, inc=(BV if i == len(fns) - 1 else None))

                def ss_(fns, waits=()):
                    for ch, n in waits:
                        p.wait("scalar", ch, n)
                    for i, f in enumerate(fns):
                        p.emit("scalar", f, inc=(BS if i == len(fns) - 1 else None))

                bld(sS[:], sinit_d.ap())
                bld(wqb[:].rearrange("p (k t) -> p k t", k=8), wq_d.ap())
                bld(wkb[:].rearrange("p (k t) -> p k t", k=8), wk_d.ap())
                NB0 = BL.n

                p.wait("vector", p.ch("SLT"), 8)
                vs_([lambda e: e.tensor_copy(dres[:], pb[1][0:1, 0:64]),
                     lambda e: e.tensor_scalar(dres[:], dres[:], 1e-8, None, OP.add),
                     lambda e: e.reciprocal(dres[:], dres[:]),
                     lambda e: e.tensor_copy(sis[:], pb[0][:, 0:512])])
                if dbg_den is not None:
                    p.wait("gpsimd", BV, BV.n)
                    p.emit("gpsimd", lambda e: e.dma_start(out=dbg_den.ap(), in_=dres[:]),
                           inc=DBG, k=16)
                ts_([_mm(pb[2][0:128, 0:64], onesr[:], dres[:], True, True)], waits=[(BV, BV.n)])
                vs_([(lambda e, g=g: e.tensor_tensor(
                        sidm[:, g * 64:(g + 1) * 64], sis[:, g * 64:(g + 1) * 64],
                        pb[2][:, 0:64], OP.mult)) for g in range(8)],
                    waits=[(BT, BT.n)])
                if dbg_si is not None:
                    p.wait("gpsimd", BV, BV.n)
                    p.emit("gpsimd", lambda e: e.dma_start(out=dbg_si.ap(), in_=sidm[:]),
                           inc=DBG, k=16)

                RB = p.ch("RB")

                def bcast64(dst, src_d, n):
                    p.wait("sync", RB, RB.n)
                    bld(rowB[0:1, 0:n], src_d.ap())
                    nldw = BL.n
                    nseg = (n + 511) // 512
                    for i in range(nseg):
                        w = min(512, n - i * 512)
                        p.wait("tensor", p.ch("BL"), nldw)
                        p.wait("tensor", BV, BV.n)
                        p.emit("tensor", _mm(pb[3][0:64, 0:w], onesr[:, 0:64],
                                             rowB[0:1, i * 512:i * 512 + w], True, True),
                               inc=RB)
                        vs_([lambda e, dd=dst, ii=i, ww=w: e.tensor_copy(
                            dd[0:64, ii * 512:ii * 512 + ww], pb[3][0:64, 0:ww])],
                            waits=[(RB, RB.n)])

                bcast64(bihb, bihrow_d, 3072)
                bcast64(bhhb, bhhrow_d, 3072)
                bcast64(bqb, bqrow_d, 256)
                bcast64(bkb, bkrow_d, 256)
                bcast64(bvb, bvrow_d, D)
                bcast64(bvpb, bvprow_d, D)
                bcast64(lngb, lngrow_d, D)
                bcast64(lnbb, lnbrow_d, D)

                WS = p.ch("WS")
                ws_free = {0: 0, 1: 0}

                for wd, dstbuf, lhs in ((wihT_d, gis, sidm), (whhT_d, ghs, sinitT)):
                    for kc in range(8):
                        par = kc % 2
                        buf = wstr[:, par * 3072:(par + 1) * 3072]
                        p.wait("sync", WS, ws_free[par])
                        p.emit("sync", lambda e, o=buf, i=wd.ap()[kc]:
                               e.dma_start(out=o, in_=i), inc=BL, k=16)
                        p.wait("tensor", BL, BL.n)
                        if kc == 0:
                            p.wait("tensor", BV, BV.n)
                        for seg in range(6):
                            p.emit("tensor", _mm(pb[2 + seg][0:64, 0:512],
                                                 lhs[:, kc * 64:(kc + 1) * 64],
                                                 buf[:, seg * 512:(seg + 1) * 512],
                                                 kc == 0, kc == 7),
                                   inc=(WS if seg == 5 else None))
                        ws_free[par] = WS.n
                    vs_([(lambda e, dd=dstbuf, s=seg: e.tensor_copy(
                            dd[0:64, s * 512:(s + 1) * 512], pb[2 + s][0:64, 0:512]))
                         for seg in range(6)], waits=[(WS, WS.n)])

                for gate, dstg in ((0, rg), (1, zg)):
                    o = gate * D
                    vs_([lambda e, o=o: e.tensor_tensor(tg1[:], gis[0:64, o:o + D],
                                                        bihb[0:64, o:o + D], OP.add),
                         lambda e, o=o: e.tensor_tensor(tg2[:], ghs[0:64, o:o + D],
                                                        bhhb[0:64, o:o + D], OP.add),
                         lambda e: e.tensor_tensor(tg1[:], tg1[:], tg2[:], OP.add)])
                    if gate == 0:
                        act_fence(AF.Sigmoid)
                    ss_([lambda e, d=dstg: e.activation(d[:], tg1[:], AF.Sigmoid)],
                        waits=[(BV, BV.n)])
                    p.wait("vector", BS, BS.n)
                o = 2 * D
                vs_([lambda e: e.tensor_tensor(tg1[:], ghs[0:64, o:o + D],
                                               bhhb[0:64, o:o + D], OP.add),
                     lambda e: e.tensor_tensor(tg1[:], rg[:], tg1[:], OP.mult),
                     lambda e: e.tensor_tensor(tg2[:], gis[0:64, o:o + D],
                                               bihb[0:64, o:o + D], OP.add),
                     lambda e: e.tensor_tensor(tg1[:], tg1[:], tg2[:], OP.add)])
                ss_([lambda e: e.activation(ngb[:], tg1[:], AF.Tanh)], waits=[(BV, BV.n)])
                vs_([lambda e: e.tensor_scalar(tg1[:], zg[:], -1.0, 1.0, OP.mult, OP.add),
                     lambda e: e.tensor_tensor(tg1[:], tg1[:], ngb[:], OP.mult),
                     lambda e: e.tensor_tensor(tg2[:], zg[:], sS[:], OP.mult),
                     lambda e: e.tensor_tensor(snew[:], tg1[:], tg2[:], OP.add)],
                    waits=[(BS, BS.n), (BL, NB0)])
                if dbg_snew is not None:
                    p.wait("gpsimd", BV, BV.n)
                    p.emit("gpsimd", lambda e: e.dma_start(out=dbg_snew.ap(), in_=snew[:]),
                           inc=DBG, k=16)
                for ch_, n_ in [(BV, BV.n)]:
                    p.wait("tensor", ch_, n_)
                for g in range(8):
                    p.emit("tensor", _mm(pb[2][:, g * 64:(g + 1) * 64],
                                         snew[0:64, g * 128:(g + 1) * 128],
                                         eye[0:64, 0:64], g == 0, g == 7, is_t=True))
                p.emit("tensor", tclose(BT), inc=BT)
                vs_([lambda e: e.tensor_copy(sndm[:], pb[2][:, 0:512])], waits=[(BT, BT.n)])
                ts_([_mm(pb[3][0:64, 0:256], sndm[:, kc * 64:(kc + 1) * 64],
                         wqb[:, kc * 256:(kc + 1) * 256], kc == 0, kc == 7) for kc in range(8)]
                    + [_mm(pb[4][0:64, 0:256], sndm[:, kc * 64:(kc + 1) * 64],
                           wkb[:, kc * 256:(kc + 1) * 256], kc == 0, kc == 7) for kc in range(8)],
                    waits=[(BV, BV.n)])
                for kc in range(8):
                    par = kc % 2
                    vbuf = wstr[:, par * 3072:par * 3072 + 1024]
                    p.wait("sync", WS, ws_free[par])
                    p.emit("sync", lambda e, o=vbuf, i=wv_d.ap()[:, kc, :]:
                           e.dma_start(out=o, in_=i), inc=BL, k=16)
                    p.wait("tensor", BL, BL.n)
                    p.emit("tensor", _mm(pb[5][0:64, 0:512], sndm[:, kc * 64:(kc + 1) * 64],
                                         vbuf[:, 0:512], kc == 0, kc == 7))
                    p.emit("tensor", _mm(pb[6][0:64, 0:512], sndm[:, kc * 64:(kc + 1) * 64],
                                         vbuf[:, 512:1024], kc == 0, kc == 7),
                           inc=WS)
                    ws_free[par] = WS.n
                p.emit("tensor", lambda e: e.nop(), inc=BT)
                vs_([lambda e: e.tensor_tensor(qv[:], pb[3][0:64, 0:256], bqb[:], OP.add),
                     lambda e: e.tensor_tensor(kvb[:], pb[4][0:64, 0:256], bkb[:], OP.add),
                     lambda e: e.tensor_tensor(vv[:, 0:512], pb[5][0:64, 0:512],
                                               bvb[:, 0:512], OP.add),
                     lambda e: e.tensor_tensor(vv[:, 512:1024], pb[6][0:64, 0:512],
                                               bvb[:, 512:1024], OP.add)],
                    waits=[(BT, BT.n)])
                p.wait("tensor", BV, BV.n)
                for i in range(2):
                    p.emit("tensor", _mm(pb[2][:, i * 64:(i + 1) * 64],
                                         qv[0:64, i * 128:(i + 1) * 128],
                                         eye[0:64, 0:64], i == 0, i == 1, is_t=True))
                for i in range(2):
                    p.emit("tensor", _mm(pb[3][:, i * 64:(i + 1) * 64],
                                         kvb[0:64, i * 128:(i + 1) * 128],
                                         eye[0:64, 0:64], i == 0, i == 1, is_t=True))
                p.emit("tensor", tclose(BT), inc=BT)
                vs_([lambda e: e.tensor_copy(qdm[:], pb[2][:, 0:128]),
                     lambda e: e.tensor_copy(kdm[:], pb[3][:, 0:128])], waits=[(BT, BT.n)])
                ts_([_mm(pb[4][0:64, 0:64], qdm[:, i * 64:(i + 1) * 64],
                         kdm[:, i * 64:(i + 1) * 64], i == 0, i == 1) for i in range(2)],
                    waits=[(BV, BV.n)])
                ss_([lambda e: e.activation(sc64[:], pb[4][0:64, 0:64], AF.Copy, scale=0.0625)],
                    waits=[(BT, BT.n)])
                vs_([lambda e: e.tensor_reduce(cB[:, 0:1], sc64[:], AX.X, OP.max),
                     lambda e: e.tensor_scalar(cB[:, 1:2], cB[:, 0:1], -1.0, None, OP.mult)],
                    waits=[(BS, BS.n)])
                act_fence(AF.Exp)
                ss_([lambda e: e.activation(at64[:], sc64[:], AF.Exp, bias=cB[:, 1:2])],
                    waits=[(BV, BV.n)])
                vs_([lambda e: e.tensor_reduce(cB[:, 2:3], at64[:], AX.X, OP.add),
                     lambda e: e.reciprocal(cB[:, 3:4], cB[:, 2:3]),
                     lambda e: e.tensor_scalar(at64[:], at64[:], cB[:, 3:4], None, OP.mult)],
                    waits=[(BS, BS.n)])
                p.wait("tensor", BV, BV.n)
                p.emit("tensor", _mm(pb[2][0:64, 0:64], at64[:], eye[0:64, 0:64],
                                     True, True, is_t=True))
                p.emit("tensor", tclose(BT), inc=BT)
                vs_([lambda e: e.tensor_copy(sc64[:], pb[2][0:64, 0:64])], waits=[(BT, BT.n)])
                ts_([_mm(pb[5][0:64, 0:512], sc64[:], vv[:, 0:512], True, True),
                     _mm(pb[6][0:64, 0:512], sc64[:], vv[:, 512:1024], True, True)],
                    waits=[(BV, BV.n)])
                vs_([lambda e: e.tensor_tensor(tg2[:, 0:512], snew[0:64, 0:512],
                                               pb[5][0:64, 0:512], OP.add),
                     lambda e: e.tensor_tensor(tg2[:, 512:1024], snew[0:64, 512:1024],
                                               pb[6][0:64, 0:512], OP.add),
                     lambda e: e.tensor_reduce(cB[:, 0:1], tg2[:], AX.X, OP.add)],
                    waits=[(BT, BT.n)])
                ss_([lambda e: e.activation(tg1[:], tg2[:], AF.Square, accum_out=cB[:, 1:2])],
                    waits=[(BV, BV.n)])
                vs_([lambda e: e.tensor_scalar(cB[:, 4:5], cB[:, 0:1], INV_D, None, OP.mult),
                     lambda e: e.tensor_scalar(cB[:, 5:6], cB[:, 1:2], INV_D, None, OP.mult),
                     lambda e: e.tensor_tensor(cB[:, 2:3], cB[:, 4:5], cB[:, 4:5], OP.mult),
                     lambda e: e.tensor_tensor(cB[:, 2:3], cB[:, 5:6], cB[:, 2:3], OP.subtract),
                     lambda e: e.tensor_scalar(cB[:, 2:3], cB[:, 2:3], 1e-5, None, OP.add)],
                    waits=[(BS, BS.n)])
                ss_([lambda e: e.activation(cB[:, 6:7], cB[:, 2:3], AF.Ln),
                     lambda e: e.activation(cB[:, 3:4], cB[:, 6:7], AF.Exp, scale=-0.5)],
                    waits=[(BV, BV.n)])
                vs_([lambda e: e.tensor_scalar(yv[:], tg2[:], cB[:, 4:5], cB[:, 3:4],
                                               OP.subtract, OP.mult),
                     lambda e: e.tensor_tensor(tg1[:], yv[:], lngb[:], OP.mult),
                     lambda e: e.tensor_tensor(yv[:], tg1[:], lnbb[:], OP.add)],
                    waits=[(BS, BS.n)])
                p.wait("tensor", BV, BV.n)
                for g in range(8):
                    p.emit("tensor", _mm(pb[2][:, g * 64:(g + 1) * 64],
                                         yv[0:64, g * 128:(g + 1) * 128],
                                         eye[0:64, 0:64], g == 0, g == 7, is_t=True))
                p.emit("tensor", tclose(BT), inc=BT)
                vs_([lambda e: e.tensor_copy(ydm[:], pb[2][:, 0:512])], waits=[(BT, BT.n)])
                p.wait("tensor", BV, BV.n)
                for kc in range(8):
                    par = kc % 2
                    vbuf = wstr[:, par * 3072:par * 3072 + 1024]
                    p.wait("sync", WS, ws_free[par])
                    p.emit("sync", lambda e, o=vbuf, i=wvp_d.ap()[:, kc, :]:
                           e.dma_start(out=o, in_=i), inc=BL, k=16)
                    p.wait("tensor", BL, BL.n)
                    p.emit("tensor", _mm(pb[5][0:64, 0:512], ydm[:, kc * 64:(kc + 1) * 64],
                                         vbuf[:, 0:512], kc == 0, kc == 7))
                    p.emit("tensor", _mm(pb[6][0:64, 0:512], ydm[:, kc * 64:(kc + 1) * 64],
                                         vbuf[:, 512:1024], kc == 0, kc == 7),
                           inc=WS)
                    ws_free[par] = WS.n
                p.emit("tensor", lambda e: e.nop(), inc=BT)
                vs_([lambda e: e.tensor_tensor(svb[0:64, 0:512], pb[5][0:64, 0:512],
                                               bvpb[:, 0:512], OP.add),
                     lambda e: e.tensor_tensor(svb[0:64, 512:1024], pb[6][0:64, 0:512],
                                               bvpb[:, 512:1024], OP.add)],
                    waits=[(BT, BT.n)])
                if dbg_sv is not None:
                    p.wait("gpsimd", BV, BV.n)
                    p.emit("gpsimd", lambda e: e.dma_start(out=dbg_sv.ap(), in_=svb[:]),
                           inc=DBG, k=16)
                if dbg_bp is not None:
                    p.wait("gpsimd", BV, BV.n)
                    p.wait("gpsimd", BS, BS.n)
                    taps = [(gis[0:64, :], 3072), (ghs[0:64, :], 3072), (rg[:], 1024),
                            (zg[:], 1024), (ngb[:], 1024), (snew[:], 1024),
                            (qv[:], 256), (kvb[:], 256), (vv[:], 1024),
                            (sndm[0:64, :], 512), (at64[:], 64), (yv[:], 1024), (cB[:], 8), (tg2[:], 1024)]
                    for i_, (ap_, w_) in enumerate(taps):
                        p.emit("gpsimd", lambda e, i_=i_, a_=ap_, w_=w_: e.dma_start(
                            out=dbg_bp.ap()[i_][:, 0:w_], in_=a_), inc=DBG, k=16)
                p.barrier()
                p.flush(block)

            with ExitStack() as cscope:
                csb = lambda name, shape: cscope.enter_context(nc.sbuf_tensor(name, shape, F32))
                csbb = lambda name, shape: cscope.enter_context(
                    nc.sbuf_tensor(name, shape, BF16))
                x2t = csb("x2t", [128, 8 * TC])
                finth = csbb("finth", [128, 8 * TC])
                fintl = csbb("fintl", [128, 8 * TC])
                Pth = csbb("Pth", [128, 16 * TC])
                Ptl = csbb("Ptl", [128, 16 * TC])
                ust = csb("ust", [128, TC])
                st_ = csb("st_", [128, 8 * TC])
                osth = csbb("osth", [128, 8 * TC])
                ostl = csbb("ostl", [128, 8 * TC])
                gt = csb("gt", [128, TC])
                sgt = csb("sgt", [128, 8 * TC])
                finb = csb("finb", [128, 2 * D])
                xmt = csb("xmt", [128, 2 * D])
                w1b2h = csbb("w1b2h", [128, 2 * 1024])
                w1b2l = csbb("w1b2l", [128, 2 * 1024])
                w3b2h = csbb("w3b2h", [128, 2 * 1024])
                w3b2l = csbb("w3b2l", [128, 2 * 1024])
                w2b2h = csbb("w2b2h", [128, 2 * 2048])
                w2b2l = csbb("w2b2l", [128, 2 * 2048])
                wob2h = csbb("wob2h", [128, 2 * 1024])
                wob2l = csbb("wob2l", [128, 2 * 1024])
                rowC = csb("rowC", [1, 512])
                sqsc = csb("sqsc", [128, TC])
                ptmp = csb("ptmp", [128, TC])
                r2rows = csb("r2rows", [1, 4 * TC])
                gs_ = csb("gs_", [128, TC])
                us2 = csb("us2", [128, TC])
                asmt = csb("asmt", [64, 2 * TC])

                CUS = p.ch("CUS")
                LD2 = p.ch("LD2"); LW13 = p.ch("LW13"); LW2 = p.ch("LW2")
                LWO = p.ch("LWO"); LXM = p.ch("LXM")
                CSQ = p.ch("CSQ"); CR2M = p.ch("CR2M"); CR2V = p.ch("CR2V")
                CR2S = p.ch("CR2S"); CBC = p.ch("CBC"); CFIN = p.ch("CFIN")
                CU = p.ch("CU"); CSIL = p.ch("CSIL"); CV = p.ch("CV"); CP = p.ch("CP")
                CW2G = p.ch("CW2G"); CSEV = p.ch("CSEV"); COS = p.ch("COS")
                COSE = p.ch("COSE"); CWOT = p.ch("CWOT"); CGE = p.ch("CGE")
                CSG = p.ch("CSG"); CTRF = p.ch("CTRF"); CFV = p.ch("CFV")
                COUT = p.ch("COUT"); LAS = p.ch("LAS")


                for tt in range(4):
                    if tt == 0:
                        p.wait("sync", SPV, SPV.n)
                    p.wait("sync", CSQ, 8 * tt)
                    p.emit("sync", lambda e, tt=tt: e.dma_start(
                        out=x2t[:].rearrange("p (g t) -> p g t", g=8),
                        in_=x2s_d.ap()[tt].rearrange("g p t -> p g t")), inc=LD2, k=16)
                    p.wait("scalar", LD2, 16 * (tt + 1))
                    for g in range(8):
                        if g > 0:
                            p.wait("scalar", CR2M, 8 * tt + g)
                        p.emit("scalar", lambda e, g=g: e.activation(
                            sqsc[:], x2t[:, g * TC:(g + 1) * TC], AF.Square), inc=CSQ)
                    for g in range(8):
                        p.wait("tensor", CSQ, 8 * tt + g + 1)
                        p.emit("tensor", _mm(pb[0][0:1, 0:512], onesc[:], sqsc[:],
                                             g == 0, g == 7), inc=CR2M)
                    p.wait("vector", CR2M, 8 * (tt + 1))
                    p.emit("vector", lambda e, tt=tt: e.tensor_scalar(
                        r2rows[0:1, tt * TC:(tt + 1) * TC], pb[0][0:1, 0:512],
                        INV_D, 1e-6, OP.mult, OP.add), inc=CR2V)
                p.wait("scalar", CR2V, 4)
                p.emit("scalar", lambda e: e.activation(r2rows[:], r2rows[:], AF.Ln))
                p.emit("scalar", lambda e: e.activation(r2rows[:], r2rows[:], AF.Exp, scale=-0.5),
                       inc=CR2S)
                act_fence(AF.Sigmoid)

                for tt in range(4):
                    if tt == 0:
                        p.wait("sync", CSQ, 32)
                    p.wait("sync", CSEV, 16 * tt)
                    p.wait("sync", CFIN, tt)
                    p.emit("sync", lambda e, tt=tt: e.dma_start(
                        out=x2t[:].rearrange("p (g t) -> p g t", g=8),
                        in_=x2s_d.ap()[tt].rearrange("g p t -> p g t")), inc=LD2, k=16)
                    p.wait("sync", COS, 8 * (tt - 1))
                    p.emit("sync", lambda e, tt=tt: e.dma_start(
                        out=asmt[0:64, (tt % 2) * TC:(tt % 2 + 1) * TC],
                        in_=alpha_d.ap()[tt]), inc=LAS, k=16)
                    p.wait("tensor", CR2S, 1)
                    if tt > 0:
                        p.wait("tensor", CFIN, tt)
                    p.emit("tensor", _mm(pb[1][:, 0:512], onesr[:],
                                         r2rows[0:1, tt * TC:(tt + 1) * TC], True, True),
                           inc=CBC)
                    p.wait("vector", CBC, tt + 1)
                    p.wait("vector", LD2, 16 * (4 + tt + 1))
                    if tt == 0:
                        p.wait("vector", SPW, SPW.n)
                    for g in range(8):
                        p.emit("vector", lambda e, g=g: e.scalar_tensor_tensor(
                            ptmp[:], x2t[:, g * TC:(g + 1) * TC],
                            n2c[:, g:g + 1], pb[1][:, 0:512], OP.mult, OP.mult))
                        p.emit("vector", lambda e, g=g: e.tensor_copy(
                            finth[:, g * TC:(g + 1) * TC], ptmp[:]))
                        p.emit("vector", lambda e, g=g: e.tensor_tensor(
                            fintl[:, g * TC:(g + 1) * TC], ptmp[:],
                            finth[:, g * TC:(g + 1) * TC], OP.subtract),
                            inc=(CFIN if g == 7 else None))
                    if dbg_fin0 is not None and tt == 0:
                        p.wait("gpsimd", CFIN, 1)
                        p.emit("gpsimd", lambda e: e.dma_start(out=dbg_fin0.ap(), in_=finth[:]),
                               inc=DBG, k=16)

                    cu_base = CU.n
                    sev_base = CSEV.n
                    w2g_base = CW2G.n
                    for h in range(2):
                        for i in range(16):
                            idx = h * 16 + i
                            par_ = idx % 2
                            wb1h = w1b2h[:, par_ * 1024:(par_ + 1) * 1024]
                            wb1l = w1b2l[:, par_ * 1024:(par_ + 1) * 1024]
                            wb3h = w3b2h[:, par_ * 1024:(par_ + 1) * 1024]
                            wb3l = w3b2l[:, par_ * 1024:(par_ + 1) * 1024]
                            dff = h * 16 + i
                            if tt == 0:
                                p.wait("sync", SPW, 128 * (dff + 1))
                            if idx >= 2:
                                p.wait("sync", CU, cu_base + idx - 1)
                            p.emit("sync", lambda e, o=wb1h, d=dff: e.dma_start(
                                out=o, in_=w1h_d.ap()[d]), inc=LW13, k=16)
                            p.emit("sync", lambda e, o=wb1l, d=dff: e.dma_start(
                                out=o, in_=w1l_d.ap()[d]), inc=LW13, k=16)
                            if idx >= 2:
                                p.wait("sync", CV, cu_base + idx - 1)
                            p.emit("sync", lambda e, o=wb3h, d=dff: e.dma_start(
                                out=o, in_=w3h_d.ap()[d]), inc=LW13, k=16)
                            p.emit("sync", lambda e, o=wb3l, d=dff: e.dma_start(
                                out=o, in_=w3l_d.ap()[d]), inc=LW13, k=16)
                            p.wait("tensor", LW13, LW13.n - 32)
                            p.wait("tensor", CUS, cu_base + idx)
                            if tt > 0 and idx == 0:
                                p.wait("tensor", CFV, 4 * tt)
                            if idx == 0:
                                p.wait("tensor", CFIN, tt + 1)
                            for kc in range(8):
                                p.emit("tensor", _mm(pb[2][:, 0:512],
                                                     wb1h[:, kc * 128:(kc + 1) * 128],
                                                     finth[:, kc * TC:(kc + 1) * TC],
                                                     kc == 0, False))
                                p.emit("tensor", _mm(pb[2][:, 0:512],
                                                     wb1h[:, kc * 128:(kc + 1) * 128],
                                                     fintl[:, kc * TC:(kc + 1) * TC],
                                                     False, False))
                            for kc in range(8):
                                p.emit("tensor", _mm(pb[2][:, 0:512],
                                                     wb1l[:, kc * 128:(kc + 1) * 128],
                                                     finth[:, kc * TC:(kc + 1) * TC],
                                                     False, kc == 7),
                                       inc=(CU if kc == 7 else None))
                            p.wait("scalar", CU, cu_base + idx + 1)
                            p.wait("scalar", CUS, cu_base + idx)
                            p.emit("scalar", lambda e: e.activation(us2[:], pb[2][:, 0:512],
                                                                    AF.Sigmoid), inc=CSIL)
                            p.wait("tensor", LW13, LW13.n)
                            p.wait("tensor", CP, cu_base + idx)
                            for kc in range(8):
                                p.emit("tensor", _mm(pb[3][:, 0:512],
                                                     wb3h[:, kc * 128:(kc + 1) * 128],
                                                     finth[:, kc * TC:(kc + 1) * TC],
                                                     kc == 0, False))
                                p.emit("tensor", _mm(pb[3][:, 0:512],
                                                     wb3h[:, kc * 128:(kc + 1) * 128],
                                                     fintl[:, kc * TC:(kc + 1) * TC],
                                                     False, False))
                            for kc in range(8):
                                p.emit("tensor", _mm(pb[3][:, 0:512],
                                                     wb3l[:, kc * 128:(kc + 1) * 128],
                                                     finth[:, kc * TC:(kc + 1) * TC],
                                                     False, kc == 7),
                                       inc=(CV if kc == 7 else None))
                            p.wait("vector", CSIL, cu_base + idx + 1)
                            p.emit("vector", lambda e: e.tensor_tensor(
                                ust[:], us2[:], pb[2][:, 0:512], OP.mult), inc=CUS)
                            p.wait("vector", CV, cu_base + idx + 1)
                            if h == 1 and i == 0:
                                p.wait("vector", CW2G, w2g_base + 8)
                            p.emit("vector", lambda e: e.tensor_tensor(
                                ptmp[:], ust[:], pb[3][:, 0:512], OP.mult))
                            p.emit("vector", lambda e, i=i: e.tensor_copy(
                                Pth[:, i * TC:(i + 1) * TC], ptmp[:]))
                            p.emit("vector", lambda e, i=i: e.tensor_tensor(
                                Ptl[:, i * TC:(i + 1) * TC], ptmp[:],
                                Pth[:, i * TC:(i + 1) * TC], OP.subtract),
                                inc=CP)
                        for g in range(8):
                            gi_ = h * 8 + g
                            wb2h = w2b2h[:, (g % 2) * 2048:(g % 2 + 1) * 2048]
                            wb2l = w2b2l[:, (g % 2) * 2048:(g % 2 + 1) * 2048]
                            if tt == 0 and gi_ < 2:
                                p.wait("sync", SPW, SPW.n)
                            if gi_ >= 2:
                                p.wait("sync", CW2G, w2g_base + gi_ - 1)
                            p.emit("sync", lambda e, o=wb2h, g=g, h=h: e.dma_start(
                                out=o, in_=w2h_d.ap()[g][:, h * 2048:(h + 1) * 2048]),
                                inc=LW2, k=16)
                            p.emit("sync", lambda e, o=wb2l, g=g, h=h: e.dma_start(
                                out=o, in_=w2l_d.ap()[g][:, h * 2048:(h + 1) * 2048]),
                                inc=LW2, k=16)
                            p.wait("tensor", LW2, LW2.n)
                            p.wait("tensor", CP, cu_base + (h + 1) * 16)
                            if gi_ >= 2:
                                p.wait("tensor", CSEV, sev_base + gi_ - 1)
                            bank = pb[4 + g % 2]
                            for di in range(16):
                                p.emit("tensor", _mm(bank[:, 0:512],
                                                     wb2h[:, di * 128:(di + 1) * 128],
                                                     Pth[:, di * TC:(di + 1) * TC],
                                                     di == 0, False))
                                p.emit("tensor", _mm(bank[:, 0:512],
                                                     wb2h[:, di * 128:(di + 1) * 128],
                                                     Ptl[:, di * TC:(di + 1) * TC],
                                                     False, False))
                            for di in range(16):
                                p.emit("tensor", _mm(bank[:, 0:512],
                                                     wb2l[:, di * 128:(di + 1) * 128],
                                                     Pth[:, di * TC:(di + 1) * TC],
                                                     False, di == 15),
                                       inc=(CW2G if di == 15 else None))
                            p.wait("vector", CW2G, w2g_base + gi_ + 1)
                            if h == 0:
                                p.emit("vector", lambda e, g=g, b=bank: e.tensor_tensor(
                                    st_[:, g * TC:(g + 1) * TC], b[:, 0:512],
                                    x2t[:, g * TC:(g + 1) * TC], OP.add), inc=CSEV)
                            else:
                                p.emit("vector", lambda e, g=g, b=bank: e.tensor_tensor(
                                    st_[:, g * TC:(g + 1) * TC], b[:, 0:512],
                                    st_[:, g * TC:(g + 1) * TC], OP.add), inc=CSEV)
                    if dbg_s0 is not None and tt == 0:
                        p.wait("gpsimd", CSEV, 16)
                        p.emit("gpsimd", lambda e: e.dma_start(out=dbg_s0.ap(), in_=st_[:]),
                               inc=DBG, k=16)

                    for g in range(8):
                        if g == 0:
                            p.wait("tensor", CGE, 8 * tt)
                            p.wait("tensor", LAS, 16 * (tt + 1))
                        p.wait("tensor", COSE, 8 * tt + g - 1)
                        p.emit("tensor", _mm(pb[6 + g % 2][:, 0:512],
                                             svb[0:64, g * 128:(g + 1) * 128],
                                             asmt[0:64, (tt % 2) * TC:(tt % 2 + 1) * TC],
                                             True, True), inc=COS)
                        p.wait("vector", COS, 8 * tt + g + 1)
                        p.wait("vector", CWOT, 8 * tt)
                        p.emit("vector", lambda e, g=g: e.tensor_copy(ptmp[:],
                            pb[6 + g % 2][:, 0:512]))
                        p.emit("vector", lambda e, g=g: e.tensor_copy(
                            osth[:, g * TC:(g + 1) * TC], ptmp[:]))
                        p.emit("vector", lambda e, g=g: e.tensor_tensor(
                            ostl[:, g * TC:(g + 1) * TC], ptmp[:],
                            osth[:, g * TC:(g + 1) * TC], OP.subtract), inc=COSE)
                    for ec in range(8):
                        wobh = wob2h[:, (ec % 2) * 1024:(ec % 2 + 1) * 1024]
                        wobl = wob2l[:, (ec % 2) * 1024:(ec % 2 + 1) * 1024]
                        if tt == 0 and ec < 2:
                            p.wait("sync", SPW, SPW.n)
                        if 8 * tt + ec >= 2:
                            p.wait("sync", CWOT, 8 * tt + ec - 1)
                        p.emit("sync", lambda e, o=wobh, ec=ec: e.dma_start(
                            out=o, in_=woh_d.ap()[ec]), inc=LWO, k=16)
                        p.emit("sync", lambda e, o=wobl, ec=ec: e.dma_start(
                            out=o, in_=wol_d.ap()[ec]), inc=LWO, k=16)
                        p.wait("tensor", LWO, LWO.n)
                        p.wait("tensor", COSE, 8 * (tt + 1))
                        if ec >= 2:
                            p.wait("tensor", CGE, 8 * tt + ec - 1)
                        for kc in range(8):
                            p.emit("tensor", _mm(pb[6 + ec % 2][:, 0:512],
                                                 wobh[:, kc * 128:(kc + 1) * 128],
                                                 osth[:, kc * TC:(kc + 1) * TC],
                                                 kc == 0, False))
                            p.emit("tensor", _mm(pb[6 + ec % 2][:, 0:512],
                                                 wobh[:, kc * 128:(kc + 1) * 128],
                                                 ostl[:, kc * TC:(kc + 1) * TC],
                                                 False, False))
                        for kc in range(8):
                            p.emit("tensor", _mm(pb[6 + ec % 2][:, 0:512],
                                                 wobl[:, kc * 128:(kc + 1) * 128],
                                                 osth[:, kc * TC:(kc + 1) * TC],
                                                 False, kc == 7),
                                   inc=(CWOT if kc == 7 else None))
                        p.wait("scalar", CWOT, 8 * tt + ec + 1)
                        p.wait("scalar", CSG, 8 * tt + ec)
                        p.emit("scalar", lambda e, ec=ec: e.activation(
                            gt[:], pb[6 + ec % 2][:, 0:512], AF.Identity,
                            bias=boc[:, ec:ec + 1]))
                        p.emit("scalar", lambda e: e.activation(gs_[:], gt[:], AF.Sigmoid),
                               inc=CGE)
                        p.wait("vector", CGE, 8 * tt + ec + 1)
                        p.wait("vector", CSEV, sev_base + 16)
                        if tt > 0 and ec == 0:
                            p.wait("vector", CTRF, 4 * tt)
                        p.emit("vector", lambda e: e.tensor_tensor(gt[:], gt[:], gs_[:], OP.mult))
                        p.emit("vector", lambda e, ec=ec: e.tensor_tensor(
                            sgt[:, ec * TC:(ec + 1) * TC], gt[:], st_[:, ec * TC:(ec + 1) * TC],
                            OP.mult), inc=CSG)
                    if dbg_g0 is not None and tt == 0:
                        p.wait("gpsimd", CSG, 8)
                        p.emit("gpsimd", lambda e: e.dma_start(out=dbg_g0.ap(), in_=sgt[:]),
                               inc=DBG, k=16)
                    for j in range(4):
                        jj = 4 * tt + j
                        xm = xmt[:, (j % 2) * D:(j % 2 + 1) * D]
                        if jj >= 2:
                            p.wait("sync", CFV, jj - 1)
                        p.emit("sync", lambda e, o=xm, tt=tt, j=j: e.dma_start(
                            out=o, in_=xin.ap()[2048 + tt * TC + j * 128:
                                                2048 + tt * TC + (j + 1) * 128, :]),
                            inc=LXM, k=16)
                        p.wait("tensor", CSG, 8 * (tt + 1))
                        p.wait("tensor", CFV, jj)
                        for g in range(8):
                            bank = pb[2 + g // 4]
                            p.emit("tensor", _mm(bank[:, (g % 4) * 128:(g % 4 + 1) * 128],
                                                 sgt[:, g * TC + j * 128:g * TC + (j + 1) * 128],
                                                 eye[:], g % 4 == 0, g % 4 == 3, is_t=True))
                        p.emit("tensor", tclose(CTRF), inc=CTRF)
                        p.wait("vector", CTRF, jj + 1)
                        p.wait("vector", LXM, 16 * (jj + 1))
                        if jj >= 2:
                            p.wait("vector", COUT, 16 * (jj - 1))
                        fb = finb[:, (j % 2) * D:(j % 2 + 1) * D]
                        p.emit("vector", lambda e, fb=fb, xm=xm: e.tensor_tensor(
                            fb[:, 0:512], pb[2][:, 0:512], xm[:, 0:512], OP.add))
                        p.emit("vector", lambda e, fb=fb, xm=xm: e.tensor_tensor(
                            fb[:, 512:1024], pb[3][:, 0:512], xm[:, 512:1024], OP.add), inc=CFV)
                        p.wait("gpsimd", CFV, jj + 1)
                        p.emit("gpsimd", lambda e, tt=tt, j=j, fb=fb: e.dma_start(
                            out=out_d.ap()[tt * TC + j * 128:tt * TC + (j + 1) * 128, :],
                            in_=fb), inc=COUT, k=16)

                p.wait("gpsimd", COUT, 16 * 16)
                if dbg:
                    p.wait("gpsimd", DBG, DBG.n)
                p.flush(block)

    return nc


def _stage(inputs, core):
    b, h = core // 2, core % 2
    C = np.ascontiguousarray
    x = inputs["L0_attn_x"][b]
    if h == 0:
        xr = np.concatenate([x[2048:], x[:2048]], axis=0)
    else:
        xr = x
    blk = lambda w, K, E: C(w.reshape(K // 128, 128, E // 128, 128).transpose(1, 0, 2, 3))
    m = {
        "xin": C(xr),
        "wg": blk(inputs["L0_attn_Wg"], 1024, 1024),
        "wc": blk(inputs["L0_attn_Wc"], 1024, 1024),
        "w1a": blk(inputs["L0_ffn_w1"][:1024], 1024, 512),
        "w1b": C(inputs["L0_ffn_w1"][1024:].reshape(8, 128, 512)),
        "w2r": C(inputs["L0_ffn_w2"].reshape(4, 128, 64)),
        "b1row": C(inputs["L0_ffn_b1"].reshape(1, 512)),
        "b2col": C(inputs["L0_ffn_b2"].reshape(64, 1)),
        "tau": C(inputs["L0_ffn_tau"].reshape(1, 1)),
        "n1col": C(inputs["L0_attn_n1_w"].reshape(8, 128).T),
        "n2col": C(inputs["L0_attn_n2_w"].reshape(8, 128).T),
        "bgcol": C(inputs["L0_attn_bg"].reshape(8, 128).T),
        "bccol": C(inputs["L0_attn_bc"].reshape(8, 128).T),
        "bocol": C(inputs["L0_ffn_bo"].reshape(8, 128).T),
        "sinitT": C(inputs["L0_ffn_slot_init"].T.reshape(8, 128, 64).transpose(1, 0, 2)),
        "sinit": C(inputs["L0_ffn_slot_init"]),
        "wihT": C(inputs["L0_ffn_Wih"].T.reshape(8, 128, 3072)),
        "whhT": C(inputs["L0_ffn_Whh"].T.reshape(8, 128, 3072)),
        "bihrow": C(inputs["L0_ffn_bih"].reshape(1, 3072)),
        "bhhrow": C(inputs["L0_ffn_bhh"].reshape(1, 3072)),
        "wq": C(inputs["L0_ffn_wq"].reshape(8, 128, 256).transpose(1, 0, 2)),
        "wk": C(inputs["L0_ffn_wk"].reshape(8, 128, 256).transpose(1, 0, 2)),
        "wv": C(inputs["L0_ffn_wv"].reshape(8, 128, 1024).transpose(1, 0, 2)),
        "wvp": C(inputs["L0_ffn_wvp"].reshape(8, 128, 1024).transpose(1, 0, 2)),
        "bqrow": C(inputs["L0_ffn_bq"].reshape(1, 256)),
        "bkrow": C(inputs["L0_ffn_bk"].reshape(1, 256)),
        "bvrow": C(inputs["L0_ffn_bv"].reshape(1, 1024)),
        "bvprow": C(inputs["L0_ffn_bvp"].reshape(1, 1024)),
        "lngrow": C(inputs["L0_ffn_ln_g"].reshape(1, 1024)),
        "lnbrow": C(inputs["L0_ffn_ln_b"].reshape(1, 1024)),
        "wo": C(inputs["L0_ffn_wo"].reshape(8, 128, 8, 128).transpose(2, 1, 0, 3)),
        "w1f": C(inputs["L0_attn_W1"].reshape(8, 128, 32, 128).transpose(2, 1, 0, 3)),
        "w3f": C(inputs["L0_attn_W3"].reshape(8, 128, 32, 128).transpose(2, 1, 0, 3)),
        "w2f": C(inputs["L0_attn_W2"].reshape(32, 128, 8, 128).transpose(2, 1, 0, 3)),
        "eye": np.eye(128, dtype=np.float32),
        "onesc": np.ones((128, 1), np.float32),
        "onesr": np.ones((1, 128), np.float32),
        "flag": np.full((128, 1), float(h), np.float32),
    }
    return {k: np.asarray(v, dtype=np.float32) for k, v in m.items()}


def kernel(**inputs):
    nc = build_program()
    in_maps = [_stage(inputs, core) for core in range(8)]
    res = run_bass_kernel_spmd(nc, in_maps, core_ids=list(range(8)))
    out = np.empty((4, 4096, 1024), dtype=np.float32)
    for core in range(8):
        b, h = core // 2, core % 2
        out[b, h * 2048:(h + 1) * 2048] = res.results[core]["out"]
    return out


# version 07
# speedup vs baseline: 1.0849x; 1.0267x over previous
import numpy as np
from contextlib import ExitStack

import concourse.bass as bass
import concourse.mybir as mybir
from concourse.bass_utils import run_bass_kernel_spmd

F32 = mybir.dt.float32
BF16 = mybir.dt.bfloat16
AF = mybir.ActivationFunctionType
OP = mybir.AluOpType
AX = mybir.AxisListType

D = 1024
T = 4096
TC = 512
NCH = 8
MY0 = 4
NDFF = 32
NS = 64
RS2 = 0.7071067811865476
INV_D = 1.0 / D
ENGS = ("tensor", "vector", "scalar", "sync", "gpsimd")


class Ch:
    __slots__ = ("sem", "n")

    def __init__(self, sem):
        self.sem = sem
        self.n = 0


class Prog:
    def __init__(self, nc, stack):
        self.nc = nc
        self.stack = stack
        self.q = {e: [] for e in ENGS}
        self._ch = {}

    def ch(self, name):
        c = self._ch.get(name)
        if c is None:
            c = Ch(self.stack.enter_context(self.nc.semaphore(name)))
            self._ch[name] = c
        return c

    def wait(self, eng, ch, n=None):
        v = ch.n if n is None else n
        if v <= 0:
            return
        self.q[eng].append(lambda e, s=ch.sem, vv=v: e.wait_ge(s, vv))

    def emit(self, eng, fn, inc=None, k=1):
        if eng == "vector":
            self.q[eng].append(lambda e: e.engine_nop())
            self.q[eng].append(lambda e: e.engine_nop())
            self.q[eng].append(lambda e: e.engine_nop())
        if inc is None:
            self.q[eng].append(fn)
        elif k == 1:
            inc.n += k
            self.q[eng].append(fn)
            self.q[eng].append(lambda e, s=inc.sem, kk=k: e.drain().then_inc(s, kk))
        else:
            inc.n += k
            self.q[eng].append(lambda e, f=fn, s=inc.sem, kk=k: f(e).then_inc(s, kk))

    act_trailer = None

    def barrier(self):
        ch = self.ch("bar")
        for eng in ENGS:
            self.q[eng].append(lambda e, s=ch.sem: e.drain().then_inc(s, 1))
        ch.n += len(ENGS)
        for eng in ENGS:
            self.wait(eng, ch)

    def flush(self, block):
        for eng in ENGS:
            lst = self.q[eng]
            if not lst:
                continue
            self.q[eng] = []
            getattr(block, eng)(lambda e, L=lst: [f(e) for f in L])


def _mm(out_ap, lhsT_ap, rhs_ap, start, stop, is_t=False):
    return lambda e: e.matmul(out_ap, lhsT_ap, rhs_ap, start=start, stop=stop,
                              is_transpose=is_t)


def build_program(debug_outputs=()):
    nc = bass.Bass("TRN2", target_bir_lowering=False)
    dt = lambda name, shape, kind="ExternalInput": nc.dram_tensor(name, shape, F32, kind=kind)

    xin = dt("xin", [T, D])
    wg_d = dt("wg", [128, 8, 8, 128])
    wc_d = dt("wc", [128, 8, 8, 128])
    w1a_d = dt("w1a", [128, 8, 4, 128])
    w1b_d = dt("w1b", [8, 128, 512])
    w2r_d = dt("w2r", [4, 128, 64])
    b1row_d = dt("b1row", [1, 512])
    b2col_d = dt("b2col", [64, 1])
    tau_d = dt("tau", [1, 1])
    n1col_d = dt("n1col", [128, 8])
    n2col_d = dt("n2col", [128, 8])
    bgcol_d = dt("bgcol", [128, 8])
    bccol_d = dt("bccol", [128, 8])
    bocol_d = dt("bocol", [128, 8])
    sinitT_d = dt("sinitT", [128, 8, 64])
    sinit_d = dt("sinit", [64, D])
    wihT_d = dt("wihT", [8, 128, 3072])
    whhT_d = dt("whhT", [8, 128, 3072])
    bihrow_d = dt("bihrow", [1, 3072])
    bhhrow_d = dt("bhhrow", [1, 3072])
    wq_d = dt("wq", [128, 8, 256])
    wk_d = dt("wk", [128, 8, 256])
    wv_d = dt("wv", [128, 8, D])
    wvp_d = dt("wvp", [128, 8, D])
    bqrow_d = dt("bqrow", [1, 256])
    bkrow_d = dt("bkrow", [1, 256])
    bvrow_d = dt("bvrow", [1, D])
    bvprow_d = dt("bvprow", [1, D])
    lngrow_d = dt("lngrow", [1, D])
    lnbrow_d = dt("lnbrow", [1, D])
    wo_d = dt("wo", [8, 128, 8, 128])
    w1f_d = dt("w1f", [32, 128, 8, 128])
    w3f_d = dt("w3f", [32, 128, 8, 128])
    w2f_d = dt("w2f", [8, 128, 32, 128])
    eye_d = dt("eye", [128, 128])
    onesc_d = dt("onesc", [128, 1])
    onesr_d = dt("onesr", [1, 128])
    flag_d = dt("flag", [128, 1])
    out_d = dt("out", [2048, D], kind="ExternalOutput")

    w1h_d = nc.dram_tensor("w1h", [32, 128, 1024], BF16, kind="Internal")
    w1l_d = nc.dram_tensor("w1l", [32, 128, 1024], BF16, kind="Internal")
    w3h_d = nc.dram_tensor("w3h", [32, 128, 1024], BF16, kind="Internal")
    w3l_d = nc.dram_tensor("w3l", [32, 128, 1024], BF16, kind="Internal")
    w2h_d = nc.dram_tensor("w2h", [8, 128, 4096], BF16, kind="Internal")
    w2l_d = nc.dram_tensor("w2l", [8, 128, 4096], BF16, kind="Internal")
    woh_d = nc.dram_tensor("woh", [8, 128, 1024], BF16, kind="Internal")
    wol_d = nc.dram_tensor("wol", [8, 128, 1024], BF16, kind="Internal")

    alpha_d = nc.dram_tensor("alphasm", [4, 64, TC], F32, kind="Internal")
    x2kind = "ExternalOutput" if "x2" in debug_outputs else "Internal"
    x2s_d = nc.dram_tensor("dbg_x2" if x2kind == "ExternalOutput" else "x2scr",
                           [4, 8, 128, TC], F32, kind=x2kind)

    dbg = {}
    def dbg_out(name, shape):
        if name in debug_outputs:
            h = nc.dram_tensor("dbg_" + name, shape, F32, kind="ExternalOutput")
            dbg[name] = h
            return h
        return None

    dbg_xn = dbg_out("xn", [NCH, 128, 4 * D])
    dbg_st = dbg_out("st", [NCH, 128, 40])
    dbg_lg = dbg_out("lg", [NCH, 128, 256])
    dbg_h1 = dbg_out("h1", [NCH, 128, 2048])
    dbg_tk = dbg_out("tk", [NCH, 128, 16])
    dbg_bp = dbg_out("bp", [14, 64, 3072])
    dbg_hin = dbg_out("hin", [NCH, 128, 8 * TC])
    dbg_alpha = dbg_out("alpha", [NCH, 128, 4 * 64])
    dbg_si = dbg_out("si", [128, 512])
    dbg_den = dbg_out("den", [1, 64])
    dbg_snew = dbg_out("snew", [64, D])
    dbg_sv = dbg_out("sv", [64, D])
    dbg_fin0 = dbg_out("fin0", [128, 8 * TC])
    dbg_s0 = dbg_out("s0", [128, 8 * TC])
    dbg_g0 = dbg_out("g0", [128, 8 * TC])

    stack = ExitStack()
    with stack:
        p = Prog(nc, stack)
        sb = lambda name, shape: stack.enter_context(nc.sbuf_tensor(name, shape, F32))

        eye = sb("eye_t", [128, 128])
        onesc = sb("onesc_t", [128, 1])
        onesr = sb("onesr_t", [1, 128])
        flagc = sb("flagc_t", [128, 1])
        n1c = sb("n1c_t", [128, 8])
        n2c = sb("n2c_t", [128, 8])
        bgc = sb("bgc_t", [128, 8])
        bcc = sb("bcc_t", [128, 8])
        boc = sb("boc_t", [128, 8])
        b2c = sb("b2c_t", [64, 1])
        taub = sb("taub_t", [1, 1])
        itau1 = sb("itau1_t", [1, 1])
        itau64 = sb("itau64_t", [64, 1])
        b2s = sb("b2s_t", [64, 1])
        sinitT = sb("sinitT_t", [128, 8 * 64])
        smdm = sb("smdm_t", [128, 8])
        sm1c = sb("sm1c_t", [128, 4])
        sm1sc = sb("sm1sc_t", [128, 4])
        state = sb("state_t", [128, 8])
        actw = sb("actw_t", [128, 1])
        gwarm = sb("gwarm_t", [128, 128])
        p.act_trailer = lambda e: e.activation(actw[:], onesc[:], AF.Square)
        svb = sb("svb_t", [64, D])
        pb = [stack.enter_context(nc.psum_tensor("pb%d" % i, [128, 512], F32))
              for i in range(8)]

        def tclose(ch):
            return lambda e: e.matmul(pb[7][0:1, 0:1], onesr[0:1, 0:1],
                                      onesr[0:1, 0:1], start=True, stop=True)

        DBG = p.ch("DBG")
        RQF = p.ch("RQF")
        AKF = p.ch("AKF")

        def act_fence(func):
            p.emit("scalar", lambda e, f=func: e.activation(actw[:], onesc[:], f))
            p.emit("scalar", lambda e, s_=RQF.sem: e.sem_inc(s_, 1))
            RQF.n += 1
            p.wait("gpsimd", RQF, RQF.n)
            for _ in range(6):
                p.emit("gpsimd", lambda e: e.tensor_copy(gwarm[:], gwarm[:]))
            p.emit("gpsimd", lambda e, s_=AKF.sem: e.drain().then_inc(s_, 1))
            AKF.n += 1
            p.wait("scalar", AKF, AKF.n)

        with nc.Block() as block:
            with ExitStack() as ascope:
                asb = lambda name, shape, dt_=F32: ascope.enter_context(nc.sbuf_tensor(name, shape, dt_))
                wgh = asb("wgh", [128, 8192], BF16)
                wgl = asb("wgl", [128, 8192], BF16)
                wch = asb("wch", [128, 8192], BF16)
                wcl = asb("wcl", [128, 8192], BF16)
                w1a = asb("w1at", [128, 4096])
                w2r = asb("w2rt", [128, 256])
                xld = asb("xld", [128, 2 * 4 * D])
                w1bt = xld
                xn_tm = asb("xn_tm", [128, 4 * D])
                xn_fm = asb("xn_fm", [128, 8 * TC])
                hinh = asb("hinh", [128, 8 * TC], BF16)
                hinl = asb("hinl", [128, 8 * TC], BF16)
                h1b = asb("h1b", [128, 4 * TC])
                scr = asb("scr", [128, D])
                b1r = scr[0:1, 0:512]
                sm1r = scr[0:1, 512:1024]
                ab = asb("ab_t", [128, TC])
                cb = asb("cb_t", [128, TC])
                bbb = asb("bb_t", [128, TC])
                hbb = asb("hb_t", [128, TC])
                lgf = asb("lgf", [64, TC])
                lgtm = asb("lgtm", [128, 256])
                cur = asb("cur", [128, 256])
                eb = asb("eb", [128, 256])
                aet = asb("aet", [128, 256])
                alf = asb("alf", [128, 256])
                r1r4 = asb("r1r4", [4, 128])
                rowr1 = asb("rowr1", [1, 512])
                mall = asb("mall", [128, 32])
                rhoall = asb("rhoall", [128, 32])
                r1all = asb("r1all", [128, 32])
                mkc = asb("mkc", [128, 4])
                nmc = asb("nmc", [128, 4])
                thrc = asb("thrc", [128, 4])
                zc = asb("zc", [128, 4])
                rzc = asb("rzc", [128, 4])
                spst = asb("spst", [128, 2 * 512])
                sphi = asb("sphi", [128, 2 * 512], BF16)
                splo = asb("splo", [128, 2 * 512], BF16)

                LDW = p.ch("LDW"); LDX = p.ch("LDX")
                IT = p.ch("IT"); IV = p.ch("IV"); IS = p.ch("IS")
                RST0 = p.ch("RST0")
                WLD = p.ch("WLD"); WSP = p.ch("WSP")

                def ld(dst_ap, src_ap):
                    p.emit("sync", lambda e, o=dst_ap, i=src_ap: e.dma_start(out=o, in_=i),
                           inc=LDW, k=16)

                ld(eye[:], eye_d.ap())
                ld(onesc[:], onesc_d.ap())
                ld(onesr[:], onesr_d.ap())
                ld(flagc[:], flag_d.ap())
                ld(n1c[:], n1col_d.ap())
                ld(n2c[:], n2col_d.ap())
                ld(bgc[:], bgcol_d.ap())
                ld(bcc[:], bccol_d.ap())
                ld(boc[:], bocol_d.ap())
                ld(b2c[:], b2col_d.ap())
                ld(taub[:], tau_d.ap())
                ld(sinitT[:].rearrange("p (k s) -> p k s", k=8), sinitT_d.ap())
                ld(w1a[:].rearrange("p (k e c) -> p k e c", k=8, e=4), w1a_d.ap())
                ld(w2r[:].rearrange("p (d s) -> p d s", d=4), w2r_d.ap().rearrange("d p s -> p d s"))
                N_LDW = LDW.n

                for (wd_, hi_, lo_) in ((wg_d, wgh, wgl), (wc_d, wch, wcl)):
                    p.wait("sync", WSP, WSP.n)
                    p.emit("sync", lambda e, i=wd_: e.dma_start(
                        out=xld[:].rearrange("p (k e c) -> p k e c", k=8, e=8),
                        in_=i.ap()), inc=WLD, k=16)
                    p.wait("vector", LDW, N_LDW)
                    p.wait("vector", WLD, WLD.n)
                    for kc in range(8):
                        p.emit("vector", lambda e, k=kc: e.tensor_scalar(
                            xld[:, k * 1024:(k + 1) * 1024], xld[:, k * 1024:(k + 1) * 1024],
                            n1c[:, k:k + 1], None, OP.mult))
                    p.emit("vector", lambda e, h_=hi_: e.tensor_copy(h_[:], xld[:]))
                    p.emit("vector", lambda e, h_=hi_, l_=lo_: e.tensor_tensor(
                        l_[:], xld[:], h_[:], OP.subtract), inc=WSP)
                p.wait("sync", WSP, 2)
                ld(xld[:, 0:4096].rearrange("p (k t) -> p k t", k=8),
                   w1b_d.ap().rearrange("k p t -> p k t"))
                ld(b1r, b1row_d.ap())
                N_LDW2 = LDW.n
                p.wait("vector", LDW, N_LDW)
                p.emit("vector", lambda e: e.memzero(state[:]))
                p.emit("vector", lambda e: e.tensor_reduce(
                    smdm[:].rearrange("p (k o) -> p k o", k=8, o=1),
                    sinitT[:].rearrange("p (k s) -> p k s", k=8), AX.X, OP.add))
                p.emit("vector", lambda e: e.tensor_scalar(smdm[:], smdm[:], 1.0 / 64, None, OP.mult),
                       inc=IV)
                p.emit("gpsimd", lambda e: e.memset(gwarm[:], 1.0))
                p.wait("scalar", LDW, N_LDW)
                act_fence(AF.Ln)
                p.emit("scalar", lambda e: e.activation(itau1[:], taub[:], AF.Abs), inc=IS)
                p.wait("vector", IS)
                p.emit("vector", lambda e: e.tensor_scalar(itau1[:], itau1[:], 0.1, None, OP.add))
                p.emit("vector", lambda e: e.reciprocal(itau1[:], itau1[:]), inc=IV)
                p.wait("tensor", LDW, N_LDW2)
                p.wait("tensor", IV)
                p.emit("tensor", _mm(pb[7][0:64, 0:1], onesr[:, 0:64], itau1[:], True, True), inc=IT)
                for kc in range(8):
                    p.emit("tensor", _mm(pb[6][0:1, 0:512], smdm[:, kc:kc + 1],
                                         w1bt[:, kc * 512:(kc + 1) * 512], kc == 0, kc == 7),
                           inc=(IT if kc == 7 else None))
                p.wait("vector", IT)
                p.emit("vector", lambda e: e.tensor_copy(itau64[:], pb[7][0:64, 0:1]))
                p.emit("vector", lambda e: e.tensor_tensor(b2s[:], b2c[:], itau64[:], OP.mult))
                p.emit("vector", lambda e: e.tensor_tensor(sm1r, pb[6][0:1, 0:512], b1r, OP.add),
                       inc=IV)
                p.wait("tensor", IV)
                for j in range(4):
                    p.emit("tensor", _mm(pb[5][0:128, j:j + 1], scr[0:1, 512 + j * 128:512 + (j + 1) * 128],
                                         onesc[0:1, 0:1], j == 0, j == 3),
                           inc=(IT if j == 3 else None))
                p.wait("vector", IT)
                p.emit("vector", lambda e: e.tensor_copy(sm1c[:], pb[5][0:128, 0:4]))
                p.emit("vector", lambda e: e.tensor_scalar(sm1sc[:], sm1c[:], RS2, None, OP.mult),
                       inc=IV)
                N_IV = IV.n

                PLX = p.ch("PLX"); PSQ = p.ch("PSQ"); PRD = p.ch("PRD")
                s1c = h1b[:, 0:32]
                s2c = h1b[:, 32:64]
                varc = h1b[:, 64:96]
                vpec = h1b[:, 96:128]
                recc = h1b[:, 128:160]
                rpc = h1b[:, 160:192]
                for c in range(NCH):
                    xb0 = xld[:, (c % 2) * 4 * D:(c % 2 + 1) * 4 * D]
                    p.wait("sync", PSQ, 4 * (c - 1))
                    p.wait("sync", PRD, 4 * (c - 1))
                    if c < 2:
                        p.wait("sync", WSP, 2)
                    if c == 0:
                        p.wait("sync", IT, IT.n)
                    p.emit("sync", lambda e, c=c, o=xb0: e.dma_start(
                        out=o.rearrange("p (j d) -> p j d", j=4),
                        in_=xin.ap()[c * TC:(c + 1) * TC, :].rearrange(
                            "(j p) d -> p j d", j=4)), inc=PLX, k=16)
                    p.wait("scalar", PLX, 16 * (c + 1))
                    if c == 0:
                        p.wait("scalar", LDW, N_LDW)
                        act_fence(AF.Ln)
                    for j in range(4):
                        p.emit("scalar", lambda e, x=xb0[:, j * D:(j + 1) * D], c=c, j=j:
                               e.activation(scr[:], x, AF.Square,
                                            accum_out=h1b[:, 32 + c * 4 + j:32 + c * 4 + j + 1]),
                               inc=PSQ)
                    p.wait("vector", PLX, 16 * (c + 1))
                    for j in range(4):
                        p.emit("vector", lambda e, x=xb0[:, j * D:(j + 1) * D], c=c, j=j:
                               e.tensor_reduce(h1b[:, c * 4 + j:c * 4 + j + 1], x,
                                               AX.X, OP.add), inc=PRD)
                p.wait("vector", PSQ, 32)
                p.emit("vector", lambda e: e.tensor_scalar(mall[:], s1c, INV_D, None, OP.mult))
                p.emit("vector", lambda e: e.tensor_scalar(s2c, s2c, INV_D, None, OP.mult))
                p.emit("vector", lambda e: e.tensor_tensor(varc, mall[:], mall[:], OP.mult))
                p.emit("vector", lambda e: e.tensor_tensor(varc, s2c, varc, OP.subtract))
                p.emit("vector", lambda e: e.tensor_scalar(vpec, varc, 1e-5, None, OP.add))
                p.emit("vector", lambda e: e.reciprocal(recc, vpec))
                p.emit("vector", lambda e: e.tensor_tensor(rpc, varc, recc, OP.mult))
                p.emit("vector", lambda e: e.tensor_scalar(rpc, rpc, 1e-6, None, OP.add),
                       inc=PRD)
                p.wait("scalar", PRD, PRD.n)
                p.emit("scalar", lambda e: e.activation(s1c, vpec, AF.Ln))
                p.emit("scalar", lambda e: e.activation(s2c, rpc, AF.Ln))
                p.emit("scalar", lambda e: e.activation(rhoall[:], s1c, AF.Exp, scale=-0.5))
                p.emit("scalar", lambda e: e.activation(r1all[:], s2c, AF.Exp, scale=-0.5),
                       inc=RST0)
                act_fence(AF.Sigmoid)

                SQ = p.ch("SQ"); VST = p.ch("VST"); RST = p.ch("RST"); XNT = p.ch("XNT")
                TG = p.ch("TG"); TGE = p.ch("TGE"); BC = p.ch("BC"); HIN = p.ch("HIN")
                H1T = p.ch("H1T"); H1S = p.ch("H1S"); H1V = p.ch("H1V")
                LGT = p.ch("LGT"); LGE = p.ch("LGE"); TOPK = p.ch("TOPK")
                SEXP = p.ch("SEXP"); ALPHA = p.ch("ALPHA")
                GCT = p.ch("GCT"); GCE = p.ch("GCE"); SCE = p.ch("SCE")
                SLT = p.ch("SLT"); SPL = p.ch("SPL"); RR = p.ch("RR")

                WIN = p.ch("WIN"); SPV = p.ch("SPV"); SPW = p.ch("SPW")
                ASM = p.ch("ASM")
                split_jobs = []
                for i in range(32):
                    for (srcw, hiw, low) in ((w1f_d, w1h_d, w1l_d),
                                             (w3f_d, w3h_d, w3l_d)):
                        for q in range(2):
                            split_jobs.append((
                                srcw.ap()[i][:, q * 4:(q + 1) * 4, :].rearrange(
                                    "p k e -> p (k e)"),
                                hiw.ap()[i][:, q * 512:(q + 1) * 512],
                                low.ap()[i][:, q * 512:(q + 1) * 512]))
                for g in range(8):
                    for q in range(8):
                        split_jobs.append((
                            w2f_d.ap()[g][:, q * 4:(q + 1) * 4, :].rearrange(
                                "p k e -> p (k e)"),
                            w2h_d.ap()[g][:, q * 512:(q + 1) * 512],
                            w2l_d.ap()[g][:, q * 512:(q + 1) * 512]))
                for g in range(8):
                    for q in range(2):
                        split_jobs.append((
                            wo_d.ap()[g][:, q * 4:(q + 1) * 4, :].rearrange(
                                "p k e -> p (k e)"),
                            woh_d.ap()[g][:, q * 512:(q + 1) * 512],
                            wol_d.ap()[g][:, q * 512:(q + 1) * 512]))

                NJOBS = len(split_jobs)

                def _sp_load(i):
                    o = (i % 2) * 512
                    p.emit("gpsimd", lambda e, oo=spst[:, o:o + 512],
                           ii=split_jobs[i][0]: e.dma_start(out=oo, in_=ii),
                           inc=WIN, k=16)

                def pump_split(n):
                    for _ in range(n):
                        i = pump_split.i
                        if i >= NJOBS:
                            return
                        if i == 0:
                            _sp_load(0)
                            if NJOBS > 1:
                                _sp_load(1)
                        _, hi_, lo_ = split_jobs[i]
                        o = (i % 2) * 512
                        p.wait("gpsimd", WIN, 16 * (i + 1))
                        p.wait("gpsimd", SPW, 32 * (i - 1))
                        p.emit("gpsimd", lambda e, o=o: e.tensor_copy(
                            sphi[:, o:o + 512], spst[:, o:o + 512]))
                        for _ in range(3):
                            p.emit("gpsimd", lambda e: e.engine_nop())
                        p.emit("gpsimd", lambda e, o=o: e.tensor_tensor(
                            splo[:, o:o + 512], spst[:, o:o + 512],
                            sphi[:, o:o + 512], OP.subtract), inc=SPV)
                        p.wait("gpsimd", SPV, SPV.n)
                        p.emit("gpsimd", lambda e, oo=hi_, o=o: e.dma_start(
                            out=oo, in_=sphi[:, o:o + 512]), inc=SPW, k=16)
                        p.emit("gpsimd", lambda e, oo=lo_, o=o: e.dma_start(
                            out=oo, in_=splo[:, o:o + 512]), inc=SPW, k=16)
                        if i + 2 < NJOBS:
                            _sp_load(i + 2)
                        pump_split.i += 1
                pump_split.i = 0

                for c in range(NCH):
                    mine = c >= MY0
                    xb = xld[:, (c % 2) * 4 * D:(c % 2 + 1) * 4 * D]
                    tg_base = TG.n

                    if c < 2:
                        p.wait("sync", PSQ, 32)
                        p.wait("sync", PRD, 32)
                    p.wait("sync", XNT, c - 1)
                    p.emit("sync", lambda e, c=c, o=xb: e.dma_start(
                        out=o.rearrange("p (j d) -> p j d", j=4),
                        in_=xin.ap()[c * TC:(c + 1) * TC, :].rearrange("(j p) d -> p j d", j=4)),
                        inc=LDX, k=16)

                    for dh in range(4):
                        p.wait("scalar", H1T, 4 * c + dh + 1)
                        p.wait("scalar", H1V, 4 * c + dh)
                        p.emit("scalar", lambda e, dh=dh: e.activation(
                            scr[:, 0:512], pb[5][:, 0:512], AF.Erf,
                            bias=sm1sc[:, dh:dh + 1], scale=RS2))
                        p.emit("scalar", lambda e, dh=dh: e.activation(
                            scr[:, 512:1024], pb[5][:, 0:512], AF.Identity,
                            bias=sm1c[:, dh:dh + 1]), inc=H1S)
                    p.wait("scalar", LGT, c + 1)
                    if c > MY0:
                        p.wait("scalar", ASM, 16 * (c - MY0))
                    p.emit("scalar", lambda e: e.activation(
                        lgf[:], pb[6][0:64, 0:512], AF.Identity, bias=b2s[:], scale=itau64[:]),
                        inc=LGE)
                    for ec in range(8):
                        p.wait("scalar", GCT, 16 * c + 2 * ec + 1)
                        p.wait("scalar", SCE, 8 * c + ec)
                        p.emit("scalar", lambda e, ec=ec: e.activation(
                            ab[:], pb[2][:, 0:512], AF.Sigmoid, bias=bgc[:, ec:ec + 1]), inc=GCE)
                        p.wait("scalar", GCT, 16 * c + 2 * ec + 2)
                        p.emit("scalar", lambda e, ec=ec: e.activation(
                            cb[:], pb[3][:, 0:512], AF.Tanh, bias=bcc[:, ec:ec + 1]), inc=GCE)
                    p.wait("scalar", TOPK, c + 1)
                    for j in range(4):
                        p.emit("scalar", lambda e, j=j: e.activation(
                            eb[:, j * 64:(j + 1) * 64], lgtm[:, j * 64:(j + 1) * 64],
                            AF.Tanh, bias=nmc[:, j:j + 1], scale=0.5),
                            inc=(SEXP if j == 3 else None))

                    p.wait("vector", LDX, 16 * (c + 1))
                    if c == 0:
                        p.wait("vector", RST0, 1)
                    p.wait("vector", SLT, c)
                    for j in range(4):
                        p.emit("vector", lambda e, x=xb[:, j * D:(j + 1) * D], c=c, j=j:
                               e.tensor_scalar(
                            xn_tm[:, j * D:(j + 1) * D], x,
                            mall[:, c * 4 + j:c * 4 + j + 1],
                            rhoall[:, c * 4 + j:c * 4 + j + 1],
                            OP.subtract, OP.mult), inc=(XNT if j == 3 else None))
                    if dbg_xn is not None:
                        p.wait("gpsimd", XNT, c + 1)
                        p.emit("gpsimd", lambda e, c=c: e.dma_start(
                            out=dbg_xn.ap()[c], in_=xn_tm[:]), inc=DBG, k=16)
                    if dbg_st is not None:
                        p.emit("gpsimd", lambda e, c=c: e.dma_start(
                            out=dbg_st.ap()[c][:, 0:4], in_=mall[:, c * 4:(c + 1) * 4]),
                            inc=DBG, k=16)
                        p.emit("gpsimd", lambda e, c=c: e.dma_start(
                            out=dbg_st.ap()[c][:, 4:8], in_=rhoall[:, c * 4:(c + 1) * 4]),
                            inc=DBG, k=16)
                    p.wait("vector", H1T, 4 * c)
                    if c > MY0:
                        p.wait("vector", SPL, 16 * (c - MY0))
                    for g in range(8):
                        p.wait("vector", TG, tg_base + g + 1)
                        p.emit("vector", lambda e, g=g: e.tensor_copy(
                            xn_fm[:, g * TC:(g + 1) * TC], pb[4][:, 0:512]), inc=TGE)
                    p.wait("vector", TG, tg_base + 9)
                    p.wait("vector", RR, 16 * c)
                    p.emit("vector", lambda e: e.tensor_copy(r1r4[:], pb[4][0:4, 0:128]), inc=TGE)
                    p.wait("sync", TGE, TGE.n)
                    p.emit("sync", lambda e: e.dma_start(out=rowr1[:], in_=r1r4[:]),
                           inc=RR, k=16)
                    pump_split(26)
                    p.wait("vector", BC, c + 1)
                    for g in range(8):
                        p.emit("vector", lambda e, g=g: e.tensor_tensor(
                            hbb[:], xn_fm[:, g * TC:(g + 1) * TC],
                            pb[6][:, 0:512], OP.mult))
                        p.emit("vector", lambda e, g=g: e.tensor_copy(
                            hinh[:, g * TC:(g + 1) * TC], hbb[:]))
                        p.emit("vector", lambda e, g=g: e.tensor_tensor(
                            hinl[:, g * TC:(g + 1) * TC], hbb[:],
                            hinh[:, g * TC:(g + 1) * TC], OP.subtract),
                            inc=(HIN if g == 7 else None))
                    if dbg_hin is not None:
                        p.wait("gpsimd", HIN, c + 1)
                        p.emit("gpsimd", lambda e, c=c: e.dma_start(
                            out=dbg_hin.ap()[c], in_=hinh[:]), inc=DBG, k=16)
                    for dh in range(4):
                        p.wait("vector", H1S, 4 * c + dh + 1)
                        p.emit("vector", lambda e: e.tensor_scalar(
                            scr[:, 0:512], scr[:, 0:512], 0.5, 0.5, OP.mult, OP.add))
                        p.emit("vector", lambda e, dh=dh: e.tensor_tensor(
                            h1b[:, dh * TC:(dh + 1) * TC], scr[:, 0:512], scr[:, 512:1024],
                            OP.mult), inc=H1V)
                    p.wait("vector", TG, tg_base + 10)
                    p.emit("vector", lambda e: e.tensor_copy(lgtm[:], pb[4][:, 0:256]), inc=TGE)
                    p.emit("vector", lambda e: e.tensor_copy(cur[:], lgtm[:]))
                    for r in range(8):
                        dst = mkc if r == 0 else (thrc if r == 7 else zc)
                        p.emit("vector", lambda e, d=dst: e.tensor_reduce(
                            d[:].rearrange("p (j o) -> p j o", j=4, o=1),
                            cur[:].rearrange("p (j s) -> p j s", j=4), AX.X, OP.max))
                        if r < 7:
                            for j in range(4):
                                p.emit("vector", lambda e, j=j, d=dst: e.tensor_scalar(
                                    eb[:, j * 64:(j + 1) * 64], cur[:, j * 64:(j + 1) * 64],
                                    d[:, j:j + 1], None, OP.is_equal))
                            for j in range(4):
                                p.emit("vector", lambda e, j=j: e.scalar_tensor_tensor(
                                    cur[:, j * 64:(j + 1) * 64], eb[:, j * 64:(j + 1) * 64],
                                    -1e30, cur[:, j * 64:(j + 1) * 64], OP.mult, OP.add))
                    p.emit("vector", lambda e: e.tensor_scalar(nmc[:], mkc[:], -0.5, None, OP.mult),
                           inc=TOPK)
                    for ec in range(8):
                        p.wait("vector", GCE, 16 * c + 2 * ec + 2)
                        p.emit("vector", lambda e: e.tensor_scalar(
                            bbb[:], ab[:], -1.0, 1.0, OP.mult, OP.add))
                        p.emit("vector", lambda e: e.tensor_tensor(bbb[:], bbb[:], cb[:], OP.mult))
                        p.emit("vector", lambda e, ec=ec: e.tensor_tensor_scan(
                            hbb[:], ab[:], bbb[:], state[:, ec:ec + 1], OP.mult, OP.add))
                        if mine:
                            p.emit("vector", lambda e, ec=ec: e.tensor_tensor(
                                xn_fm[:, ec * TC:(ec + 1) * TC], xn_fm[:, ec * TC:(ec + 1) * TC],
                                hbb[:], OP.add), inc=SCE)
                        else:
                            p.emit("vector", lambda e: e.tensor_copy(bbb[:], hbb[:]), inc=SCE)
                        p.emit("vector", lambda e, ec=ec: e.tensor_copy(
                            state[:, ec:ec + 1], hbb[:, TC - 1:TC]))
                    if c == MY0 - 1:
                        p.emit("vector", lambda e: e.tensor_scalar(
                            state[:], state[:], flagc[:, 0:1], None, OP.mult))
                    p.wait("vector", SEXP, c + 1)
                    p.emit("vector", lambda e: e.tensor_scalar(cur[:], eb[:], -1.0, 1.0,
                                                               OP.mult, OP.add))
                    p.emit("vector", lambda e: e.reciprocal(cur[:], cur[:]))
                    p.emit("vector", lambda e: e.tensor_scalar(eb[:], eb[:], 1.0, None, OP.add))
                    p.emit("vector", lambda e: e.tensor_tensor(eb[:], eb[:], cur[:], OP.mult))
                    for j in range(4):
                        p.emit("vector", lambda e, j=j: e.tensor_scalar(
                            aet[:, j * 64:(j + 1) * 64], lgtm[:, j * 64:(j + 1) * 64],
                            thrc[:, j:j + 1], None, OP.is_ge))
                    p.emit("vector", lambda e: e.tensor_tensor(aet[:], aet[:], eb[:], OP.mult))
                    p.emit("vector", lambda e: e.tensor_reduce(
                        zc[:].rearrange("p (j o) -> p j o", j=4, o=1),
                        aet[:].rearrange("p (j s) -> p j s", j=4), AX.X, OP.add))
                    p.emit("vector", lambda e: e.reciprocal(rzc[:], zc[:]))
                    for j in range(4):
                        p.emit("vector", lambda e, j=j: e.tensor_scalar(
                            alf[:, j * 64:(j + 1) * 64], aet[:, j * 64:(j + 1) * 64],
                            rzc[:, j:j + 1], None, OP.mult), inc=(ALPHA if j == 3 else None))
                    if dbg_lg is not None:
                        p.wait("gpsimd", TOPK, c + 1)
                        p.emit("gpsimd", lambda e, c=c: e.dma_start(
                            out=dbg_lg.ap()[c], in_=lgtm[:]), inc=DBG, k=16)
                        for i_, t_ in enumerate([mkc, nmc, thrc, zc]):
                            p.emit("gpsimd", lambda e, c=c, i_=i_, t_=t_: e.dma_start(
                                out=dbg_tk.ap()[c][:, i_ * 4:(i_ + 1) * 4], in_=t_[:]),
                                inc=DBG, k=16)
                    if dbg_h1 is not None:
                        p.wait("gpsimd", H1V, 4 * (c + 1))
                        p.emit("gpsimd", lambda e, c=c: e.dma_start(
                            out=dbg_h1.ap()[c], in_=h1b[:]), inc=DBG, k=16)
                    if dbg_alpha is not None:
                        p.wait("gpsimd", ALPHA, c + 1)
                        p.emit("gpsimd", lambda e, c=c: e.dma_start(
                            out=dbg_alpha.ap()[c], in_=alf[:]), inc=DBG, k=16)
                    if mine:
                        p.wait("vector", TG, tg_base + 11)
                        p.emit("vector", lambda e: e.tensor_copy(
                            lgf[0:64, 0:512], pb[4][0:64, 0:512]), inc=TGE)
                        p.wait("gpsimd", TGE, TGE.n)
                        p.emit("gpsimd", lambda e, c=c: e.dma_start(
                            out=alpha_d.ap()[c - MY0], in_=lgf[0:64, 0:512]),
                            inc=ASM, k=16)

                    if c == 0:
                        p.wait("tensor", IV, N_IV)
                    p.wait("tensor", XNT, c + 1)
                    for g in range(8):
                        p.wait("tensor", TGE, tg_base + g)
                        for j in range(4):
                            p.emit("tensor", _mm(pb[4][:, j * 128:(j + 1) * 128],
                                                 xn_tm[:, j * D + g * 128:j * D + (g + 1) * 128],
                                                 eye[:], j == 0, j == 3, is_t=True))
                        p.emit("tensor", tclose(TG), inc=TG)
                    if c == 0:
                        p.wait("tensor", RST0, 1)
                    p.wait("tensor", TGE, tg_base + 8)
                    p.emit("tensor", _mm(pb[4][0:4, 0:128], r1all[:, c * 4:(c + 1) * 4],
                                         eye[:], True, True, is_t=True))
                    p.emit("tensor", tclose(TG), inc=TG)
                    p.wait("tensor", RR, 16 * (c + 1))
                    p.wait("tensor", LGE, c)
                    p.emit("tensor", _mm(pb[6][:, 0:512], onesr[:], rowr1[:], True, True),
                           inc=BC)
                    for dh in range(4):
                        p.wait("tensor", H1S, 4 * c + dh)
                        for kc in range(8):
                            p.emit("tensor", _mm(pb[5][:, 0:512],
                                                 w1a[:, (kc * 4 + dh) * 128:(kc * 4 + dh + 1) * 128],
                                                 xn_fm[:, kc * TC:(kc + 1) * TC], kc == 0, kc == 7),
                                   inc=(H1T if kc == 7 else None))
                    p.wait("tensor", HIN, c + 1)
                    p.wait("tensor", H1V, 4 * (c + 1))
                    for dh in range(4):
                        p.emit("tensor", _mm(pb[6][0:64, 0:512], w2r[:, dh * 64:(dh + 1) * 64],
                                             h1b[:, dh * TC:(dh + 1) * TC], dh == 0, dh == 3),
                               inc=(LGT if dh == 3 else None))
                    p.wait("tensor", LGE, c + 1)
                    p.wait("tensor", TGE, tg_base + 9)
                    for j in range(4):
                        p.emit("tensor", _mm(pb[4][:, j * 64:(j + 1) * 64],
                                             lgf[:, j * 128:(j + 1) * 128], eye[0:64, 0:64],
                                             j == 0, j == 3, is_t=True))
                    p.emit("tensor", tclose(TG), inc=TG)
                    p.wait("tensor", HIN, c + 1)
                    for ec in range(8):
                        for bank, wh_, wl_, gch in ((2, wgh, wgl, 0), (3, wch, wcl, 1)):
                            p.wait("tensor", GCE, 16 * c + 2 * ec - 1 + gch)
                            for kc in range(8):
                                sl = slice((kc * 8 + ec) * 128, (kc * 8 + ec + 1) * 128)
                                tsl = slice(kc * TC, (kc + 1) * TC)
                                p.emit("tensor", _mm(pb[bank][:, 0:512], wh_[:, sl],
                                                     hinh[:, tsl], kc == 0, False))
                                p.emit("tensor", _mm(pb[bank][:, 0:512], wh_[:, sl],
                                                     hinl[:, tsl], False, False))
                            for kc in range(8):
                                sl = slice((kc * 8 + ec) * 128, (kc * 8 + ec + 1) * 128)
                                tsl = slice(kc * TC, (kc + 1) * TC)
                                p.emit("tensor", _mm(pb[bank][:, 0:512], wl_[:, sl],
                                                     hinh[:, tsl], False, kc == 7),
                                       inc=(GCT if kc == 7 else None))
                    p.wait("tensor", ALPHA, c + 1)
                    first = (c == 0)
                    last = (c == NCH - 1)
                    for j in range(4):
                        for dchi in range(8):
                            p.emit("tensor", _mm(
                                pb[0][:, dchi * 64:(dchi + 1) * 64],
                                xn_tm[:, j * D + dchi * 128:j * D + (dchi + 1) * 128],
                                alf[:, j * 64:(j + 1) * 64],
                                first and j == 0 and dchi == 0, last and j == 3 and dchi == 7))
                        p.emit("tensor", _mm(pb[1][0:1, 0:64], onesc[:],
                                             alf[:, j * 64:(j + 1) * 64],
                                             first and j == 0, last and j == 3),
                               inc=(SLT if j == 3 else None))
                    if mine:
                        p.wait("tensor", TGE, tg_base + 10)
                        for j in range(4):
                            p.emit("tensor", _mm(pb[4][0:64, j * 128:(j + 1) * 128],
                                                 alf[:, j * 64:(j + 1) * 64], eye[:],
                                                 j == 0, j == 3, is_t=True))
                        p.emit("tensor", tclose(TG), inc=TG)

                    if dbg and "serial" in debug_outputs:
                        p.wait("gpsimd", DBG, DBG.n)
                        p.barrier()
                    if mine:
                        p.wait("gpsimd", SCE, 8 * c + 8)
                        p.emit("gpsimd", lambda e, c=c: e.dma_start(
                            out=x2s_d.ap()[c - MY0].rearrange("g p t -> p g t"),
                            in_=xn_fm[:].rearrange("p (g t) -> p g t", g=8)), inc=SPL, k=16)
                assert pump_split.i == NJOBS
                p.wait("gpsimd", SPL, 64)
                p.wait("gpsimd", SPW, SPW.n)
                p.wait("gpsimd", ASM, 64)
                if dbg:
                    p.wait("gpsimd", DBG, DBG.n)
                p.barrier()
                p.flush(block)

            with ExitStack() as bscope:
                bsb = lambda name, shape: bscope.enter_context(nc.sbuf_tensor(name, shape, F32))
                wstr = bsb("wstr", [128, 2 * 3072])
                gis = bsb("gis", [64, 3072])
                ghs = bsb("ghs", [64, 3072])
                bihb = bsb("bihb", [64, 3072])
                bhhb = bsb("bhhb", [64, 3072])
                bqb = bsb("bqb", [64, 256])
                bkb = bsb("bkb", [64, 256])
                bvb = bsb("bvb", [64, D])
                bvpb = bsb("bvpb", [64, D])
                lngb = bsb("lngb", [64, D])
                lnbb = bsb("lnbb", [64, D])
                rowB = bsb("rowB", [1, 3072])
                sis = bsb("sis", [128, 512])
                sidm = bsb("sidm", [128, 512])
                dres = bsb("dres", [1, 64])
                sS = bsb("sS", [64, D])
                tg1 = bsb("tg1", [64, D])
                tg2 = bsb("tg2", [64, D])
                rg = bsb("rg", [64, D])
                zg = bsb("zg", [64, D])
                ngb = bsb("ngb", [64, D])
                snew = bsb("snew", [64, D])
                sndm = bsb("sndm", [128, 512])
                qv = bsb("qv", [64, 256])
                kvb = bsb("kvb", [64, 256])
                vv = bsb("vv", [64, D])
                qdm = bsb("qdm", [128, 128])
                kdm = bsb("kdm", [128, 128])
                sc64 = bsb("sc64", [64, 64])
                at64 = bsb("at64", [64, 64])
                yv = bsb("yv", [64, D])
                ydm = bsb("ydm", [128, 512])
                wqb = bsb("wqb", [128, 2048])
                wkb = bsb("wkb", [128, 2048])
                cB = bsb("cB", [64, 8])

                BL = p.ch("BL"); BT = p.ch("BT"); BV = p.ch("BV"); BS = p.ch("BS")

                def bld(dst_ap, src_ap):
                    p.emit("sync", lambda e, o=dst_ap, i=src_ap: e.dma_start(out=o, in_=i),
                           inc=BL, k=16)

                def ts_(fns, waits=()):
                    for ch, n in waits:
                        p.wait("tensor", ch, n)
                    for i, f in enumerate(fns):
                        p.emit("tensor", f, inc=(BT if i == len(fns) - 1 else None))

                def vs_(fns, waits=()):
                    for ch, n in waits:
                        p.wait("vector", ch, n)
                    for i, f in enumerate(fns):
                        p.emit("vector", f, inc=(BV if i == len(fns) - 1 else None))

                def ss_(fns, waits=()):
                    for ch, n in waits:
                        p.wait("scalar", ch, n)
                    for i, f in enumerate(fns):
                        p.emit("scalar", f, inc=(BS if i == len(fns) - 1 else None))

                bld(sS[:], sinit_d.ap())
                bld(wqb[:].rearrange("p (k t) -> p k t", k=8), wq_d.ap())
                bld(wkb[:].rearrange("p (k t) -> p k t", k=8), wk_d.ap())
                NB0 = BL.n

                p.wait("vector", p.ch("SLT"), 8)
                vs_([lambda e: e.tensor_copy(dres[:], pb[1][0:1, 0:64]),
                     lambda e: e.tensor_scalar(dres[:], dres[:], 1e-8, None, OP.add),
                     lambda e: e.reciprocal(dres[:], dres[:]),
                     lambda e: e.tensor_copy(sis[:], pb[0][:, 0:512])])
                if dbg_den is not None:
                    p.wait("gpsimd", BV, BV.n)
                    p.emit("gpsimd", lambda e: e.dma_start(out=dbg_den.ap(), in_=dres[:]),
                           inc=DBG, k=16)
                ts_([_mm(pb[2][0:128, 0:64], onesr[:], dres[:], True, True)], waits=[(BV, BV.n)])
                vs_([(lambda e, g=g: e.tensor_tensor(
                        sidm[:, g * 64:(g + 1) * 64], sis[:, g * 64:(g + 1) * 64],
                        pb[2][:, 0:64], OP.mult)) for g in range(8)],
                    waits=[(BT, BT.n)])
                if dbg_si is not None:
                    p.wait("gpsimd", BV, BV.n)
                    p.emit("gpsimd", lambda e: e.dma_start(out=dbg_si.ap(), in_=sidm[:]),
                           inc=DBG, k=16)

                RB = p.ch("RB")

                def bcast64(dst, src_d, n):
                    p.wait("sync", RB, RB.n)
                    bld(rowB[0:1, 0:n], src_d.ap())
                    nldw = BL.n
                    nseg = (n + 511) // 512
                    for i in range(nseg):
                        w = min(512, n - i * 512)
                        p.wait("tensor", p.ch("BL"), nldw)
                        p.wait("tensor", BV, BV.n)
                        p.emit("tensor", _mm(pb[3][0:64, 0:w], onesr[:, 0:64],
                                             rowB[0:1, i * 512:i * 512 + w], True, True),
                               inc=RB)
                        vs_([lambda e, dd=dst, ii=i, ww=w: e.tensor_copy(
                            dd[0:64, ii * 512:ii * 512 + ww], pb[3][0:64, 0:ww])],
                            waits=[(RB, RB.n)])

                bcast64(bihb, bihrow_d, 3072)
                bcast64(bhhb, bhhrow_d, 3072)
                bcast64(bqb, bqrow_d, 256)
                bcast64(bkb, bkrow_d, 256)
                bcast64(bvb, bvrow_d, D)
                bcast64(bvpb, bvprow_d, D)
                bcast64(lngb, lngrow_d, D)
                bcast64(lnbb, lnbrow_d, D)

                WS = p.ch("WS")
                ws_free = {0: 0, 1: 0}

                for wd, dstbuf, lhs in ((wihT_d, gis, sidm), (whhT_d, ghs, sinitT)):
                    for kc in range(8):
                        par = kc % 2
                        buf = wstr[:, par * 3072:(par + 1) * 3072]
                        p.wait("sync", WS, ws_free[par])
                        p.emit("sync", lambda e, o=buf, i=wd.ap()[kc]:
                               e.dma_start(out=o, in_=i), inc=BL, k=16)
                        p.wait("tensor", BL, BL.n)
                        if kc == 0:
                            p.wait("tensor", BV, BV.n)
                        for seg in range(6):
                            p.emit("tensor", _mm(pb[2 + seg][0:64, 0:512],
                                                 lhs[:, kc * 64:(kc + 1) * 64],
                                                 buf[:, seg * 512:(seg + 1) * 512],
                                                 kc == 0, kc == 7),
                                   inc=(WS if seg == 5 else None))
                        ws_free[par] = WS.n
                    vs_([(lambda e, dd=dstbuf, s=seg: e.tensor_copy(
                            dd[0:64, s * 512:(s + 1) * 512], pb[2 + s][0:64, 0:512]))
                         for seg in range(6)], waits=[(WS, WS.n)])

                for gate, dstg in ((0, rg), (1, zg)):
                    o = gate * D
                    vs_([lambda e, o=o: e.tensor_tensor(tg1[:], gis[0:64, o:o + D],
                                                        bihb[0:64, o:o + D], OP.add),
                         lambda e, o=o: e.tensor_tensor(tg2[:], ghs[0:64, o:o + D],
                                                        bhhb[0:64, o:o + D], OP.add),
                         lambda e: e.tensor_tensor(tg1[:], tg1[:], tg2[:], OP.add)])
                    if gate == 0:
                        act_fence(AF.Sigmoid)
                    ss_([lambda e, d=dstg: e.activation(d[:], tg1[:], AF.Sigmoid)],
                        waits=[(BV, BV.n)])
                    p.wait("vector", BS, BS.n)
                o = 2 * D
                vs_([lambda e: e.tensor_tensor(tg1[:], ghs[0:64, o:o + D],
                                               bhhb[0:64, o:o + D], OP.add),
                     lambda e: e.tensor_tensor(tg1[:], rg[:], tg1[:], OP.mult),
                     lambda e: e.tensor_tensor(tg2[:], gis[0:64, o:o + D],
                                               bihb[0:64, o:o + D], OP.add),
                     lambda e: e.tensor_tensor(tg1[:], tg1[:], tg2[:], OP.add)])
                ss_([lambda e: e.activation(ngb[:], tg1[:], AF.Tanh)], waits=[(BV, BV.n)])
                vs_([lambda e: e.tensor_scalar(tg1[:], zg[:], -1.0, 1.0, OP.mult, OP.add),
                     lambda e: e.tensor_tensor(tg1[:], tg1[:], ngb[:], OP.mult),
                     lambda e: e.tensor_tensor(tg2[:], zg[:], sS[:], OP.mult),
                     lambda e: e.tensor_tensor(snew[:], tg1[:], tg2[:], OP.add)],
                    waits=[(BS, BS.n), (BL, NB0)])
                if dbg_snew is not None:
                    p.wait("gpsimd", BV, BV.n)
                    p.emit("gpsimd", lambda e: e.dma_start(out=dbg_snew.ap(), in_=snew[:]),
                           inc=DBG, k=16)
                for ch_, n_ in [(BV, BV.n)]:
                    p.wait("tensor", ch_, n_)
                for g in range(8):
                    p.emit("tensor", _mm(pb[2][:, g * 64:(g + 1) * 64],
                                         snew[0:64, g * 128:(g + 1) * 128],
                                         eye[0:64, 0:64], g == 0, g == 7, is_t=True))
                p.emit("tensor", tclose(BT), inc=BT)
                vs_([lambda e: e.tensor_copy(sndm[:], pb[2][:, 0:512])], waits=[(BT, BT.n)])
                ts_([_mm(pb[3][0:64, 0:256], sndm[:, kc * 64:(kc + 1) * 64],
                         wqb[:, kc * 256:(kc + 1) * 256], kc == 0, kc == 7) for kc in range(8)]
                    + [_mm(pb[4][0:64, 0:256], sndm[:, kc * 64:(kc + 1) * 64],
                           wkb[:, kc * 256:(kc + 1) * 256], kc == 0, kc == 7) for kc in range(8)],
                    waits=[(BV, BV.n)])
                for kc in range(8):
                    par = kc % 2
                    vbuf = wstr[:, par * 3072:par * 3072 + 1024]
                    p.wait("sync", WS, ws_free[par])
                    p.emit("sync", lambda e, o=vbuf, i=wv_d.ap()[:, kc, :]:
                           e.dma_start(out=o, in_=i), inc=BL, k=16)
                    p.wait("tensor", BL, BL.n)
                    p.emit("tensor", _mm(pb[5][0:64, 0:512], sndm[:, kc * 64:(kc + 1) * 64],
                                         vbuf[:, 0:512], kc == 0, kc == 7))
                    p.emit("tensor", _mm(pb[6][0:64, 0:512], sndm[:, kc * 64:(kc + 1) * 64],
                                         vbuf[:, 512:1024], kc == 0, kc == 7),
                           inc=WS)
                    ws_free[par] = WS.n
                p.emit("tensor", lambda e: e.nop(), inc=BT)
                vs_([lambda e: e.tensor_tensor(qv[:], pb[3][0:64, 0:256], bqb[:], OP.add),
                     lambda e: e.tensor_tensor(kvb[:], pb[4][0:64, 0:256], bkb[:], OP.add),
                     lambda e: e.tensor_tensor(vv[:, 0:512], pb[5][0:64, 0:512],
                                               bvb[:, 0:512], OP.add),
                     lambda e: e.tensor_tensor(vv[:, 512:1024], pb[6][0:64, 0:512],
                                               bvb[:, 512:1024], OP.add)],
                    waits=[(BT, BT.n)])
                p.wait("tensor", BV, BV.n)
                for i in range(2):
                    p.emit("tensor", _mm(pb[2][:, i * 64:(i + 1) * 64],
                                         qv[0:64, i * 128:(i + 1) * 128],
                                         eye[0:64, 0:64], i == 0, i == 1, is_t=True))
                for i in range(2):
                    p.emit("tensor", _mm(pb[3][:, i * 64:(i + 1) * 64],
                                         kvb[0:64, i * 128:(i + 1) * 128],
                                         eye[0:64, 0:64], i == 0, i == 1, is_t=True))
                p.emit("tensor", tclose(BT), inc=BT)
                vs_([lambda e: e.tensor_copy(qdm[:], pb[2][:, 0:128]),
                     lambda e: e.tensor_copy(kdm[:], pb[3][:, 0:128])], waits=[(BT, BT.n)])
                ts_([_mm(pb[4][0:64, 0:64], qdm[:, i * 64:(i + 1) * 64],
                         kdm[:, i * 64:(i + 1) * 64], i == 0, i == 1) for i in range(2)],
                    waits=[(BV, BV.n)])
                ss_([lambda e: e.activation(sc64[:], pb[4][0:64, 0:64], AF.Copy, scale=0.0625)],
                    waits=[(BT, BT.n)])
                vs_([lambda e: e.tensor_reduce(cB[:, 0:1], sc64[:], AX.X, OP.max),
                     lambda e: e.tensor_scalar(cB[:, 1:2], cB[:, 0:1], -1.0, None, OP.mult)],
                    waits=[(BS, BS.n)])
                act_fence(AF.Exp)
                ss_([lambda e: e.activation(at64[:], sc64[:], AF.Exp, bias=cB[:, 1:2])],
                    waits=[(BV, BV.n)])
                vs_([lambda e: e.tensor_reduce(cB[:, 2:3], at64[:], AX.X, OP.add),
                     lambda e: e.reciprocal(cB[:, 3:4], cB[:, 2:3]),
                     lambda e: e.tensor_scalar(at64[:], at64[:], cB[:, 3:4], None, OP.mult)],
                    waits=[(BS, BS.n)])
                p.wait("tensor", BV, BV.n)
                p.emit("tensor", _mm(pb[2][0:64, 0:64], at64[:], eye[0:64, 0:64],
                                     True, True, is_t=True))
                p.emit("tensor", tclose(BT), inc=BT)
                vs_([lambda e: e.tensor_copy(sc64[:], pb[2][0:64, 0:64])], waits=[(BT, BT.n)])
                ts_([_mm(pb[5][0:64, 0:512], sc64[:], vv[:, 0:512], True, True),
                     _mm(pb[6][0:64, 0:512], sc64[:], vv[:, 512:1024], True, True)],
                    waits=[(BV, BV.n)])
                vs_([lambda e: e.tensor_tensor(tg2[:, 0:512], snew[0:64, 0:512],
                                               pb[5][0:64, 0:512], OP.add),
                     lambda e: e.tensor_tensor(tg2[:, 512:1024], snew[0:64, 512:1024],
                                               pb[6][0:64, 0:512], OP.add),
                     lambda e: e.tensor_reduce(cB[:, 0:1], tg2[:], AX.X, OP.add)],
                    waits=[(BT, BT.n)])
                ss_([lambda e: e.activation(tg1[:], tg2[:], AF.Square, accum_out=cB[:, 1:2])],
                    waits=[(BV, BV.n)])
                vs_([lambda e: e.tensor_scalar(cB[:, 4:5], cB[:, 0:1], INV_D, None, OP.mult),
                     lambda e: e.tensor_scalar(cB[:, 5:6], cB[:, 1:2], INV_D, None, OP.mult),
                     lambda e: e.tensor_tensor(cB[:, 2:3], cB[:, 4:5], cB[:, 4:5], OP.mult),
                     lambda e: e.tensor_tensor(cB[:, 2:3], cB[:, 5:6], cB[:, 2:3], OP.subtract),
                     lambda e: e.tensor_scalar(cB[:, 2:3], cB[:, 2:3], 1e-5, None, OP.add)],
                    waits=[(BS, BS.n)])
                ss_([lambda e: e.activation(cB[:, 6:7], cB[:, 2:3], AF.Ln),
                     lambda e: e.activation(cB[:, 3:4], cB[:, 6:7], AF.Exp, scale=-0.5)],
                    waits=[(BV, BV.n)])
                vs_([lambda e: e.tensor_scalar(yv[:], tg2[:], cB[:, 4:5], cB[:, 3:4],
                                               OP.subtract, OP.mult),
                     lambda e: e.tensor_tensor(tg1[:], yv[:], lngb[:], OP.mult),
                     lambda e: e.tensor_tensor(yv[:], tg1[:], lnbb[:], OP.add)],
                    waits=[(BS, BS.n)])
                p.wait("tensor", BV, BV.n)
                for g in range(8):
                    p.emit("tensor", _mm(pb[2][:, g * 64:(g + 1) * 64],
                                         yv[0:64, g * 128:(g + 1) * 128],
                                         eye[0:64, 0:64], g == 0, g == 7, is_t=True))
                p.emit("tensor", tclose(BT), inc=BT)
                vs_([lambda e: e.tensor_copy(ydm[:], pb[2][:, 0:512])], waits=[(BT, BT.n)])
                p.wait("tensor", BV, BV.n)
                for kc in range(8):
                    par = kc % 2
                    vbuf = wstr[:, par * 3072:par * 3072 + 1024]
                    p.wait("sync", WS, ws_free[par])
                    p.emit("sync", lambda e, o=vbuf, i=wvp_d.ap()[:, kc, :]:
                           e.dma_start(out=o, in_=i), inc=BL, k=16)
                    p.wait("tensor", BL, BL.n)
                    p.emit("tensor", _mm(pb[5][0:64, 0:512], ydm[:, kc * 64:(kc + 1) * 64],
                                         vbuf[:, 0:512], kc == 0, kc == 7))
                    p.emit("tensor", _mm(pb[6][0:64, 0:512], ydm[:, kc * 64:(kc + 1) * 64],
                                         vbuf[:, 512:1024], kc == 0, kc == 7),
                           inc=WS)
                    ws_free[par] = WS.n
                p.emit("tensor", lambda e: e.nop(), inc=BT)
                vs_([lambda e: e.tensor_tensor(svb[0:64, 0:512], pb[5][0:64, 0:512],
                                               bvpb[:, 0:512], OP.add),
                     lambda e: e.tensor_tensor(svb[0:64, 512:1024], pb[6][0:64, 0:512],
                                               bvpb[:, 512:1024], OP.add)],
                    waits=[(BT, BT.n)])
                if dbg_sv is not None:
                    p.wait("gpsimd", BV, BV.n)
                    p.emit("gpsimd", lambda e: e.dma_start(out=dbg_sv.ap(), in_=svb[:]),
                           inc=DBG, k=16)
                if dbg_bp is not None:
                    p.wait("gpsimd", BV, BV.n)
                    p.wait("gpsimd", BS, BS.n)
                    taps = [(gis[0:64, :], 3072), (ghs[0:64, :], 3072), (rg[:], 1024),
                            (zg[:], 1024), (ngb[:], 1024), (snew[:], 1024),
                            (qv[:], 256), (kvb[:], 256), (vv[:], 1024),
                            (sndm[0:64, :], 512), (at64[:], 64), (yv[:], 1024), (cB[:], 8), (tg2[:], 1024)]
                    for i_, (ap_, w_) in enumerate(taps):
                        p.emit("gpsimd", lambda e, i_=i_, a_=ap_, w_=w_: e.dma_start(
                            out=dbg_bp.ap()[i_][:, 0:w_], in_=a_), inc=DBG, k=16)
                p.barrier()
                p.flush(block)

            with ExitStack() as cscope:
                csb = lambda name, shape: cscope.enter_context(nc.sbuf_tensor(name, shape, F32))
                csbb = lambda name, shape: cscope.enter_context(
                    nc.sbuf_tensor(name, shape, BF16))
                x2t = csb("x2t", [128, 8 * TC])
                finth = csbb("finth", [128, 8 * TC])
                fintl = csbb("fintl", [128, 8 * TC])
                Pth = csbb("Pth", [128, 16 * TC])
                Ptl = csbb("Ptl", [128, 16 * TC])
                ust = csb("ust", [128, TC])
                st_ = csb("st_", [128, 8 * TC])
                osth = csbb("osth", [128, 8 * TC])
                ostl = csbb("ostl", [128, 8 * TC])
                gt = csb("gt", [128, TC])
                sgt = csb("sgt", [128, 8 * TC])
                finb = csb("finb", [128, 2 * D])
                xmt = csb("xmt", [128, 2 * D])
                w1b2h = csbb("w1b2h", [128, 2 * 1024])
                w1b2l = csbb("w1b2l", [128, 2 * 1024])
                w3b2h = csbb("w3b2h", [128, 2 * 1024])
                w3b2l = csbb("w3b2l", [128, 2 * 1024])
                w2b2h = csbb("w2b2h", [128, 2 * 2048])
                w2b2l = csbb("w2b2l", [128, 2 * 2048])
                wob2h = csbb("wob2h", [128, 2 * 1024])
                wob2l = csbb("wob2l", [128, 2 * 1024])
                rowC = csb("rowC", [1, 512])
                sqsc = csb("sqsc", [128, TC])
                ptmp = csb("ptmp", [128, TC])
                r2rows = csb("r2rows", [1, 4 * TC])
                gs_ = csb("gs_", [128, TC])
                us2 = csb("us2", [128, TC])
                asmt = csb("asmt", [64, 2 * TC])

                CUS = p.ch("CUS")
                LD2 = p.ch("LD2"); LW13 = p.ch("LW13"); LW2 = p.ch("LW2")
                LWO = p.ch("LWO"); LXM = p.ch("LXM")
                CSQ = p.ch("CSQ"); CR2M = p.ch("CR2M"); CR2V = p.ch("CR2V")
                CR2S = p.ch("CR2S"); CBC = p.ch("CBC"); CFIN = p.ch("CFIN")
                CU = p.ch("CU"); CSIL = p.ch("CSIL"); CV = p.ch("CV"); CP = p.ch("CP")
                CW2G = p.ch("CW2G"); CSEV = p.ch("CSEV"); COS = p.ch("COS")
                COSE = p.ch("COSE"); CWOT = p.ch("CWOT"); CGE = p.ch("CGE")
                CSG = p.ch("CSG"); CTRF = p.ch("CTRF"); CFV = p.ch("CFV")
                COUT = p.ch("COUT"); LAS = p.ch("LAS")


                for tt in range(4):
                    if tt == 0:
                        p.wait("sync", SPV, SPV.n)
                    p.wait("sync", CSQ, 8 * tt)
                    p.emit("sync", lambda e, tt=tt: e.dma_start(
                        out=x2t[:].rearrange("p (g t) -> p g t", g=8),
                        in_=x2s_d.ap()[tt].rearrange("g p t -> p g t")), inc=LD2, k=16)
                    p.wait("scalar", LD2, 16 * (tt + 1))
                    for g in range(8):
                        if g > 0:
                            p.wait("scalar", CR2M, 8 * tt + g)
                        p.emit("scalar", lambda e, g=g: e.activation(
                            sqsc[:], x2t[:, g * TC:(g + 1) * TC], AF.Square), inc=CSQ)
                    for g in range(8):
                        p.wait("tensor", CSQ, 8 * tt + g + 1)
                        p.emit("tensor", _mm(pb[0][0:1, 0:512], onesc[:], sqsc[:],
                                             g == 0, g == 7), inc=CR2M)
                    p.wait("vector", CR2M, 8 * (tt + 1))
                    p.emit("vector", lambda e, tt=tt: e.tensor_scalar(
                        r2rows[0:1, tt * TC:(tt + 1) * TC], pb[0][0:1, 0:512],
                        INV_D, 1e-6, OP.mult, OP.add), inc=CR2V)
                p.wait("scalar", CR2V, 4)
                p.emit("scalar", lambda e: e.activation(r2rows[:], r2rows[:], AF.Ln))
                p.emit("scalar", lambda e: e.activation(r2rows[:], r2rows[:], AF.Exp, scale=-0.5),
                       inc=CR2S)
                act_fence(AF.Sigmoid)

                for tt in range(4):
                    if tt == 0:
                        p.wait("sync", CSQ, 32)
                    p.wait("sync", CSEV, 16 * tt)
                    p.wait("sync", CFIN, tt)
                    p.emit("sync", lambda e, tt=tt: e.dma_start(
                        out=x2t[:].rearrange("p (g t) -> p g t", g=8),
                        in_=x2s_d.ap()[tt].rearrange("g p t -> p g t")), inc=LD2, k=16)
                    p.wait("sync", COS, 8 * (tt - 1))
                    p.emit("sync", lambda e, tt=tt: e.dma_start(
                        out=asmt[0:64, (tt % 2) * TC:(tt % 2 + 1) * TC],
                        in_=alpha_d.ap()[tt]), inc=LAS, k=16)
                    p.wait("tensor", CR2S, 1)
                    if tt > 0:
                        p.wait("tensor", CFIN, tt)
                    p.emit("tensor", _mm(pb[1][:, 0:512], onesr[:],
                                         r2rows[0:1, tt * TC:(tt + 1) * TC], True, True),
                           inc=CBC)
                    p.wait("vector", CBC, tt + 1)
                    p.wait("vector", LD2, 16 * (4 + tt + 1))
                    if tt == 0:
                        p.wait("vector", SPW, SPW.n)
                    for g in range(8):
                        p.emit("vector", lambda e, g=g: e.scalar_tensor_tensor(
                            ptmp[:], x2t[:, g * TC:(g + 1) * TC],
                            n2c[:, g:g + 1], pb[1][:, 0:512], OP.mult, OP.mult))
                        p.emit("vector", lambda e, g=g: e.tensor_copy(
                            finth[:, g * TC:(g + 1) * TC], ptmp[:]))
                        p.emit("vector", lambda e, g=g: e.tensor_tensor(
                            fintl[:, g * TC:(g + 1) * TC], ptmp[:],
                            finth[:, g * TC:(g + 1) * TC], OP.subtract),
                            inc=(CFIN if g == 7 else None))
                    if dbg_fin0 is not None and tt == 0:
                        p.wait("gpsimd", CFIN, 1)
                        p.emit("gpsimd", lambda e: e.dma_start(out=dbg_fin0.ap(), in_=finth[:]),
                               inc=DBG, k=16)

                    cu_base = CU.n
                    sev_base = CSEV.n
                    w2g_base = CW2G.n
                    for h in range(2):
                        for i in range(16):
                            idx = h * 16 + i
                            par_ = idx % 2
                            wb1h = w1b2h[:, par_ * 1024:(par_ + 1) * 1024]
                            wb1l = w1b2l[:, par_ * 1024:(par_ + 1) * 1024]
                            wb3h = w3b2h[:, par_ * 1024:(par_ + 1) * 1024]
                            wb3l = w3b2l[:, par_ * 1024:(par_ + 1) * 1024]
                            dff = h * 16 + i
                            if tt == 0:
                                p.wait("sync", SPW, 128 * (dff + 1))
                            if idx >= 2:
                                p.wait("sync", CU, cu_base + idx - 1)
                            p.emit("sync", lambda e, o=wb1h, d=dff: e.dma_start(
                                out=o, in_=w1h_d.ap()[d]), inc=LW13, k=16)
                            p.emit("sync", lambda e, o=wb1l, d=dff: e.dma_start(
                                out=o, in_=w1l_d.ap()[d]), inc=LW13, k=16)
                            if idx >= 2:
                                p.wait("sync", CV, cu_base + idx - 1)
                            p.emit("sync", lambda e, o=wb3h, d=dff: e.dma_start(
                                out=o, in_=w3h_d.ap()[d]), inc=LW13, k=16)
                            p.emit("sync", lambda e, o=wb3l, d=dff: e.dma_start(
                                out=o, in_=w3l_d.ap()[d]), inc=LW13, k=16)
                            p.wait("tensor", LW13, LW13.n - 32)
                            p.wait("tensor", CUS, cu_base + idx)
                            if tt > 0 and idx == 0:
                                p.wait("tensor", CFV, 4 * tt)
                            if idx == 0:
                                p.wait("tensor", CFIN, tt + 1)
                            for kc in range(8):
                                p.emit("tensor", _mm(pb[2][:, 0:512],
                                                     wb1h[:, kc * 128:(kc + 1) * 128],
                                                     finth[:, kc * TC:(kc + 1) * TC],
                                                     kc == 0, False))
                                p.emit("tensor", _mm(pb[2][:, 0:512],
                                                     wb1h[:, kc * 128:(kc + 1) * 128],
                                                     fintl[:, kc * TC:(kc + 1) * TC],
                                                     False, False))
                            for kc in range(8):
                                p.emit("tensor", _mm(pb[2][:, 0:512],
                                                     wb1l[:, kc * 128:(kc + 1) * 128],
                                                     finth[:, kc * TC:(kc + 1) * TC],
                                                     False, kc == 7),
                                       inc=(CU if kc == 7 else None))
                            p.wait("scalar", CU, cu_base + idx + 1)
                            p.wait("scalar", CUS, cu_base + idx)
                            p.emit("scalar", lambda e: e.activation(us2[:], pb[2][:, 0:512],
                                                                    AF.Sigmoid), inc=CSIL)
                            p.wait("tensor", LW13, LW13.n)
                            p.wait("tensor", CP, cu_base + idx)
                            for kc in range(8):
                                p.emit("tensor", _mm(pb[3][:, 0:512],
                                                     wb3h[:, kc * 128:(kc + 1) * 128],
                                                     finth[:, kc * TC:(kc + 1) * TC],
                                                     kc == 0, False))
                                p.emit("tensor", _mm(pb[3][:, 0:512],
                                                     wb3h[:, kc * 128:(kc + 1) * 128],
                                                     fintl[:, kc * TC:(kc + 1) * TC],
                                                     False, False))
                            for kc in range(8):
                                p.emit("tensor", _mm(pb[3][:, 0:512],
                                                     wb3l[:, kc * 128:(kc + 1) * 128],
                                                     finth[:, kc * TC:(kc + 1) * TC],
                                                     False, kc == 7),
                                       inc=(CV if kc == 7 else None))
                            p.wait("vector", CSIL, cu_base + idx + 1)
                            p.emit("vector", lambda e: e.tensor_tensor(
                                ust[:], us2[:], pb[2][:, 0:512], OP.mult), inc=CUS)
                            p.wait("vector", CV, cu_base + idx + 1)
                            if h == 1 and i == 0:
                                p.wait("vector", CW2G, w2g_base + 8)
                            p.emit("vector", lambda e: e.tensor_tensor(
                                ptmp[:], ust[:], pb[3][:, 0:512], OP.mult))
                            p.emit("vector", lambda e, i=i: e.tensor_copy(
                                Pth[:, i * TC:(i + 1) * TC], ptmp[:]))
                            p.emit("vector", lambda e, i=i: e.tensor_tensor(
                                Ptl[:, i * TC:(i + 1) * TC], ptmp[:],
                                Pth[:, i * TC:(i + 1) * TC], OP.subtract),
                                inc=CP)
                        for g in range(8):
                            gi_ = h * 8 + g
                            wb2h = w2b2h[:, (g % 2) * 2048:(g % 2 + 1) * 2048]
                            wb2l = w2b2l[:, (g % 2) * 2048:(g % 2 + 1) * 2048]
                            if tt == 0 and gi_ < 2:
                                p.wait("sync", SPW, SPW.n)
                            if gi_ >= 2:
                                p.wait("sync", CW2G, w2g_base + gi_ - 1)
                            p.emit("sync", lambda e, o=wb2h, g=g, h=h: e.dma_start(
                                out=o, in_=w2h_d.ap()[g][:, h * 2048:(h + 1) * 2048]),
                                inc=LW2, k=16)
                            p.emit("sync", lambda e, o=wb2l, g=g, h=h: e.dma_start(
                                out=o, in_=w2l_d.ap()[g][:, h * 2048:(h + 1) * 2048]),
                                inc=LW2, k=16)
                            p.wait("tensor", LW2, LW2.n)
                            p.wait("tensor", CP, cu_base + (h + 1) * 16)
                            if gi_ >= 2:
                                p.wait("tensor", CSEV, sev_base + gi_ - 1)
                            bank = pb[4 + g % 2]
                            for di in range(16):
                                p.emit("tensor", _mm(bank[:, 0:512],
                                                     wb2h[:, di * 128:(di + 1) * 128],
                                                     Pth[:, di * TC:(di + 1) * TC],
                                                     di == 0, False))
                                p.emit("tensor", _mm(bank[:, 0:512],
                                                     wb2h[:, di * 128:(di + 1) * 128],
                                                     Ptl[:, di * TC:(di + 1) * TC],
                                                     False, False))
                            for di in range(16):
                                p.emit("tensor", _mm(bank[:, 0:512],
                                                     wb2l[:, di * 128:(di + 1) * 128],
                                                     Pth[:, di * TC:(di + 1) * TC],
                                                     False, di == 15),
                                       inc=(CW2G if di == 15 else None))
                            p.wait("vector", CW2G, w2g_base + gi_ + 1)
                            if h == 0:
                                p.emit("vector", lambda e, g=g, b=bank: e.tensor_tensor(
                                    st_[:, g * TC:(g + 1) * TC], b[:, 0:512],
                                    x2t[:, g * TC:(g + 1) * TC], OP.add), inc=CSEV)
                            else:
                                p.emit("vector", lambda e, g=g, b=bank: e.tensor_tensor(
                                    st_[:, g * TC:(g + 1) * TC], b[:, 0:512],
                                    st_[:, g * TC:(g + 1) * TC], OP.add), inc=CSEV)
                    if dbg_s0 is not None and tt == 0:
                        p.wait("gpsimd", CSEV, 16)
                        p.emit("gpsimd", lambda e: e.dma_start(out=dbg_s0.ap(), in_=st_[:]),
                               inc=DBG, k=16)

                    for g in range(8):
                        if g == 0:
                            p.wait("tensor", CGE, 8 * tt)
                            p.wait("tensor", LAS, 16 * (tt + 1))
                        p.wait("tensor", COSE, 8 * tt + g - 1)
                        p.emit("tensor", _mm(pb[6 + g % 2][:, 0:512],
                                             svb[0:64, g * 128:(g + 1) * 128],
                                             asmt[0:64, (tt % 2) * TC:(tt % 2 + 1) * TC],
                                             True, True), inc=COS)
                        p.wait("vector", COS, 8 * tt + g + 1)
                        p.wait("vector", CWOT, 8 * tt)
                        p.emit("vector", lambda e, g=g: e.tensor_copy(ptmp[:],
                            pb[6 + g % 2][:, 0:512]))
                        p.emit("vector", lambda e, g=g: e.tensor_copy(
                            osth[:, g * TC:(g + 1) * TC], ptmp[:]))
                        p.emit("vector", lambda e, g=g: e.tensor_tensor(
                            ostl[:, g * TC:(g + 1) * TC], ptmp[:],
                            osth[:, g * TC:(g + 1) * TC], OP.subtract), inc=COSE)
                    for ec in range(8):
                        wobh = wob2h[:, (ec % 2) * 1024:(ec % 2 + 1) * 1024]
                        wobl = wob2l[:, (ec % 2) * 1024:(ec % 2 + 1) * 1024]
                        if tt == 0 and ec < 2:
                            p.wait("sync", SPW, SPW.n)
                        if 8 * tt + ec >= 2:
                            p.wait("sync", CWOT, 8 * tt + ec - 1)
                        p.emit("sync", lambda e, o=wobh, ec=ec: e.dma_start(
                            out=o, in_=woh_d.ap()[ec]), inc=LWO, k=16)
                        p.emit("sync", lambda e, o=wobl, ec=ec: e.dma_start(
                            out=o, in_=wol_d.ap()[ec]), inc=LWO, k=16)
                        p.wait("tensor", LWO, LWO.n)
                        p.wait("tensor", COSE, 8 * (tt + 1))
                        if ec >= 2:
                            p.wait("tensor", CGE, 8 * tt + ec - 1)
                        for kc in range(8):
                            p.emit("tensor", _mm(pb[6 + ec % 2][:, 0:512],
                                                 wobh[:, kc * 128:(kc + 1) * 128],
                                                 osth[:, kc * TC:(kc + 1) * TC],
                                                 kc == 0, False))
                            p.emit("tensor", _mm(pb[6 + ec % 2][:, 0:512],
                                                 wobh[:, kc * 128:(kc + 1) * 128],
                                                 ostl[:, kc * TC:(kc + 1) * TC],
                                                 False, False))
                        for kc in range(8):
                            p.emit("tensor", _mm(pb[6 + ec % 2][:, 0:512],
                                                 wobl[:, kc * 128:(kc + 1) * 128],
                                                 osth[:, kc * TC:(kc + 1) * TC],
                                                 False, kc == 7),
                                   inc=(CWOT if kc == 7 else None))
                        p.wait("scalar", CWOT, 8 * tt + ec + 1)
                        p.wait("scalar", CSG, 8 * tt + ec)
                        p.emit("scalar", lambda e, ec=ec: e.activation(
                            gt[:], pb[6 + ec % 2][:, 0:512], AF.Identity,
                            bias=boc[:, ec:ec + 1]))
                        p.emit("scalar", lambda e: e.activation(gs_[:], gt[:], AF.Sigmoid),
                               inc=CGE)
                        p.wait("vector", CGE, 8 * tt + ec + 1)
                        p.wait("vector", CSEV, sev_base + 16)
                        if tt > 0 and ec == 0:
                            p.wait("vector", CTRF, 4 * tt)
                        p.emit("vector", lambda e: e.tensor_tensor(gt[:], gt[:], gs_[:], OP.mult))
                        p.emit("vector", lambda e, ec=ec: e.tensor_tensor(
                            sgt[:, ec * TC:(ec + 1) * TC], gt[:], st_[:, ec * TC:(ec + 1) * TC],
                            OP.mult), inc=CSG)
                    if dbg_g0 is not None and tt == 0:
                        p.wait("gpsimd", CSG, 8)
                        p.emit("gpsimd", lambda e: e.dma_start(out=dbg_g0.ap(), in_=sgt[:]),
                               inc=DBG, k=16)
                    for j in range(4):
                        jj = 4 * tt + j
                        xm = xmt[:, (j % 2) * D:(j % 2 + 1) * D]
                        if jj >= 2:
                            p.wait("sync", CFV, jj - 1)
                        p.emit("sync", lambda e, o=xm, tt=tt, j=j: e.dma_start(
                            out=o, in_=xin.ap()[2048 + tt * TC + j * 128:
                                                2048 + tt * TC + (j + 1) * 128, :]),
                            inc=LXM, k=16)
                        p.wait("tensor", CSG, 8 * (tt + 1))
                        p.wait("tensor", CFV, jj)
                        for g in range(8):
                            bank = pb[2 + g // 4]
                            p.emit("tensor", _mm(bank[:, (g % 4) * 128:(g % 4 + 1) * 128],
                                                 sgt[:, g * TC + j * 128:g * TC + (j + 1) * 128],
                                                 eye[:], g % 4 == 0, g % 4 == 3, is_t=True))
                        p.emit("tensor", tclose(CTRF), inc=CTRF)
                        p.wait("vector", CTRF, jj + 1)
                        p.wait("vector", LXM, 16 * (jj + 1))
                        if jj >= 2:
                            p.wait("vector", COUT, 16 * (jj - 1))
                        fb = finb[:, (j % 2) * D:(j % 2 + 1) * D]
                        p.emit("vector", lambda e, fb=fb, xm=xm: e.tensor_tensor(
                            fb[:, 0:512], pb[2][:, 0:512], xm[:, 0:512], OP.add))
                        p.emit("vector", lambda e, fb=fb, xm=xm: e.tensor_tensor(
                            fb[:, 512:1024], pb[3][:, 0:512], xm[:, 512:1024], OP.add), inc=CFV)
                        p.wait("gpsimd", CFV, jj + 1)
                        p.emit("gpsimd", lambda e, tt=tt, j=j, fb=fb: e.dma_start(
                            out=out_d.ap()[tt * TC + j * 128:tt * TC + (j + 1) * 128, :],
                            in_=fb), inc=COUT, k=16)

                p.wait("gpsimd", COUT, 16 * 16)
                if dbg:
                    p.wait("gpsimd", DBG, DBG.n)
                p.flush(block)

    return nc


def _stage(inputs, core):
    b, h = core // 2, core % 2
    C = np.ascontiguousarray
    x = inputs["L0_attn_x"][b]
    if h == 0:
        xr = np.concatenate([x[2048:], x[:2048]], axis=0)
    else:
        xr = x
    blk = lambda w, K, E: C(w.reshape(K // 128, 128, E // 128, 128).transpose(1, 0, 2, 3))
    m = {
        "xin": C(xr),
        "wg": blk(inputs["L0_attn_Wg"], 1024, 1024),
        "wc": blk(inputs["L0_attn_Wc"], 1024, 1024),
        "w1a": blk(inputs["L0_ffn_w1"][:1024], 1024, 512),
        "w1b": C(inputs["L0_ffn_w1"][1024:].reshape(8, 128, 512)),
        "w2r": C(inputs["L0_ffn_w2"].reshape(4, 128, 64)),
        "b1row": C(inputs["L0_ffn_b1"].reshape(1, 512)),
        "b2col": C(inputs["L0_ffn_b2"].reshape(64, 1)),
        "tau": C(inputs["L0_ffn_tau"].reshape(1, 1)),
        "n1col": C(inputs["L0_attn_n1_w"].reshape(8, 128).T),
        "n2col": C(inputs["L0_attn_n2_w"].reshape(8, 128).T),
        "bgcol": C(inputs["L0_attn_bg"].reshape(8, 128).T),
        "bccol": C(inputs["L0_attn_bc"].reshape(8, 128).T),
        "bocol": C(inputs["L0_ffn_bo"].reshape(8, 128).T),
        "sinitT": C(inputs["L0_ffn_slot_init"].T.reshape(8, 128, 64).transpose(1, 0, 2)),
        "sinit": C(inputs["L0_ffn_slot_init"]),
        "wihT": C(inputs["L0_ffn_Wih"].T.reshape(8, 128, 3072)),
        "whhT": C(inputs["L0_ffn_Whh"].T.reshape(8, 128, 3072)),
        "bihrow": C(inputs["L0_ffn_bih"].reshape(1, 3072)),
        "bhhrow": C(inputs["L0_ffn_bhh"].reshape(1, 3072)),
        "wq": C(inputs["L0_ffn_wq"].reshape(8, 128, 256).transpose(1, 0, 2)),
        "wk": C(inputs["L0_ffn_wk"].reshape(8, 128, 256).transpose(1, 0, 2)),
        "wv": C(inputs["L0_ffn_wv"].reshape(8, 128, 1024).transpose(1, 0, 2)),
        "wvp": C(inputs["L0_ffn_wvp"].reshape(8, 128, 1024).transpose(1, 0, 2)),
        "bqrow": C(inputs["L0_ffn_bq"].reshape(1, 256)),
        "bkrow": C(inputs["L0_ffn_bk"].reshape(1, 256)),
        "bvrow": C(inputs["L0_ffn_bv"].reshape(1, 1024)),
        "bvprow": C(inputs["L0_ffn_bvp"].reshape(1, 1024)),
        "lngrow": C(inputs["L0_ffn_ln_g"].reshape(1, 1024)),
        "lnbrow": C(inputs["L0_ffn_ln_b"].reshape(1, 1024)),
        "wo": C(inputs["L0_ffn_wo"].reshape(8, 128, 8, 128).transpose(2, 1, 0, 3)),
        "w1f": C(inputs["L0_attn_W1"].reshape(8, 128, 32, 128).transpose(2, 1, 0, 3)),
        "w3f": C(inputs["L0_attn_W3"].reshape(8, 128, 32, 128).transpose(2, 1, 0, 3)),
        "w2f": C(inputs["L0_attn_W2"].reshape(32, 128, 8, 128).transpose(2, 1, 0, 3)),
        "eye": np.eye(128, dtype=np.float32),
        "onesc": np.ones((128, 1), np.float32),
        "onesr": np.ones((1, 128), np.float32),
        "flag": np.full((128, 1), float(h), np.float32),
    }
    return {k: np.asarray(v, dtype=np.float32) for k, v in m.items()}


def kernel(**inputs):
    nc = build_program()
    in_maps = [_stage(inputs, core) for core in range(8)]
    res = run_bass_kernel_spmd(nc, in_maps, core_ids=list(range(8)))
    out = np.empty((4, 4096, 1024), dtype=np.float32)
    for core in range(8):
        b, h = core // 2, core % 2
        out[b, h * 2048:(h + 1) * 2048] = res.results[core]["out"]
    return out


# version 08
# speedup vs baseline: 1.1038x; 1.0174x over previous
import numpy as np
from contextlib import ExitStack

import concourse.bass as bass
import concourse.mybir as mybir
from concourse.bass_utils import run_bass_kernel_spmd

F32 = mybir.dt.float32
BF16 = mybir.dt.bfloat16
AF = mybir.ActivationFunctionType
OP = mybir.AluOpType
AX = mybir.AxisListType

D = 1024
T = 4096
TC = 512
NCH = 8
MY0 = 4
NDFF = 32
NS = 64
RS2 = 0.7071067811865476
INV_D = 1.0 / D
ENGS = ("tensor", "vector", "scalar", "sync", "gpsimd")


class Ch:
    __slots__ = ("sem", "n")

    def __init__(self, sem):
        self.sem = sem
        self.n = 0


class Prog:
    def __init__(self, nc, stack):
        self.nc = nc
        self.stack = stack
        self.q = {e: [] for e in ENGS}
        self._ch = {}

    def ch(self, name):
        c = self._ch.get(name)
        if c is None:
            c = Ch(self.stack.enter_context(self.nc.semaphore(name)))
            self._ch[name] = c
        return c

    def wait(self, eng, ch, n=None):
        v = ch.n if n is None else n
        if v <= 0:
            return
        self.q[eng].append(lambda e, s=ch.sem, vv=v: e.wait_ge(s, vv))

    def emit(self, eng, fn, inc=None, k=1):
        if eng == "vector":
            self.q[eng].append(lambda e: e.engine_nop())
            self.q[eng].append(lambda e: e.engine_nop())
            self.q[eng].append(lambda e: e.engine_nop())
        if inc is None:
            self.q[eng].append(fn)
        elif k == 1:
            inc.n += k
            self.q[eng].append(fn)
            self.q[eng].append(lambda e, s=inc.sem, kk=k: e.drain().then_inc(s, kk))
        else:
            inc.n += k
            self.q[eng].append(lambda e, f=fn, s=inc.sem, kk=k: f(e).then_inc(s, kk))

    act_trailer = None

    def barrier(self):
        ch = self.ch("bar")
        for eng in ENGS:
            self.q[eng].append(lambda e, s=ch.sem: e.drain().then_inc(s, 1))
        ch.n += len(ENGS)
        for eng in ENGS:
            self.wait(eng, ch)

    def flush(self, block):
        for eng in ENGS:
            lst = self.q[eng]
            if not lst:
                continue
            self.q[eng] = []
            getattr(block, eng)(lambda e, L=lst: [f(e) for f in L])


def _mm(out_ap, lhsT_ap, rhs_ap, start, stop, is_t=False):
    return lambda e: e.matmul(out_ap, lhsT_ap, rhs_ap, start=start, stop=stop,
                              is_transpose=is_t)


def build_program(debug_outputs=()):
    nc = bass.Bass("TRN2", target_bir_lowering=False)
    dt = lambda name, shape, kind="ExternalInput": nc.dram_tensor(name, shape, F32, kind=kind)

    xin = dt("xin", [T, D])
    wg_d = dt("wg", [128, 8, 8, 128])
    wc_d = dt("wc", [128, 8, 8, 128])
    w1a_d = dt("w1a", [128, 8, 4, 128])
    w1b_d = dt("w1b", [8, 128, 512])
    w2r_d = dt("w2r", [4, 128, 64])
    b1row_d = dt("b1row", [1, 512])
    b2col_d = dt("b2col", [64, 1])
    tau_d = dt("tau", [1, 1])
    n1col_d = dt("n1col", [128, 8])
    n2col_d = dt("n2col", [128, 8])
    bgcol_d = dt("bgcol", [128, 8])
    bccol_d = dt("bccol", [128, 8])
    bocol_d = dt("bocol", [128, 8])
    sinitT_d = dt("sinitT", [128, 8, 64])
    sinit_d = dt("sinit", [64, D])
    wihT_d = dt("wihT", [8, 128, 3072])
    whhT_d = dt("whhT", [8, 128, 3072])
    bihrow_d = dt("bihrow", [1, 3072])
    bhhrow_d = dt("bhhrow", [1, 3072])
    wq_d = dt("wq", [128, 8, 256])
    wk_d = dt("wk", [128, 8, 256])
    wv_d = dt("wv", [128, 8, D])
    wvp_d = dt("wvp", [128, 8, D])
    bqrow_d = dt("bqrow", [1, 256])
    bkrow_d = dt("bkrow", [1, 256])
    bvrow_d = dt("bvrow", [1, D])
    bvprow_d = dt("bvprow", [1, D])
    lngrow_d = dt("lngrow", [1, D])
    lnbrow_d = dt("lnbrow", [1, D])
    wo_d = dt("wo", [8, 128, 8, 128])
    w1f_d = dt("w1f", [32, 128, 8, 128])
    w3f_d = dt("w3f", [32, 128, 8, 128])
    w2f_d = dt("w2f", [8, 128, 32, 128])
    eye_d = dt("eye", [128, 128])
    onesc_d = dt("onesc", [128, 1])
    onesr_d = dt("onesr", [1, 128])
    flag_d = dt("flag", [128, 1])
    out_d = dt("out", [2048, D], kind="ExternalOutput")

    w1h_d = nc.dram_tensor("w1h", [32, 128, 1024], BF16, kind="Internal")
    w1l_d = nc.dram_tensor("w1l", [32, 128, 1024], BF16, kind="Internal")
    w3h_d = nc.dram_tensor("w3h", [32, 128, 1024], BF16, kind="Internal")
    w3l_d = nc.dram_tensor("w3l", [32, 128, 1024], BF16, kind="Internal")
    w2h_d = nc.dram_tensor("w2h", [8, 128, 4096], BF16, kind="Internal")
    w2l_d = nc.dram_tensor("w2l", [8, 128, 4096], BF16, kind="Internal")
    woh_d = nc.dram_tensor("woh", [8, 128, 1024], BF16, kind="Internal")
    wol_d = nc.dram_tensor("wol", [8, 128, 1024], BF16, kind="Internal")

    alpha_d = nc.dram_tensor("alphasm", [4, 64, TC], F32, kind="Internal")
    x2kind = "ExternalOutput" if "x2" in debug_outputs else "Internal"
    x2s_d = nc.dram_tensor("dbg_x2" if x2kind == "ExternalOutput" else "x2scr",
                           [4, 8, 128, TC], F32, kind=x2kind)

    dbg = {}
    def dbg_out(name, shape):
        if name in debug_outputs:
            h = nc.dram_tensor("dbg_" + name, shape, F32, kind="ExternalOutput")
            dbg[name] = h
            return h
        return None

    dbg_xn = dbg_out("xn", [NCH, 128, 4 * D])
    dbg_st = dbg_out("st", [NCH, 128, 40])
    dbg_lg = dbg_out("lg", [NCH, 128, 256])
    dbg_h1 = dbg_out("h1", [NCH, 128, 2048])
    dbg_tk = dbg_out("tk", [NCH, 128, 16])
    dbg_bp = dbg_out("bp", [14, 64, 3072])
    dbg_hin = dbg_out("hin", [NCH, 128, 8 * TC])
    dbg_alpha = dbg_out("alpha", [NCH, 128, 4 * 64])
    dbg_si = dbg_out("si", [128, 512])
    dbg_den = dbg_out("den", [1, 64])
    dbg_snew = dbg_out("snew", [64, D])
    dbg_sv = dbg_out("sv", [64, D])
    dbg_fin0 = dbg_out("fin0", [128, 8 * TC])
    dbg_s0 = dbg_out("s0", [128, 8 * TC])
    dbg_g0 = dbg_out("g0", [128, 8 * TC])

    stack = ExitStack()
    with stack:
        p = Prog(nc, stack)
        sb = lambda name, shape: stack.enter_context(nc.sbuf_tensor(name, shape, F32))

        eye = sb("eye_t", [128, 128])
        onesc = sb("onesc_t", [128, 1])
        onesr = sb("onesr_t", [1, 128])
        flagc = sb("flagc_t", [128, 1])
        n1c = sb("n1c_t", [128, 8])
        n2c = sb("n2c_t", [128, 8])
        bgc = sb("bgc_t", [128, 8])
        bcc = sb("bcc_t", [128, 8])
        boc = sb("boc_t", [128, 8])
        b2c = sb("b2c_t", [64, 1])
        taub = sb("taub_t", [1, 1])
        itau1 = sb("itau1_t", [1, 1])
        itau64 = sb("itau64_t", [64, 1])
        b2s = sb("b2s_t", [64, 1])
        sinitT = sb("sinitT_t", [128, 8 * 64])
        smdm = sb("smdm_t", [128, 8])
        sm1c = sb("sm1c_t", [128, 4])
        sm1sc = sb("sm1sc_t", [128, 4])
        state = sb("state_t", [128, 8])
        actw = sb("actw_t", [128, 1])
        gwarm = sb("gwarm_t", [128, 128])
        p.act_trailer = lambda e: e.activation(actw[:], onesc[:], AF.Square)
        svb = sb("svb_t", [64, D])
        pb = [stack.enter_context(nc.psum_tensor("pb%d" % i, [128, 512], F32))
              for i in range(8)]

        def tclose(ch):
            return lambda e: e.matmul(pb[7][0:1, 0:1], onesr[0:1, 0:1],
                                      onesr[0:1, 0:1], start=True, stop=True)

        DBG = p.ch("DBG")
        RQF = p.ch("RQF")
        AKF = p.ch("AKF")

        def act_fence(func):
            p.emit("scalar", lambda e, f=func: e.activation(actw[:], onesc[:], f))
            p.emit("scalar", lambda e, s_=RQF.sem: e.sem_inc(s_, 1))
            RQF.n += 1
            p.wait("gpsimd", RQF, RQF.n)
            for _ in range(6):
                p.emit("gpsimd", lambda e: e.tensor_copy(gwarm[:], gwarm[:]))
            p.emit("gpsimd", lambda e, s_=AKF.sem: e.drain().then_inc(s_, 1))
            AKF.n += 1
            p.wait("scalar", AKF, AKF.n)

        with nc.Block() as block:
            with ExitStack() as ascope:
                asb = lambda name, shape, dt_=F32: ascope.enter_context(nc.sbuf_tensor(name, shape, dt_))
                wgh = asb("wgh", [128, 8192], BF16)
                wgl = asb("wgl", [128, 8192], BF16)
                wch = asb("wch", [128, 8192], BF16)
                wcl = asb("wcl", [128, 8192], BF16)
                w1a = asb("w1at", [128, 4096])
                w2r = asb("w2rt", [128, 256])
                xld = asb("xld", [128, 2 * 4 * D])
                w1bt = xld
                xn_tm = asb("xn_tm", [128, 4 * D])
                xn_fm = asb("xn_fm", [128, 8 * TC])
                hinh = asb("hinh", [128, 8 * TC], BF16)
                hinl = asb("hinl", [128, 8 * TC], BF16)
                h1b = asb("h1b", [128, 4 * TC])
                scr = asb("scr", [128, D])
                b1r = scr[0:1, 0:512]
                sm1r = scr[0:1, 512:1024]
                ab = asb("ab_t", [128, TC])
                cb = asb("cb_t", [128, TC])
                bbb = asb("bb_t", [128, TC])
                hbb = asb("hb_t", [128, TC])
                lgf = asb("lgf", [64, TC])
                lgtm = asb("lgtm", [128, 256])
                cur = asb("cur", [128, 256])
                eb = asb("eb", [128, 256])
                aet = asb("aet", [128, 256])
                alf = asb("alf", [128, 256])
                r1r4 = asb("r1r4", [4, 128])
                rowr1 = asb("rowr1", [1, 512])
                mall = asb("mall", [128, 32])
                rhoall = asb("rhoall", [128, 32])
                r1all = asb("r1all", [128, 32])
                mkc = asb("mkc", [128, 4])
                nmc = asb("nmc", [128, 4])
                thrc = asb("thrc", [128, 4])
                zc = asb("zc", [128, 4])
                rzc = asb("rzc", [128, 4])
                spst = asb("spst", [128, 2 * 512])
                sphi = asb("sphi", [128, 2 * 512], BF16)
                splo = asb("splo", [128, 2 * 512], BF16)

                LDW = p.ch("LDW"); LDX = p.ch("LDX")
                IT = p.ch("IT"); IV = p.ch("IV"); IS = p.ch("IS")
                RST0 = p.ch("RST0")
                WLD = p.ch("WLD"); WSP = p.ch("WSP")

                def ld(dst_ap, src_ap):
                    p.emit("sync", lambda e, o=dst_ap, i=src_ap: e.dma_start(out=o, in_=i),
                           inc=LDW, k=16)

                ld(eye[:], eye_d.ap())
                ld(onesc[:], onesc_d.ap())
                ld(onesr[:], onesr_d.ap())
                ld(flagc[:], flag_d.ap())
                ld(n1c[:], n1col_d.ap())
                ld(n2c[:], n2col_d.ap())
                ld(bgc[:], bgcol_d.ap())
                ld(bcc[:], bccol_d.ap())
                ld(boc[:], bocol_d.ap())
                ld(b2c[:], b2col_d.ap())
                ld(taub[:], tau_d.ap())
                ld(sinitT[:].rearrange("p (k s) -> p k s", k=8), sinitT_d.ap())
                ld(w1a[:].rearrange("p (k e c) -> p k e c", k=8, e=4), w1a_d.ap())
                ld(w2r[:].rearrange("p (d s) -> p d s", d=4), w2r_d.ap().rearrange("d p s -> p d s"))
                N_LDW = LDW.n

                for (wd_, hi_, lo_) in ((wg_d, wgh, wgl), (wc_d, wch, wcl)):
                    p.wait("sync", WSP, WSP.n)
                    p.emit("sync", lambda e, i=wd_: e.dma_start(
                        out=xld[:].rearrange("p (k e c) -> p k e c", k=8, e=8),
                        in_=i.ap()), inc=WLD, k=16)
                    p.wait("vector", LDW, N_LDW)
                    p.wait("vector", WLD, WLD.n)
                    for kc in range(8):
                        p.emit("vector", lambda e, k=kc: e.tensor_scalar(
                            xld[:, k * 1024:(k + 1) * 1024], xld[:, k * 1024:(k + 1) * 1024],
                            n1c[:, k:k + 1], None, OP.mult))
                    p.emit("vector", lambda e, h_=hi_: e.tensor_copy(h_[:], xld[:]))
                    p.emit("vector", lambda e, h_=hi_, l_=lo_: e.tensor_tensor(
                        l_[:], xld[:], h_[:], OP.subtract), inc=WSP)
                p.wait("sync", WSP, 2)
                ld(xld[:, 0:4096].rearrange("p (k t) -> p k t", k=8),
                   w1b_d.ap().rearrange("k p t -> p k t"))
                ld(b1r, b1row_d.ap())
                N_LDW2 = LDW.n
                p.wait("vector", LDW, N_LDW)
                p.emit("vector", lambda e: e.memzero(state[:]))
                p.emit("vector", lambda e: e.tensor_reduce(
                    smdm[:].rearrange("p (k o) -> p k o", k=8, o=1),
                    sinitT[:].rearrange("p (k s) -> p k s", k=8), AX.X, OP.add))
                p.emit("vector", lambda e: e.tensor_scalar(smdm[:], smdm[:], 1.0 / 64, None, OP.mult),
                       inc=IV)
                p.emit("gpsimd", lambda e: e.memset(gwarm[:], 1.0))
                p.wait("scalar", LDW, N_LDW)
                act_fence(AF.Ln)
                p.emit("scalar", lambda e: e.activation(itau1[:], taub[:], AF.Abs), inc=IS)
                p.wait("vector", IS)
                p.emit("vector", lambda e: e.tensor_scalar(itau1[:], itau1[:], 0.1, None, OP.add))
                p.emit("vector", lambda e: e.reciprocal(itau1[:], itau1[:]), inc=IV)
                p.wait("tensor", LDW, N_LDW2)
                p.wait("tensor", IV)
                p.emit("tensor", _mm(pb[7][0:64, 0:1], onesr[:, 0:64], itau1[:], True, True), inc=IT)
                for kc in range(8):
                    p.emit("tensor", _mm(pb[6][0:1, 0:512], smdm[:, kc:kc + 1],
                                         w1bt[:, kc * 512:(kc + 1) * 512], kc == 0, kc == 7),
                           inc=(IT if kc == 7 else None))
                p.wait("vector", IT)
                p.emit("vector", lambda e: e.tensor_copy(itau64[:], pb[7][0:64, 0:1]))
                p.emit("vector", lambda e: e.tensor_tensor(b2s[:], b2c[:], itau64[:], OP.mult))
                p.emit("vector", lambda e: e.tensor_tensor(sm1r, pb[6][0:1, 0:512], b1r, OP.add),
                       inc=IV)
                p.wait("tensor", IV)
                for j in range(4):
                    p.emit("tensor", _mm(pb[5][0:128, j:j + 1], scr[0:1, 512 + j * 128:512 + (j + 1) * 128],
                                         onesc[0:1, 0:1], j == 0, j == 3),
                           inc=(IT if j == 3 else None))
                p.wait("vector", IT)
                p.emit("vector", lambda e: e.tensor_copy(sm1c[:], pb[5][0:128, 0:4]))
                p.emit("vector", lambda e: e.tensor_scalar(sm1sc[:], sm1c[:], RS2, None, OP.mult),
                       inc=IV)
                N_IV = IV.n

                PLX = p.ch("PLX"); PSQ = p.ch("PSQ"); PRD = p.ch("PRD")
                s1c = h1b[:, 0:32]
                s2c = h1b[:, 32:64]
                varc = h1b[:, 64:96]
                vpec = h1b[:, 96:128]
                recc = h1b[:, 128:160]
                rpc = h1b[:, 160:192]
                for c in range(NCH):
                    xb0 = xld[:, (c % 2) * 4 * D:(c % 2 + 1) * 4 * D]
                    p.wait("sync", PSQ, 4 * (c - 1))
                    p.wait("sync", PRD, 4 * (c - 1))
                    if c < 2:
                        p.wait("sync", WSP, 2)
                    if c == 0:
                        p.wait("sync", IT, IT.n)
                    p.emit("sync", lambda e, c=c, o=xb0: e.dma_start(
                        out=o.rearrange("p (j d) -> p j d", j=4),
                        in_=xin.ap()[c * TC:(c + 1) * TC, :].rearrange(
                            "(j p) d -> p j d", j=4)), inc=PLX, k=16)
                    p.wait("scalar", PLX, 16 * (c + 1))
                    if c == 0:
                        p.wait("scalar", LDW, N_LDW)
                        act_fence(AF.Ln)
                    for j in range(4):
                        p.emit("scalar", lambda e, x=xb0[:, j * D:(j + 1) * D], c=c, j=j:
                               e.activation(scr[:], x, AF.Square,
                                            accum_out=h1b[:, 32 + c * 4 + j:32 + c * 4 + j + 1]),
                               inc=PSQ)
                    p.wait("vector", PLX, 16 * (c + 1))
                    for j in range(4):
                        p.emit("vector", lambda e, x=xb0[:, j * D:(j + 1) * D], c=c, j=j:
                               e.tensor_reduce(h1b[:, c * 4 + j:c * 4 + j + 1], x,
                                               AX.X, OP.add), inc=PRD)
                p.wait("vector", PSQ, 32)
                p.emit("vector", lambda e: e.tensor_scalar(mall[:], s1c, INV_D, None, OP.mult))
                p.emit("vector", lambda e: e.tensor_scalar(s2c, s2c, INV_D, None, OP.mult))
                p.emit("vector", lambda e: e.tensor_tensor(varc, mall[:], mall[:], OP.mult))
                p.emit("vector", lambda e: e.tensor_tensor(varc, s2c, varc, OP.subtract))
                p.emit("vector", lambda e: e.tensor_scalar(vpec, varc, 1e-5, None, OP.add))
                p.emit("vector", lambda e: e.reciprocal(recc, vpec))
                p.emit("vector", lambda e: e.tensor_tensor(rpc, varc, recc, OP.mult))
                p.emit("vector", lambda e: e.tensor_scalar(rpc, rpc, 1e-6, None, OP.add),
                       inc=PRD)
                p.wait("scalar", PRD, PRD.n)
                p.emit("scalar", lambda e: e.activation(s1c, vpec, AF.Ln))
                p.emit("scalar", lambda e: e.activation(s2c, rpc, AF.Ln))
                p.emit("scalar", lambda e: e.activation(rhoall[:], s1c, AF.Exp, scale=-0.5))
                p.emit("scalar", lambda e: e.activation(r1all[:], s2c, AF.Exp, scale=-0.5),
                       inc=RST0)
                act_fence(AF.Sigmoid)

                SQ = p.ch("SQ"); VST = p.ch("VST"); RST = p.ch("RST"); XNT = p.ch("XNT")
                TG = p.ch("TG"); TGE = p.ch("TGE"); BC = p.ch("BC"); HIN = p.ch("HIN")
                H1T = p.ch("H1T"); H1S = p.ch("H1S"); H1V = p.ch("H1V")
                LGT = p.ch("LGT"); LGE = p.ch("LGE"); TOPK = p.ch("TOPK")
                SEXP = p.ch("SEXP"); ALPHA = p.ch("ALPHA")
                GCT = p.ch("GCT"); GCE = p.ch("GCE"); SCE = p.ch("SCE")
                SLT = p.ch("SLT"); SPL = p.ch("SPL"); RR = p.ch("RR")

                WIN = p.ch("WIN"); SPV = p.ch("SPV"); SPW = p.ch("SPW")
                ASM = p.ch("ASM")
                split_jobs = []
                for i in range(32):
                    for (srcw, hiw, low) in ((w1f_d, w1h_d, w1l_d),
                                             (w3f_d, w3h_d, w3l_d)):
                        for q in range(2):
                            split_jobs.append((
                                srcw.ap()[i][:, q * 4:(q + 1) * 4, :].rearrange(
                                    "p k e -> p (k e)"),
                                hiw.ap()[i][:, q * 512:(q + 1) * 512],
                                low.ap()[i][:, q * 512:(q + 1) * 512]))
                for g in range(8):
                    for q in range(8):
                        split_jobs.append((
                            w2f_d.ap()[g][:, q * 4:(q + 1) * 4, :].rearrange(
                                "p k e -> p (k e)"),
                            w2h_d.ap()[g][:, q * 512:(q + 1) * 512],
                            w2l_d.ap()[g][:, q * 512:(q + 1) * 512]))
                for g in range(8):
                    for q in range(2):
                        split_jobs.append((
                            wo_d.ap()[g][:, q * 4:(q + 1) * 4, :].rearrange(
                                "p k e -> p (k e)"),
                            woh_d.ap()[g][:, q * 512:(q + 1) * 512],
                            wol_d.ap()[g][:, q * 512:(q + 1) * 512]))

                NJOBS = len(split_jobs)

                def _sp_load(i):
                    o = (i % 2) * 512
                    p.emit("gpsimd", lambda e, oo=spst[:, o:o + 512],
                           ii=split_jobs[i][0]: e.dma_start(out=oo, in_=ii),
                           inc=WIN, k=16)

                def pump_split(n):
                    for _ in range(n):
                        i = pump_split.i
                        if i >= NJOBS:
                            return
                        if i == 0:
                            _sp_load(0)
                            if NJOBS > 1:
                                _sp_load(1)
                        _, hi_, lo_ = split_jobs[i]
                        o = (i % 2) * 512
                        p.wait("gpsimd", WIN, 16 * (i + 1))
                        p.wait("gpsimd", SPW, 32 * (i - 1))
                        p.emit("gpsimd", lambda e, o=o: e.tensor_copy(
                            sphi[:, o:o + 512], spst[:, o:o + 512]))
                        for _ in range(3):
                            p.emit("gpsimd", lambda e: e.engine_nop())
                        p.emit("gpsimd", lambda e, o=o: e.tensor_tensor(
                            splo[:, o:o + 512], spst[:, o:o + 512],
                            sphi[:, o:o + 512], OP.subtract), inc=SPV)
                        p.wait("gpsimd", SPV, SPV.n)
                        p.emit("gpsimd", lambda e, oo=hi_, o=o: e.dma_start(
                            out=oo, in_=sphi[:, o:o + 512]), inc=SPW, k=16)
                        p.emit("gpsimd", lambda e, oo=lo_, o=o: e.dma_start(
                            out=oo, in_=splo[:, o:o + 512]), inc=SPW, k=16)
                        if i + 2 < NJOBS:
                            _sp_load(i + 2)
                        pump_split.i += 1
                pump_split.i = 0

                for c in range(NCH):
                    mine = c >= MY0
                    xb = xld[:, (c % 2) * 4 * D:(c % 2 + 1) * 4 * D]
                    tg_base = TG.n

                    if c < 2:
                        p.wait("sync", PSQ, 32)
                        p.wait("sync", PRD, 32)
                    p.wait("sync", XNT, c - 1)
                    p.emit("sync", lambda e, c=c, o=xb: e.dma_start(
                        out=o.rearrange("p (j d) -> p j d", j=4),
                        in_=xin.ap()[c * TC:(c + 1) * TC, :].rearrange("(j p) d -> p j d", j=4)),
                        inc=LDX, k=16)

                    for dh in range(4):
                        p.wait("scalar", H1T, 4 * c + dh + 1)
                        p.wait("scalar", H1V, 4 * c + dh)
                        p.emit("scalar", lambda e, dh=dh: e.activation(
                            scr[:, 0:512], pb[5][:, 0:512], AF.Erf,
                            bias=sm1sc[:, dh:dh + 1], scale=RS2))
                        p.emit("scalar", lambda e, dh=dh: e.activation(
                            scr[:, 512:1024], pb[5][:, 0:512], AF.Identity,
                            bias=sm1c[:, dh:dh + 1]), inc=H1S)
                    p.wait("scalar", LGT, c + 1)
                    if c > MY0:
                        p.wait("scalar", ASM, 16 * (c - MY0))
                    p.emit("scalar", lambda e: e.activation(
                        lgf[:], pb[6][0:64, 0:512], AF.Identity, bias=b2s[:], scale=itau64[:]),
                        inc=LGE)
                    for ec in range(8):
                        gb = 2 if ec % 2 == 0 else 5
                        cbk = 3 if ec % 2 == 0 else 6
                        p.wait("scalar", GCT, 16 * c + 2 * ec + 1)
                        p.wait("scalar", SCE, 8 * c + ec)
                        p.emit("scalar", lambda e, ec=ec, gb=gb: e.activation(
                            ab[:], pb[gb][:, 0:512], AF.Sigmoid, bias=bgc[:, ec:ec + 1]), inc=GCE)
                        p.wait("scalar", GCT, 16 * c + 2 * ec + 2)
                        p.emit("scalar", lambda e, ec=ec, cbk=cbk: e.activation(
                            cb[:], pb[cbk][:, 0:512], AF.Tanh, bias=bcc[:, ec:ec + 1]), inc=GCE)
                    p.wait("scalar", TOPK, c + 1)
                    for j in range(4):
                        p.emit("scalar", lambda e, j=j: e.activation(
                            eb[:, j * 64:(j + 1) * 64], lgtm[:, j * 64:(j + 1) * 64],
                            AF.Tanh, bias=nmc[:, j:j + 1], scale=0.5),
                            inc=(SEXP if j == 3 else None))

                    p.wait("vector", LDX, 16 * (c + 1))
                    if c == 0:
                        p.wait("vector", RST0, 1)
                    p.wait("vector", SLT, c)
                    for j in range(4):
                        p.emit("vector", lambda e, x=xb[:, j * D:(j + 1) * D], c=c, j=j:
                               e.tensor_scalar(
                            xn_tm[:, j * D:(j + 1) * D], x,
                            mall[:, c * 4 + j:c * 4 + j + 1],
                            rhoall[:, c * 4 + j:c * 4 + j + 1],
                            OP.subtract, OP.mult), inc=(XNT if j == 3 else None))
                    if dbg_xn is not None:
                        p.wait("gpsimd", XNT, c + 1)
                        p.emit("gpsimd", lambda e, c=c: e.dma_start(
                            out=dbg_xn.ap()[c], in_=xn_tm[:]), inc=DBG, k=16)
                    if dbg_st is not None:
                        p.emit("gpsimd", lambda e, c=c: e.dma_start(
                            out=dbg_st.ap()[c][:, 0:4], in_=mall[:, c * 4:(c + 1) * 4]),
                            inc=DBG, k=16)
                        p.emit("gpsimd", lambda e, c=c: e.dma_start(
                            out=dbg_st.ap()[c][:, 4:8], in_=rhoall[:, c * 4:(c + 1) * 4]),
                            inc=DBG, k=16)
                    p.wait("vector", H1T, 4 * c)
                    if c > MY0:
                        p.wait("vector", SPL, 16 * (c - MY0))
                    for g in range(8):
                        p.wait("vector", TG, tg_base + g + 1)
                        p.emit("vector", lambda e, g=g: e.tensor_copy(
                            xn_fm[:, g * TC:(g + 1) * TC], pb[4][:, 0:512]), inc=TGE)
                    p.wait("vector", TG, tg_base + 9)
                    p.wait("vector", RR, 16 * c)
                    p.emit("vector", lambda e: e.tensor_copy(r1r4[:], pb[4][0:4, 0:128]), inc=TGE)
                    p.wait("sync", TGE, TGE.n)
                    p.emit("sync", lambda e: e.dma_start(out=rowr1[:], in_=r1r4[:]),
                           inc=RR, k=16)
                    pump_split(26)
                    p.wait("vector", BC, c + 1)
                    for g in range(8):
                        p.emit("vector", lambda e, g=g: e.tensor_tensor(
                            hbb[:], xn_fm[:, g * TC:(g + 1) * TC],
                            pb[6][:, 0:512], OP.mult))
                        p.emit("vector", lambda e, g=g: e.tensor_copy(
                            hinh[:, g * TC:(g + 1) * TC], hbb[:]))
                        p.emit("vector", lambda e, g=g: e.tensor_tensor(
                            hinl[:, g * TC:(g + 1) * TC], hbb[:],
                            hinh[:, g * TC:(g + 1) * TC], OP.subtract),
                            inc=(HIN if g == 7 else None))
                    if dbg_hin is not None:
                        p.wait("gpsimd", HIN, c + 1)
                        p.emit("gpsimd", lambda e, c=c: e.dma_start(
                            out=dbg_hin.ap()[c], in_=hinh[:]), inc=DBG, k=16)
                    for dh in range(4):
                        p.wait("vector", H1S, 4 * c + dh + 1)
                        p.emit("vector", lambda e: e.tensor_scalar(
                            scr[:, 0:512], scr[:, 0:512], 0.5, 0.5, OP.mult, OP.add))
                        p.emit("vector", lambda e, dh=dh: e.tensor_tensor(
                            h1b[:, dh * TC:(dh + 1) * TC], scr[:, 0:512], scr[:, 512:1024],
                            OP.mult), inc=H1V)
                    p.wait("vector", TG, tg_base + 10)
                    p.emit("vector", lambda e: e.tensor_copy(lgtm[:], pb[4][:, 0:256]), inc=TGE)
                    p.emit("vector", lambda e: e.tensor_copy(cur[:], lgtm[:]))
                    for r in range(8):
                        dst = mkc if r == 0 else (thrc if r == 7 else zc)
                        p.emit("vector", lambda e, d=dst: e.tensor_reduce(
                            d[:].rearrange("p (j o) -> p j o", j=4, o=1),
                            cur[:].rearrange("p (j s) -> p j s", j=4), AX.X, OP.max))
                        if r < 7:
                            for j in range(4):
                                p.emit("vector", lambda e, j=j, d=dst: e.tensor_scalar(
                                    eb[:, j * 64:(j + 1) * 64], cur[:, j * 64:(j + 1) * 64],
                                    d[:, j:j + 1], None, OP.is_equal))
                            for j in range(4):
                                p.emit("vector", lambda e, j=j: e.scalar_tensor_tensor(
                                    cur[:, j * 64:(j + 1) * 64], eb[:, j * 64:(j + 1) * 64],
                                    -1e30, cur[:, j * 64:(j + 1) * 64], OP.mult, OP.add))
                    p.emit("vector", lambda e: e.tensor_scalar(nmc[:], mkc[:], -0.5, None, OP.mult),
                           inc=TOPK)
                    for ec in range(8):
                        p.wait("vector", GCE, 16 * c + 2 * ec + 2)
                        p.emit("vector", lambda e: e.tensor_scalar(
                            bbb[:], ab[:], -1.0, 1.0, OP.mult, OP.add))
                        p.emit("vector", lambda e: e.tensor_tensor(bbb[:], bbb[:], cb[:], OP.mult))
                        p.emit("vector", lambda e, ec=ec: e.tensor_tensor_scan(
                            hbb[:], ab[:], bbb[:], state[:, ec:ec + 1], OP.mult, OP.add))
                        if mine:
                            p.emit("vector", lambda e, ec=ec: e.tensor_tensor(
                                xn_fm[:, ec * TC:(ec + 1) * TC], xn_fm[:, ec * TC:(ec + 1) * TC],
                                hbb[:], OP.add), inc=SCE)
                        else:
                            p.emit("vector", lambda e: e.tensor_copy(bbb[:], hbb[:]), inc=SCE)
                        p.emit("vector", lambda e, ec=ec: e.tensor_copy(
                            state[:, ec:ec + 1], hbb[:, TC - 1:TC]))
                    if c == MY0 - 1:
                        p.emit("vector", lambda e: e.tensor_scalar(
                            state[:], state[:], flagc[:, 0:1], None, OP.mult))
                    p.wait("vector", SEXP, c + 1)
                    p.emit("vector", lambda e: e.tensor_scalar(cur[:], eb[:], -1.0, 1.0,
                                                               OP.mult, OP.add))
                    p.emit("vector", lambda e: e.reciprocal(cur[:], cur[:]))
                    p.emit("vector", lambda e: e.tensor_scalar(eb[:], eb[:], 1.0, None, OP.add))
                    p.emit("vector", lambda e: e.tensor_tensor(eb[:], eb[:], cur[:], OP.mult))
                    for j in range(4):
                        p.emit("vector", lambda e, j=j: e.tensor_scalar(
                            aet[:, j * 64:(j + 1) * 64], lgtm[:, j * 64:(j + 1) * 64],
                            thrc[:, j:j + 1], None, OP.is_ge))
                    p.emit("vector", lambda e: e.tensor_tensor(aet[:], aet[:], eb[:], OP.mult))
                    p.emit("vector", lambda e: e.tensor_reduce(
                        zc[:].rearrange("p (j o) -> p j o", j=4, o=1),
                        aet[:].rearrange("p (j s) -> p j s", j=4), AX.X, OP.add))
                    p.emit("vector", lambda e: e.reciprocal(rzc[:], zc[:]))
                    for j in range(4):
                        p.emit("vector", lambda e, j=j: e.tensor_scalar(
                            alf[:, j * 64:(j + 1) * 64], aet[:, j * 64:(j + 1) * 64],
                            rzc[:, j:j + 1], None, OP.mult), inc=(ALPHA if j == 3 else None))
                    if dbg_lg is not None:
                        p.wait("gpsimd", TOPK, c + 1)
                        p.emit("gpsimd", lambda e, c=c: e.dma_start(
                            out=dbg_lg.ap()[c], in_=lgtm[:]), inc=DBG, k=16)
                        for i_, t_ in enumerate([mkc, nmc, thrc, zc]):
                            p.emit("gpsimd", lambda e, c=c, i_=i_, t_=t_: e.dma_start(
                                out=dbg_tk.ap()[c][:, i_ * 4:(i_ + 1) * 4], in_=t_[:]),
                                inc=DBG, k=16)
                    if dbg_h1 is not None:
                        p.wait("gpsimd", H1V, 4 * (c + 1))
                        p.emit("gpsimd", lambda e, c=c: e.dma_start(
                            out=dbg_h1.ap()[c], in_=h1b[:]), inc=DBG, k=16)
                    if dbg_alpha is not None:
                        p.wait("gpsimd", ALPHA, c + 1)
                        p.emit("gpsimd", lambda e, c=c: e.dma_start(
                            out=dbg_alpha.ap()[c], in_=alf[:]), inc=DBG, k=16)
                    if mine:
                        p.wait("vector", TG, tg_base + 11)
                        p.emit("vector", lambda e: e.tensor_copy(
                            lgf[0:64, 0:512], pb[4][0:64, 0:512]), inc=TGE)
                        p.wait("gpsimd", TGE, TGE.n)
                        p.emit("gpsimd", lambda e, c=c: e.dma_start(
                            out=alpha_d.ap()[c - MY0], in_=lgf[0:64, 0:512]),
                            inc=ASM, k=16)

                    if c == 0:
                        p.wait("tensor", IV, N_IV)
                    p.wait("tensor", XNT, c + 1)
                    for g in range(8):
                        p.wait("tensor", TGE, tg_base + g)
                        for j in range(4):
                            p.emit("tensor", _mm(pb[4][:, j * 128:(j + 1) * 128],
                                                 xn_tm[:, j * D + g * 128:j * D + (g + 1) * 128],
                                                 eye[:], j == 0, j == 3, is_t=True))
                        p.emit("tensor", tclose(TG), inc=TG)
                    if c == 0:
                        p.wait("tensor", RST0, 1)
                    p.wait("tensor", TGE, tg_base + 8)
                    p.emit("tensor", _mm(pb[4][0:4, 0:128], r1all[:, c * 4:(c + 1) * 4],
                                         eye[:], True, True, is_t=True))
                    p.emit("tensor", tclose(TG), inc=TG)
                    p.wait("tensor", RR, 16 * (c + 1))
                    p.wait("tensor", LGE, c)
                    p.wait("tensor", GCE, 16 * c)
                    p.emit("tensor", _mm(pb[6][:, 0:512], onesr[:], rowr1[:], True, True),
                           inc=BC)
                    p.wait("tensor", GCE, 16 * c - 1)
                    for dh in range(4):
                        p.wait("tensor", H1S, 4 * c + dh)
                        for kc in range(8):
                            p.emit("tensor", _mm(pb[5][:, 0:512],
                                                 w1a[:, (kc * 4 + dh) * 128:(kc * 4 + dh + 1) * 128],
                                                 xn_fm[:, kc * TC:(kc + 1) * TC], kc == 0, kc == 7),
                                   inc=(H1T if kc == 7 else None))
                    p.wait("tensor", HIN, c + 1)
                    p.wait("tensor", H1V, 4 * (c + 1))
                    for dh in range(4):
                        p.emit("tensor", _mm(pb[6][0:64, 0:512], w2r[:, dh * 64:(dh + 1) * 64],
                                             h1b[:, dh * TC:(dh + 1) * TC], dh == 0, dh == 3),
                               inc=(LGT if dh == 3 else None))
                    p.wait("tensor", LGE, c + 1)
                    p.wait("tensor", TGE, tg_base + 9)
                    for j in range(4):
                        p.emit("tensor", _mm(pb[4][:, j * 64:(j + 1) * 64],
                                             lgf[:, j * 128:(j + 1) * 128], eye[0:64, 0:64],
                                             j == 0, j == 3, is_t=True))
                    p.emit("tensor", tclose(TG), inc=TG)
                    p.wait("tensor", HIN, c + 1)
                    for ec in range(8):
                        gb = 2 if ec % 2 == 0 else 5
                        cbk = 3 if ec % 2 == 0 else 6
                        if ec == 1:
                            p.wait("tensor", H1S, 4 * c + 4)
                            p.wait("tensor", LGE, c + 1)
                        for bank, wh_, wl_, gch in ((gb, wgh, wgl, 0), (cbk, wch, wcl, 1)):
                            p.wait("tensor", GCE, 16 * c + 2 * ec - 3 + gch)
                            for kc in range(8):
                                sl = slice((kc * 8 + ec) * 128, (kc * 8 + ec + 1) * 128)
                                tsl = slice(kc * TC, (kc + 1) * TC)
                                p.emit("tensor", _mm(pb[bank][:, 0:512], wh_[:, sl],
                                                     hinh[:, tsl], kc == 0, False))
                                p.emit("tensor", _mm(pb[bank][:, 0:512], wh_[:, sl],
                                                     hinl[:, tsl], False, False))
                            for kc in range(8):
                                sl = slice((kc * 8 + ec) * 128, (kc * 8 + ec + 1) * 128)
                                tsl = slice(kc * TC, (kc + 1) * TC)
                                p.emit("tensor", _mm(pb[bank][:, 0:512], wl_[:, sl],
                                                     hinh[:, tsl], False, kc == 7),
                                       inc=(GCT if kc == 7 else None))
                    p.wait("tensor", ALPHA, c + 1)
                    first = (c == 0)
                    last = (c == NCH - 1)
                    for j in range(4):
                        for dchi in range(8):
                            p.emit("tensor", _mm(
                                pb[0][:, dchi * 64:(dchi + 1) * 64],
                                xn_tm[:, j * D + dchi * 128:j * D + (dchi + 1) * 128],
                                alf[:, j * 64:(j + 1) * 64],
                                first and j == 0 and dchi == 0, last and j == 3 and dchi == 7))
                        p.emit("tensor", _mm(pb[1][0:1, 0:64], onesc[:],
                                             alf[:, j * 64:(j + 1) * 64],
                                             first and j == 0, last and j == 3),
                               inc=(SLT if j == 3 else None))
                    if mine:
                        p.wait("tensor", TGE, tg_base + 10)
                        for j in range(4):
                            p.emit("tensor", _mm(pb[4][0:64, j * 128:(j + 1) * 128],
                                                 alf[:, j * 64:(j + 1) * 64], eye[:],
                                                 j == 0, j == 3, is_t=True))
                        p.emit("tensor", tclose(TG), inc=TG)

                    if dbg and "serial" in debug_outputs:
                        p.wait("gpsimd", DBG, DBG.n)
                        p.barrier()
                    if mine:
                        p.wait("gpsimd", SCE, 8 * c + 8)
                        p.emit("gpsimd", lambda e, c=c: e.dma_start(
                            out=x2s_d.ap()[c - MY0].rearrange("g p t -> p g t"),
                            in_=xn_fm[:].rearrange("p (g t) -> p g t", g=8)), inc=SPL, k=16)
                assert pump_split.i == NJOBS
                p.wait("gpsimd", SPL, 64)
                p.wait("gpsimd", SPW, SPW.n)
                p.wait("gpsimd", ASM, 64)
                if dbg:
                    p.wait("gpsimd", DBG, DBG.n)
                p.barrier()
                p.flush(block)

            with ExitStack() as bscope:
                bsb = lambda name, shape: bscope.enter_context(nc.sbuf_tensor(name, shape, F32))
                wstr = bsb("wstr", [128, 2 * 3072])
                gis = bsb("gis", [64, 3072])
                ghs = bsb("ghs", [64, 3072])
                bihb = bsb("bihb", [64, 3072])
                bhhb = bsb("bhhb", [64, 3072])
                bqb = bsb("bqb", [64, 256])
                bkb = bsb("bkb", [64, 256])
                bvb = bsb("bvb", [64, D])
                bvpb = bsb("bvpb", [64, D])
                lngb = bsb("lngb", [64, D])
                lnbb = bsb("lnbb", [64, D])
                rowB = bsb("rowB", [1, 3072])
                sis = bsb("sis", [128, 512])
                sidm = bsb("sidm", [128, 512])
                dres = bsb("dres", [1, 64])
                sS = bsb("sS", [64, D])
                tg1 = bsb("tg1", [64, D])
                tg2 = bsb("tg2", [64, D])
                rg = bsb("rg", [64, D])
                zg = bsb("zg", [64, D])
                ngb = bsb("ngb", [64, D])
                snew = bsb("snew", [64, D])
                sndm = bsb("sndm", [128, 512])
                qv = bsb("qv", [64, 256])
                kvb = bsb("kvb", [64, 256])
                vv = bsb("vv", [64, D])
                qdm = bsb("qdm", [128, 128])
                kdm = bsb("kdm", [128, 128])
                sc64 = bsb("sc64", [64, 64])
                at64 = bsb("at64", [64, 64])
                yv = bsb("yv", [64, D])
                ydm = bsb("ydm", [128, 512])
                wqb = bsb("wqb", [128, 2048])
                wkb = bsb("wkb", [128, 2048])
                cB = bsb("cB", [64, 8])

                BL = p.ch("BL"); BT = p.ch("BT"); BV = p.ch("BV"); BS = p.ch("BS")

                def bld(dst_ap, src_ap):
                    p.emit("sync", lambda e, o=dst_ap, i=src_ap: e.dma_start(out=o, in_=i),
                           inc=BL, k=16)

                def ts_(fns, waits=()):
                    for ch, n in waits:
                        p.wait("tensor", ch, n)
                    for i, f in enumerate(fns):
                        p.emit("tensor", f, inc=(BT if i == len(fns) - 1 else None))

                def vs_(fns, waits=()):
                    for ch, n in waits:
                        p.wait("vector", ch, n)
                    for i, f in enumerate(fns):
                        p.emit("vector", f, inc=(BV if i == len(fns) - 1 else None))

                def ss_(fns, waits=()):
                    for ch, n in waits:
                        p.wait("scalar", ch, n)
                    for i, f in enumerate(fns):
                        p.emit("scalar", f, inc=(BS if i == len(fns) - 1 else None))

                bld(sS[:], sinit_d.ap())
                bld(wqb[:].rearrange("p (k t) -> p k t", k=8), wq_d.ap())
                bld(wkb[:].rearrange("p (k t) -> p k t", k=8), wk_d.ap())
                NB0 = BL.n

                p.wait("vector", p.ch("SLT"), 8)
                vs_([lambda e: e.tensor_copy(dres[:], pb[1][0:1, 0:64]),
                     lambda e: e.tensor_scalar(dres[:], dres[:], 1e-8, None, OP.add),
                     lambda e: e.reciprocal(dres[:], dres[:]),
                     lambda e: e.tensor_copy(sis[:], pb[0][:, 0:512])])
                if dbg_den is not None:
                    p.wait("gpsimd", BV, BV.n)
                    p.emit("gpsimd", lambda e: e.dma_start(out=dbg_den.ap(), in_=dres[:]),
                           inc=DBG, k=16)
                ts_([_mm(pb[2][0:128, 0:64], onesr[:], dres[:], True, True)], waits=[(BV, BV.n)])
                vs_([(lambda e, g=g: e.tensor_tensor(
                        sidm[:, g * 64:(g + 1) * 64], sis[:, g * 64:(g + 1) * 64],
                        pb[2][:, 0:64], OP.mult)) for g in range(8)],
                    waits=[(BT, BT.n)])
                if dbg_si is not None:
                    p.wait("gpsimd", BV, BV.n)
                    p.emit("gpsimd", lambda e: e.dma_start(out=dbg_si.ap(), in_=sidm[:]),
                           inc=DBG, k=16)

                RB = p.ch("RB")

                def bcast64(dst, src_d, n):
                    p.wait("sync", RB, RB.n)
                    bld(rowB[0:1, 0:n], src_d.ap())
                    nldw = BL.n
                    nseg = (n + 511) // 512
                    for i in range(nseg):
                        w = min(512, n - i * 512)
                        p.wait("tensor", p.ch("BL"), nldw)
                        p.wait("tensor", BV, BV.n)
                        p.emit("tensor", _mm(pb[3][0:64, 0:w], onesr[:, 0:64],
                                             rowB[0:1, i * 512:i * 512 + w], True, True),
                               inc=RB)
                        vs_([lambda e, dd=dst, ii=i, ww=w: e.tensor_copy(
                            dd[0:64, ii * 512:ii * 512 + ww], pb[3][0:64, 0:ww])],
                            waits=[(RB, RB.n)])

                bcast64(bihb, bihrow_d, 3072)
                bcast64(bhhb, bhhrow_d, 3072)
                bcast64(bqb, bqrow_d, 256)
                bcast64(bkb, bkrow_d, 256)
                bcast64(bvb, bvrow_d, D)
                bcast64(bvpb, bvprow_d, D)
                bcast64(lngb, lngrow_d, D)
                bcast64(lnbb, lnbrow_d, D)

                WS = p.ch("WS")
                ws_free = {0: 0, 1: 0}

                for wd, dstbuf, lhs in ((wihT_d, gis, sidm), (whhT_d, ghs, sinitT)):
                    for kc in range(8):
                        par = kc % 2
                        buf = wstr[:, par * 3072:(par + 1) * 3072]
                        p.wait("sync", WS, ws_free[par])
                        p.emit("sync", lambda e, o=buf, i=wd.ap()[kc]:
                               e.dma_start(out=o, in_=i), inc=BL, k=16)
                        p.wait("tensor", BL, BL.n)
                        if kc == 0:
                            p.wait("tensor", BV, BV.n)
                        for seg in range(6):
                            p.emit("tensor", _mm(pb[2 + seg][0:64, 0:512],
                                                 lhs[:, kc * 64:(kc + 1) * 64],
                                                 buf[:, seg * 512:(seg + 1) * 512],
                                                 kc == 0, kc == 7),
                                   inc=(WS if seg == 5 else None))
                        ws_free[par] = WS.n
                    vs_([(lambda e, dd=dstbuf, s=seg: e.tensor_copy(
                            dd[0:64, s * 512:(s + 1) * 512], pb[2 + s][0:64, 0:512]))
                         for seg in range(6)], waits=[(WS, WS.n)])

                for gate, dstg in ((0, rg), (1, zg)):
                    o = gate * D
                    vs_([lambda e, o=o: e.tensor_tensor(tg1[:], gis[0:64, o:o + D],
                                                        bihb[0:64, o:o + D], OP.add),
                         lambda e, o=o: e.tensor_tensor(tg2[:], ghs[0:64, o:o + D],
                                                        bhhb[0:64, o:o + D], OP.add),
                         lambda e: e.tensor_tensor(tg1[:], tg1[:], tg2[:], OP.add)])
                    if gate == 0:
                        act_fence(AF.Sigmoid)
                    ss_([lambda e, d=dstg: e.activation(d[:], tg1[:], AF.Sigmoid)],
                        waits=[(BV, BV.n)])
                    p.wait("vector", BS, BS.n)
                o = 2 * D
                vs_([lambda e: e.tensor_tensor(tg1[:], ghs[0:64, o:o + D],
                                               bhhb[0:64, o:o + D], OP.add),
                     lambda e: e.tensor_tensor(tg1[:], rg[:], tg1[:], OP.mult),
                     lambda e: e.tensor_tensor(tg2[:], gis[0:64, o:o + D],
                                               bihb[0:64, o:o + D], OP.add),
                     lambda e: e.tensor_tensor(tg1[:], tg1[:], tg2[:], OP.add)])
                ss_([lambda e: e.activation(ngb[:], tg1[:], AF.Tanh)], waits=[(BV, BV.n)])
                vs_([lambda e: e.tensor_scalar(tg1[:], zg[:], -1.0, 1.0, OP.mult, OP.add),
                     lambda e: e.tensor_tensor(tg1[:], tg1[:], ngb[:], OP.mult),
                     lambda e: e.tensor_tensor(tg2[:], zg[:], sS[:], OP.mult),
                     lambda e: e.tensor_tensor(snew[:], tg1[:], tg2[:], OP.add)],
                    waits=[(BS, BS.n), (BL, NB0)])
                if dbg_snew is not None:
                    p.wait("gpsimd", BV, BV.n)
                    p.emit("gpsimd", lambda e: e.dma_start(out=dbg_snew.ap(), in_=snew[:]),
                           inc=DBG, k=16)
                for ch_, n_ in [(BV, BV.n)]:
                    p.wait("tensor", ch_, n_)
                for g in range(8):
                    p.emit("tensor", _mm(pb[2][:, g * 64:(g + 1) * 64],
                                         snew[0:64, g * 128:(g + 1) * 128],
                                         eye[0:64, 0:64], g == 0, g == 7, is_t=True))
                p.emit("tensor", tclose(BT), inc=BT)
                vs_([lambda e: e.tensor_copy(sndm[:], pb[2][:, 0:512])], waits=[(BT, BT.n)])
                ts_([_mm(pb[3][0:64, 0:256], sndm[:, kc * 64:(kc + 1) * 64],
                         wqb[:, kc * 256:(kc + 1) * 256], kc == 0, kc == 7) for kc in range(8)]
                    + [_mm(pb[4][0:64, 0:256], sndm[:, kc * 64:(kc + 1) * 64],
                           wkb[:, kc * 256:(kc + 1) * 256], kc == 0, kc == 7) for kc in range(8)],
                    waits=[(BV, BV.n)])
                for kc in range(8):
                    par = kc % 2
                    vbuf = wstr[:, par * 3072:par * 3072 + 1024]
                    p.wait("sync", WS, ws_free[par])
                    p.emit("sync", lambda e, o=vbuf, i=wv_d.ap()[:, kc, :]:
                           e.dma_start(out=o, in_=i), inc=BL, k=16)
                    p.wait("tensor", BL, BL.n)
                    p.emit("tensor", _mm(pb[5][0:64, 0:512], sndm[:, kc * 64:(kc + 1) * 64],
                                         vbuf[:, 0:512], kc == 0, kc == 7))
                    p.emit("tensor", _mm(pb[6][0:64, 0:512], sndm[:, kc * 64:(kc + 1) * 64],
                                         vbuf[:, 512:1024], kc == 0, kc == 7),
                           inc=WS)
                    ws_free[par] = WS.n
                p.emit("tensor", lambda e: e.nop(), inc=BT)
                vs_([lambda e: e.tensor_tensor(qv[:], pb[3][0:64, 0:256], bqb[:], OP.add),
                     lambda e: e.tensor_tensor(kvb[:], pb[4][0:64, 0:256], bkb[:], OP.add),
                     lambda e: e.tensor_tensor(vv[:, 0:512], pb[5][0:64, 0:512],
                                               bvb[:, 0:512], OP.add),
                     lambda e: e.tensor_tensor(vv[:, 512:1024], pb[6][0:64, 0:512],
                                               bvb[:, 512:1024], OP.add)],
                    waits=[(BT, BT.n)])
                p.wait("tensor", BV, BV.n)
                for i in range(2):
                    p.emit("tensor", _mm(pb[2][:, i * 64:(i + 1) * 64],
                                         qv[0:64, i * 128:(i + 1) * 128],
                                         eye[0:64, 0:64], i == 0, i == 1, is_t=True))
                for i in range(2):
                    p.emit("tensor", _mm(pb[3][:, i * 64:(i + 1) * 64],
                                         kvb[0:64, i * 128:(i + 1) * 128],
                                         eye[0:64, 0:64], i == 0, i == 1, is_t=True))
                p.emit("tensor", tclose(BT), inc=BT)
                vs_([lambda e: e.tensor_copy(qdm[:], pb[2][:, 0:128]),
                     lambda e: e.tensor_copy(kdm[:], pb[3][:, 0:128])], waits=[(BT, BT.n)])
                ts_([_mm(pb[4][0:64, 0:64], qdm[:, i * 64:(i + 1) * 64],
                         kdm[:, i * 64:(i + 1) * 64], i == 0, i == 1) for i in range(2)],
                    waits=[(BV, BV.n)])
                ss_([lambda e: e.activation(sc64[:], pb[4][0:64, 0:64], AF.Copy, scale=0.0625)],
                    waits=[(BT, BT.n)])
                vs_([lambda e: e.tensor_reduce(cB[:, 0:1], sc64[:], AX.X, OP.max),
                     lambda e: e.tensor_scalar(cB[:, 1:2], cB[:, 0:1], -1.0, None, OP.mult)],
                    waits=[(BS, BS.n)])
                act_fence(AF.Exp)
                ss_([lambda e: e.activation(at64[:], sc64[:], AF.Exp, bias=cB[:, 1:2])],
                    waits=[(BV, BV.n)])
                vs_([lambda e: e.tensor_reduce(cB[:, 2:3], at64[:], AX.X, OP.add),
                     lambda e: e.reciprocal(cB[:, 3:4], cB[:, 2:3]),
                     lambda e: e.tensor_scalar(at64[:], at64[:], cB[:, 3:4], None, OP.mult)],
                    waits=[(BS, BS.n)])
                p.wait("tensor", BV, BV.n)
                p.emit("tensor", _mm(pb[2][0:64, 0:64], at64[:], eye[0:64, 0:64],
                                     True, True, is_t=True))
                p.emit("tensor", tclose(BT), inc=BT)
                vs_([lambda e: e.tensor_copy(sc64[:], pb[2][0:64, 0:64])], waits=[(BT, BT.n)])
                ts_([_mm(pb[5][0:64, 0:512], sc64[:], vv[:, 0:512], True, True),
                     _mm(pb[6][0:64, 0:512], sc64[:], vv[:, 512:1024], True, True)],
                    waits=[(BV, BV.n)])
                vs_([lambda e: e.tensor_tensor(tg2[:, 0:512], snew[0:64, 0:512],
                                               pb[5][0:64, 0:512], OP.add),
                     lambda e: e.tensor_tensor(tg2[:, 512:1024], snew[0:64, 512:1024],
                                               pb[6][0:64, 0:512], OP.add),
                     lambda e: e.tensor_reduce(cB[:, 0:1], tg2[:], AX.X, OP.add)],
                    waits=[(BT, BT.n)])
                ss_([lambda e: e.activation(tg1[:], tg2[:], AF.Square, accum_out=cB[:, 1:2])],
                    waits=[(BV, BV.n)])
                vs_([lambda e: e.tensor_scalar(cB[:, 4:5], cB[:, 0:1], INV_D, None, OP.mult),
                     lambda e: e.tensor_scalar(cB[:, 5:6], cB[:, 1:2], INV_D, None, OP.mult),
                     lambda e: e.tensor_tensor(cB[:, 2:3], cB[:, 4:5], cB[:, 4:5], OP.mult),
                     lambda e: e.tensor_tensor(cB[:, 2:3], cB[:, 5:6], cB[:, 2:3], OP.subtract),
                     lambda e: e.tensor_scalar(cB[:, 2:3], cB[:, 2:3], 1e-5, None, OP.add)],
                    waits=[(BS, BS.n)])
                ss_([lambda e: e.activation(cB[:, 6:7], cB[:, 2:3], AF.Ln),
                     lambda e: e.activation(cB[:, 3:4], cB[:, 6:7], AF.Exp, scale=-0.5)],
                    waits=[(BV, BV.n)])
                vs_([lambda e: e.tensor_scalar(yv[:], tg2[:], cB[:, 4:5], cB[:, 3:4],
                                               OP.subtract, OP.mult),
                     lambda e: e.tensor_tensor(tg1[:], yv[:], lngb[:], OP.mult),
                     lambda e: e.tensor_tensor(yv[:], tg1[:], lnbb[:], OP.add)],
                    waits=[(BS, BS.n)])
                p.wait("tensor", BV, BV.n)
                for g in range(8):
                    p.emit("tensor", _mm(pb[2][:, g * 64:(g + 1) * 64],
                                         yv[0:64, g * 128:(g + 1) * 128],
                                         eye[0:64, 0:64], g == 0, g == 7, is_t=True))
                p.emit("tensor", tclose(BT), inc=BT)
                vs_([lambda e: e.tensor_copy(ydm[:], pb[2][:, 0:512])], waits=[(BT, BT.n)])
                p.wait("tensor", BV, BV.n)
                for kc in range(8):
                    par = kc % 2
                    vbuf = wstr[:, par * 3072:par * 3072 + 1024]
                    p.wait("sync", WS, ws_free[par])
                    p.emit("sync", lambda e, o=vbuf, i=wvp_d.ap()[:, kc, :]:
                           e.dma_start(out=o, in_=i), inc=BL, k=16)
                    p.wait("tensor", BL, BL.n)
                    p.emit("tensor", _mm(pb[5][0:64, 0:512], ydm[:, kc * 64:(kc + 1) * 64],
                                         vbuf[:, 0:512], kc == 0, kc == 7))
                    p.emit("tensor", _mm(pb[6][0:64, 0:512], ydm[:, kc * 64:(kc + 1) * 64],
                                         vbuf[:, 512:1024], kc == 0, kc == 7),
                           inc=WS)
                    ws_free[par] = WS.n
                p.emit("tensor", lambda e: e.nop(), inc=BT)
                vs_([lambda e: e.tensor_tensor(svb[0:64, 0:512], pb[5][0:64, 0:512],
                                               bvpb[:, 0:512], OP.add),
                     lambda e: e.tensor_tensor(svb[0:64, 512:1024], pb[6][0:64, 0:512],
                                               bvpb[:, 512:1024], OP.add)],
                    waits=[(BT, BT.n)])
                if dbg_sv is not None:
                    p.wait("gpsimd", BV, BV.n)
                    p.emit("gpsimd", lambda e: e.dma_start(out=dbg_sv.ap(), in_=svb[:]),
                           inc=DBG, k=16)
                if dbg_bp is not None:
                    p.wait("gpsimd", BV, BV.n)
                    p.wait("gpsimd", BS, BS.n)
                    taps = [(gis[0:64, :], 3072), (ghs[0:64, :], 3072), (rg[:], 1024),
                            (zg[:], 1024), (ngb[:], 1024), (snew[:], 1024),
                            (qv[:], 256), (kvb[:], 256), (vv[:], 1024),
                            (sndm[0:64, :], 512), (at64[:], 64), (yv[:], 1024), (cB[:], 8), (tg2[:], 1024)]
                    for i_, (ap_, w_) in enumerate(taps):
                        p.emit("gpsimd", lambda e, i_=i_, a_=ap_, w_=w_: e.dma_start(
                            out=dbg_bp.ap()[i_][:, 0:w_], in_=a_), inc=DBG, k=16)
                p.barrier()
                p.flush(block)

            with ExitStack() as cscope:
                csb = lambda name, shape: cscope.enter_context(nc.sbuf_tensor(name, shape, F32))
                csbb = lambda name, shape: cscope.enter_context(
                    nc.sbuf_tensor(name, shape, BF16))
                x2t = csb("x2t", [128, 8 * TC])
                finth = csbb("finth", [128, 8 * TC])
                fintl = csbb("fintl", [128, 8 * TC])
                Pth = csbb("Pth", [128, 16 * TC])
                Ptl = csbb("Ptl", [128, 16 * TC])
                ust = csb("ust", [128, TC])
                st_ = csb("st_", [128, 8 * TC])
                osth = csbb("osth", [128, 8 * TC])
                ostl = csbb("ostl", [128, 8 * TC])
                gt = csb("gt", [128, TC])
                sgt = csb("sgt", [128, 8 * TC])
                finb = csb("finb", [128, 2 * D])
                xmt = csb("xmt", [128, 2 * D])
                w1b2h = csbb("w1b2h", [128, 2 * 1024])
                w1b2l = csbb("w1b2l", [128, 2 * 1024])
                w3b2h = csbb("w3b2h", [128, 2 * 1024])
                w3b2l = csbb("w3b2l", [128, 2 * 1024])
                w2b2h = csbb("w2b2h", [128, 2 * 2048])
                w2b2l = csbb("w2b2l", [128, 2 * 2048])
                wob2h = csbb("wob2h", [128, 2 * 1024])
                wob2l = csbb("wob2l", [128, 2 * 1024])
                rowC = csb("rowC", [1, 512])
                sqsc = csb("sqsc", [128, TC])
                ptmp = csb("ptmp", [128, TC])
                r2rows = csb("r2rows", [1, 4 * TC])
                gs_ = csb("gs_", [128, TC])
                us2 = csb("us2", [128, TC])
                asmt = csb("asmt", [64, 2 * TC])

                CUS = p.ch("CUS")
                LD2 = p.ch("LD2"); LW13 = p.ch("LW13"); LW2 = p.ch("LW2")
                LWO = p.ch("LWO"); LXM = p.ch("LXM")
                CSQ = p.ch("CSQ"); CR2M = p.ch("CR2M"); CR2V = p.ch("CR2V")
                CR2S = p.ch("CR2S"); CBC = p.ch("CBC"); CFIN = p.ch("CFIN")
                CU = p.ch("CU"); CSIL = p.ch("CSIL"); CV = p.ch("CV"); CP = p.ch("CP")
                CW2G = p.ch("CW2G"); CSEV = p.ch("CSEV"); COS = p.ch("COS")
                COSE = p.ch("COSE"); CWOT = p.ch("CWOT"); CGE = p.ch("CGE")
                CSG = p.ch("CSG"); CTRF = p.ch("CTRF"); CFV = p.ch("CFV")
                COUT = p.ch("COUT"); LAS = p.ch("LAS")


                for tt in range(4):
                    if tt == 0:
                        p.wait("sync", SPV, SPV.n)
                    p.wait("sync", CSQ, 8 * tt)
                    p.emit("sync", lambda e, tt=tt: e.dma_start(
                        out=x2t[:].rearrange("p (g t) -> p g t", g=8),
                        in_=x2s_d.ap()[tt].rearrange("g p t -> p g t")), inc=LD2, k=16)
                    p.wait("scalar", LD2, 16 * (tt + 1))
                    for g in range(8):
                        if g > 0:
                            p.wait("scalar", CR2M, 8 * tt + g)
                        p.emit("scalar", lambda e, g=g: e.activation(
                            sqsc[:], x2t[:, g * TC:(g + 1) * TC], AF.Square), inc=CSQ)
                    for g in range(8):
                        p.wait("tensor", CSQ, 8 * tt + g + 1)
                        p.emit("tensor", _mm(pb[0][0:1, 0:512], onesc[:], sqsc[:],
                                             g == 0, g == 7), inc=CR2M)
                    p.wait("vector", CR2M, 8 * (tt + 1))
                    p.emit("vector", lambda e, tt=tt: e.tensor_scalar(
                        r2rows[0:1, tt * TC:(tt + 1) * TC], pb[0][0:1, 0:512],
                        INV_D, 1e-6, OP.mult, OP.add), inc=CR2V)
                p.wait("scalar", CR2V, 4)
                p.emit("scalar", lambda e: e.activation(r2rows[:], r2rows[:], AF.Ln))
                p.emit("scalar", lambda e: e.activation(r2rows[:], r2rows[:], AF.Exp, scale=-0.5),
                       inc=CR2S)
                act_fence(AF.Sigmoid)

                for tt in range(4):
                    if tt == 0:
                        p.wait("sync", CSQ, 32)
                    p.wait("sync", CSEV, 16 * tt)
                    p.wait("sync", CFIN, tt)
                    p.emit("sync", lambda e, tt=tt: e.dma_start(
                        out=x2t[:].rearrange("p (g t) -> p g t", g=8),
                        in_=x2s_d.ap()[tt].rearrange("g p t -> p g t")), inc=LD2, k=16)
                    p.wait("sync", COS, 8 * (tt - 1))
                    p.emit("sync", lambda e, tt=tt: e.dma_start(
                        out=asmt[0:64, (tt % 2) * TC:(tt % 2 + 1) * TC],
                        in_=alpha_d.ap()[tt]), inc=LAS, k=16)
                    p.wait("tensor", CR2S, 1)
                    if tt > 0:
                        p.wait("tensor", CFIN, tt)
                    p.emit("tensor", _mm(pb[1][:, 0:512], onesr[:],
                                         r2rows[0:1, tt * TC:(tt + 1) * TC], True, True),
                           inc=CBC)
                    p.wait("vector", CBC, tt + 1)
                    p.wait("vector", LD2, 16 * (4 + tt + 1))
                    if tt == 0:
                        p.wait("vector", SPW, SPW.n)
                    for g in range(8):
                        p.emit("vector", lambda e, g=g: e.scalar_tensor_tensor(
                            ptmp[:], x2t[:, g * TC:(g + 1) * TC],
                            n2c[:, g:g + 1], pb[1][:, 0:512], OP.mult, OP.mult))
                        p.emit("vector", lambda e, g=g: e.tensor_copy(
                            finth[:, g * TC:(g + 1) * TC], ptmp[:]))
                        p.emit("vector", lambda e, g=g: e.tensor_tensor(
                            fintl[:, g * TC:(g + 1) * TC], ptmp[:],
                            finth[:, g * TC:(g + 1) * TC], OP.subtract),
                            inc=(CFIN if g == 7 else None))
                    if dbg_fin0 is not None and tt == 0:
                        p.wait("gpsimd", CFIN, 1)
                        p.emit("gpsimd", lambda e: e.dma_start(out=dbg_fin0.ap(), in_=finth[:]),
                               inc=DBG, k=16)

                    cu_base = CU.n
                    sev_base = CSEV.n
                    w2g_base = CW2G.n
                    for h in range(2):
                        for i in range(16):
                            idx = h * 16 + i
                            par_ = idx % 2
                            wb1h = w1b2h[:, par_ * 1024:(par_ + 1) * 1024]
                            wb1l = w1b2l[:, par_ * 1024:(par_ + 1) * 1024]
                            wb3h = w3b2h[:, par_ * 1024:(par_ + 1) * 1024]
                            wb3l = w3b2l[:, par_ * 1024:(par_ + 1) * 1024]
                            dff = h * 16 + i
                            if tt == 0:
                                p.wait("sync", SPW, 128 * (dff + 1))
                            if idx >= 2:
                                p.wait("sync", CU, cu_base + idx - 1)
                            p.emit("sync", lambda e, o=wb1h, d=dff: e.dma_start(
                                out=o, in_=w1h_d.ap()[d]), inc=LW13, k=16)
                            p.emit("sync", lambda e, o=wb1l, d=dff: e.dma_start(
                                out=o, in_=w1l_d.ap()[d]), inc=LW13, k=16)
                            if idx >= 2:
                                p.wait("sync", CV, cu_base + idx - 1)
                            p.emit("sync", lambda e, o=wb3h, d=dff: e.dma_start(
                                out=o, in_=w3h_d.ap()[d]), inc=LW13, k=16)
                            p.emit("sync", lambda e, o=wb3l, d=dff: e.dma_start(
                                out=o, in_=w3l_d.ap()[d]), inc=LW13, k=16)
                            p.wait("tensor", LW13, LW13.n - 32)
                            p.wait("tensor", CUS, cu_base + idx)
                            if tt > 0 and idx == 0:
                                p.wait("tensor", CFV, 4 * tt)
                            if idx == 0:
                                p.wait("tensor", CFIN, tt + 1)
                            for kc in range(8):
                                p.emit("tensor", _mm(pb[2][:, 0:512],
                                                     wb1h[:, kc * 128:(kc + 1) * 128],
                                                     finth[:, kc * TC:(kc + 1) * TC],
                                                     kc == 0, False))
                                p.emit("tensor", _mm(pb[2][:, 0:512],
                                                     wb1h[:, kc * 128:(kc + 1) * 128],
                                                     fintl[:, kc * TC:(kc + 1) * TC],
                                                     False, False))
                            for kc in range(8):
                                p.emit("tensor", _mm(pb[2][:, 0:512],
                                                     wb1l[:, kc * 128:(kc + 1) * 128],
                                                     finth[:, kc * TC:(kc + 1) * TC],
                                                     False, kc == 7),
                                       inc=(CU if kc == 7 else None))
                            p.wait("scalar", CU, cu_base + idx + 1)
                            p.wait("scalar", CUS, cu_base + idx)
                            p.emit("scalar", lambda e: e.activation(us2[:], pb[2][:, 0:512],
                                                                    AF.Sigmoid), inc=CSIL)
                            p.wait("tensor", LW13, LW13.n)
                            p.wait("tensor", CP, cu_base + idx)
                            for kc in range(8):
                                p.emit("tensor", _mm(pb[3][:, 0:512],
                                                     wb3h[:, kc * 128:(kc + 1) * 128],
                                                     finth[:, kc * TC:(kc + 1) * TC],
                                                     kc == 0, False))
                                p.emit("tensor", _mm(pb[3][:, 0:512],
                                                     wb3h[:, kc * 128:(kc + 1) * 128],
                                                     fintl[:, kc * TC:(kc + 1) * TC],
                                                     False, False))
                            for kc in range(8):
                                p.emit("tensor", _mm(pb[3][:, 0:512],
                                                     wb3l[:, kc * 128:(kc + 1) * 128],
                                                     finth[:, kc * TC:(kc + 1) * TC],
                                                     False, kc == 7),
                                       inc=(CV if kc == 7 else None))
                            p.wait("vector", CSIL, cu_base + idx + 1)
                            p.emit("vector", lambda e: e.tensor_tensor(
                                ust[:], us2[:], pb[2][:, 0:512], OP.mult), inc=CUS)
                            p.wait("vector", CV, cu_base + idx + 1)
                            if h == 1 and i == 0:
                                p.wait("vector", CW2G, w2g_base + 8)
                            p.emit("vector", lambda e: e.tensor_tensor(
                                ptmp[:], ust[:], pb[3][:, 0:512], OP.mult))
                            p.emit("vector", lambda e, i=i: e.tensor_copy(
                                Pth[:, i * TC:(i + 1) * TC], ptmp[:]))
                            p.emit("vector", lambda e, i=i: e.tensor_tensor(
                                Ptl[:, i * TC:(i + 1) * TC], ptmp[:],
                                Pth[:, i * TC:(i + 1) * TC], OP.subtract),
                                inc=CP)
                        for g in range(8):
                            gi_ = h * 8 + g
                            wb2h = w2b2h[:, (g % 2) * 2048:(g % 2 + 1) * 2048]
                            wb2l = w2b2l[:, (g % 2) * 2048:(g % 2 + 1) * 2048]
                            if tt == 0 and gi_ < 2:
                                p.wait("sync", SPW, SPW.n)
                            if gi_ >= 2:
                                p.wait("sync", CW2G, w2g_base + gi_ - 1)
                            p.emit("sync", lambda e, o=wb2h, g=g, h=h: e.dma_start(
                                out=o, in_=w2h_d.ap()[g][:, h * 2048:(h + 1) * 2048]),
                                inc=LW2, k=16)
                            p.emit("sync", lambda e, o=wb2l, g=g, h=h: e.dma_start(
                                out=o, in_=w2l_d.ap()[g][:, h * 2048:(h + 1) * 2048]),
                                inc=LW2, k=16)
                            p.wait("tensor", LW2, LW2.n)
                            p.wait("tensor", CP, cu_base + (h + 1) * 16)
                            if gi_ >= 2:
                                p.wait("tensor", CSEV, sev_base + gi_ - 1)
                            bank = pb[4 + g % 2]
                            for di in range(16):
                                p.emit("tensor", _mm(bank[:, 0:512],
                                                     wb2h[:, di * 128:(di + 1) * 128],
                                                     Pth[:, di * TC:(di + 1) * TC],
                                                     di == 0, False))
                                p.emit("tensor", _mm(bank[:, 0:512],
                                                     wb2h[:, di * 128:(di + 1) * 128],
                                                     Ptl[:, di * TC:(di + 1) * TC],
                                                     False, False))
                            for di in range(16):
                                p.emit("tensor", _mm(bank[:, 0:512],
                                                     wb2l[:, di * 128:(di + 1) * 128],
                                                     Pth[:, di * TC:(di + 1) * TC],
                                                     False, di == 15),
                                       inc=(CW2G if di == 15 else None))
                            p.wait("vector", CW2G, w2g_base + gi_ + 1)
                            if h == 0:
                                p.emit("vector", lambda e, g=g, b=bank: e.tensor_tensor(
                                    st_[:, g * TC:(g + 1) * TC], b[:, 0:512],
                                    x2t[:, g * TC:(g + 1) * TC], OP.add), inc=CSEV)
                            else:
                                p.emit("vector", lambda e, g=g, b=bank: e.tensor_tensor(
                                    st_[:, g * TC:(g + 1) * TC], b[:, 0:512],
                                    st_[:, g * TC:(g + 1) * TC], OP.add), inc=CSEV)
                    if dbg_s0 is not None and tt == 0:
                        p.wait("gpsimd", CSEV, 16)
                        p.emit("gpsimd", lambda e: e.dma_start(out=dbg_s0.ap(), in_=st_[:]),
                               inc=DBG, k=16)

                    for g in range(8):
                        if g == 0:
                            p.wait("tensor", CGE, 8 * tt)
                            p.wait("tensor", LAS, 16 * (tt + 1))
                        p.wait("tensor", COSE, 8 * tt + g - 1)
                        p.emit("tensor", _mm(pb[6 + g % 2][:, 0:512],
                                             svb[0:64, g * 128:(g + 1) * 128],
                                             asmt[0:64, (tt % 2) * TC:(tt % 2 + 1) * TC],
                                             True, True), inc=COS)
                        p.wait("vector", COS, 8 * tt + g + 1)
                        p.wait("vector", CWOT, 8 * tt)
                        p.emit("vector", lambda e, g=g: e.tensor_copy(ptmp[:],
                            pb[6 + g % 2][:, 0:512]))
                        p.emit("vector", lambda e, g=g: e.tensor_copy(
                            osth[:, g * TC:(g + 1) * TC], ptmp[:]))
                        p.emit("vector", lambda e, g=g: e.tensor_tensor(
                            ostl[:, g * TC:(g + 1) * TC], ptmp[:],
                            osth[:, g * TC:(g + 1) * TC], OP.subtract), inc=COSE)
                    for ec in range(8):
                        wobh = wob2h[:, (ec % 2) * 1024:(ec % 2 + 1) * 1024]
                        wobl = wob2l[:, (ec % 2) * 1024:(ec % 2 + 1) * 1024]
                        if tt == 0 and ec < 2:
                            p.wait("sync", SPW, SPW.n)
                        if 8 * tt + ec >= 2:
                            p.wait("sync", CWOT, 8 * tt + ec - 1)
                        p.emit("sync", lambda e, o=wobh, ec=ec: e.dma_start(
                            out=o, in_=woh_d.ap()[ec]), inc=LWO, k=16)
                        p.emit("sync", lambda e, o=wobl, ec=ec: e.dma_start(
                            out=o, in_=wol_d.ap()[ec]), inc=LWO, k=16)
                        p.wait("tensor", LWO, LWO.n)
                        p.wait("tensor", COSE, 8 * (tt + 1))
                        if ec >= 2:
                            p.wait("tensor", CGE, 8 * tt + ec - 1)
                        for kc in range(8):
                            p.emit("tensor", _mm(pb[6 + ec % 2][:, 0:512],
                                                 wobh[:, kc * 128:(kc + 1) * 128],
                                                 osth[:, kc * TC:(kc + 1) * TC],
                                                 kc == 0, False))
                            p.emit("tensor", _mm(pb[6 + ec % 2][:, 0:512],
                                                 wobh[:, kc * 128:(kc + 1) * 128],
                                                 ostl[:, kc * TC:(kc + 1) * TC],
                                                 False, False))
                        for kc in range(8):
                            p.emit("tensor", _mm(pb[6 + ec % 2][:, 0:512],
                                                 wobl[:, kc * 128:(kc + 1) * 128],
                                                 osth[:, kc * TC:(kc + 1) * TC],
                                                 False, kc == 7),
                                   inc=(CWOT if kc == 7 else None))
                        p.wait("scalar", CWOT, 8 * tt + ec + 1)
                        p.wait("scalar", CSG, 8 * tt + ec)
                        p.emit("scalar", lambda e, ec=ec: e.activation(
                            gt[:], pb[6 + ec % 2][:, 0:512], AF.Identity,
                            bias=boc[:, ec:ec + 1]))
                        p.emit("scalar", lambda e: e.activation(gs_[:], gt[:], AF.Sigmoid),
                               inc=CGE)
                        p.wait("vector", CGE, 8 * tt + ec + 1)
                        p.wait("vector", CSEV, sev_base + 16)
                        if tt > 0 and ec == 0:
                            p.wait("vector", CTRF, 4 * tt)
                        p.emit("vector", lambda e: e.tensor_tensor(gt[:], gt[:], gs_[:], OP.mult))
                        p.emit("vector", lambda e, ec=ec: e.tensor_tensor(
                            sgt[:, ec * TC:(ec + 1) * TC], gt[:], st_[:, ec * TC:(ec + 1) * TC],
                            OP.mult), inc=CSG)
                    if dbg_g0 is not None and tt == 0:
                        p.wait("gpsimd", CSG, 8)
                        p.emit("gpsimd", lambda e: e.dma_start(out=dbg_g0.ap(), in_=sgt[:]),
                               inc=DBG, k=16)
                    for j in range(4):
                        jj = 4 * tt + j
                        xm = xmt[:, (j % 2) * D:(j % 2 + 1) * D]
                        if jj >= 2:
                            p.wait("sync", CFV, jj - 1)
                        p.emit("sync", lambda e, o=xm, tt=tt, j=j: e.dma_start(
                            out=o, in_=xin.ap()[2048 + tt * TC + j * 128:
                                                2048 + tt * TC + (j + 1) * 128, :]),
                            inc=LXM, k=16)
                        p.wait("tensor", CSG, 8 * (tt + 1))
                        p.wait("tensor", CFV, jj)
                        for g in range(8):
                            bank = pb[2 + g // 4]
                            p.emit("tensor", _mm(bank[:, (g % 4) * 128:(g % 4 + 1) * 128],
                                                 sgt[:, g * TC + j * 128:g * TC + (j + 1) * 128],
                                                 eye[:], g % 4 == 0, g % 4 == 3, is_t=True))
                        p.emit("tensor", tclose(CTRF), inc=CTRF)
                        p.wait("vector", CTRF, jj + 1)
                        p.wait("vector", LXM, 16 * (jj + 1))
                        if jj >= 2:
                            p.wait("vector", COUT, 16 * (jj - 1))
                        fb = finb[:, (j % 2) * D:(j % 2 + 1) * D]
                        p.emit("vector", lambda e, fb=fb, xm=xm: e.tensor_tensor(
                            fb[:, 0:512], pb[2][:, 0:512], xm[:, 0:512], OP.add))
                        p.emit("vector", lambda e, fb=fb, xm=xm: e.tensor_tensor(
                            fb[:, 512:1024], pb[3][:, 0:512], xm[:, 512:1024], OP.add), inc=CFV)
                        p.wait("gpsimd", CFV, jj + 1)
                        p.emit("gpsimd", lambda e, tt=tt, j=j, fb=fb: e.dma_start(
                            out=out_d.ap()[tt * TC + j * 128:tt * TC + (j + 1) * 128, :],
                            in_=fb), inc=COUT, k=16)

                p.wait("gpsimd", COUT, 16 * 16)
                if dbg:
                    p.wait("gpsimd", DBG, DBG.n)
                p.flush(block)

    return nc


def _stage(inputs, core):
    b, h = core // 2, core % 2
    C = np.ascontiguousarray
    x = inputs["L0_attn_x"][b]
    if h == 0:
        xr = np.concatenate([x[2048:], x[:2048]], axis=0)
    else:
        xr = x
    blk = lambda w, K, E: C(w.reshape(K // 128, 128, E // 128, 128).transpose(1, 0, 2, 3))
    m = {
        "xin": C(xr),
        "wg": blk(inputs["L0_attn_Wg"], 1024, 1024),
        "wc": blk(inputs["L0_attn_Wc"], 1024, 1024),
        "w1a": blk(inputs["L0_ffn_w1"][:1024], 1024, 512),
        "w1b": C(inputs["L0_ffn_w1"][1024:].reshape(8, 128, 512)),
        "w2r": C(inputs["L0_ffn_w2"].reshape(4, 128, 64)),
        "b1row": C(inputs["L0_ffn_b1"].reshape(1, 512)),
        "b2col": C(inputs["L0_ffn_b2"].reshape(64, 1)),
        "tau": C(inputs["L0_ffn_tau"].reshape(1, 1)),
        "n1col": C(inputs["L0_attn_n1_w"].reshape(8, 128).T),
        "n2col": C(inputs["L0_attn_n2_w"].reshape(8, 128).T),
        "bgcol": C(inputs["L0_attn_bg"].reshape(8, 128).T),
        "bccol": C(inputs["L0_attn_bc"].reshape(8, 128).T),
        "bocol": C(inputs["L0_ffn_bo"].reshape(8, 128).T),
        "sinitT": C(inputs["L0_ffn_slot_init"].T.reshape(8, 128, 64).transpose(1, 0, 2)),
        "sinit": C(inputs["L0_ffn_slot_init"]),
        "wihT": C(inputs["L0_ffn_Wih"].T.reshape(8, 128, 3072)),
        "whhT": C(inputs["L0_ffn_Whh"].T.reshape(8, 128, 3072)),
        "bihrow": C(inputs["L0_ffn_bih"].reshape(1, 3072)),
        "bhhrow": C(inputs["L0_ffn_bhh"].reshape(1, 3072)),
        "wq": C(inputs["L0_ffn_wq"].reshape(8, 128, 256).transpose(1, 0, 2)),
        "wk": C(inputs["L0_ffn_wk"].reshape(8, 128, 256).transpose(1, 0, 2)),
        "wv": C(inputs["L0_ffn_wv"].reshape(8, 128, 1024).transpose(1, 0, 2)),
        "wvp": C(inputs["L0_ffn_wvp"].reshape(8, 128, 1024).transpose(1, 0, 2)),
        "bqrow": C(inputs["L0_ffn_bq"].reshape(1, 256)),
        "bkrow": C(inputs["L0_ffn_bk"].reshape(1, 256)),
        "bvrow": C(inputs["L0_ffn_bv"].reshape(1, 1024)),
        "bvprow": C(inputs["L0_ffn_bvp"].reshape(1, 1024)),
        "lngrow": C(inputs["L0_ffn_ln_g"].reshape(1, 1024)),
        "lnbrow": C(inputs["L0_ffn_ln_b"].reshape(1, 1024)),
        "wo": C(inputs["L0_ffn_wo"].reshape(8, 128, 8, 128).transpose(2, 1, 0, 3)),
        "w1f": C(inputs["L0_attn_W1"].reshape(8, 128, 32, 128).transpose(2, 1, 0, 3)),
        "w3f": C(inputs["L0_attn_W3"].reshape(8, 128, 32, 128).transpose(2, 1, 0, 3)),
        "w2f": C(inputs["L0_attn_W2"].reshape(32, 128, 8, 128).transpose(2, 1, 0, 3)),
        "eye": np.eye(128, dtype=np.float32),
        "onesc": np.ones((128, 1), np.float32),
        "onesr": np.ones((1, 128), np.float32),
        "flag": np.full((128, 1), float(h), np.float32),
    }
    return {k: np.asarray(v, dtype=np.float32) for k, v in m.items()}


def kernel(**inputs):
    nc = build_program()
    in_maps = [_stage(inputs, core) for core in range(8)]
    res = run_bass_kernel_spmd(nc, in_maps, core_ids=list(range(8)))
    out = np.empty((4, 4096, 1024), dtype=np.float32)
    for core in range(8):
        b, h = core // 2, core % 2
        out[b, h * 2048:(h + 1) * 2048] = res.results[core]["out"]
    return out
